# version 09
# baseline (speedup 1.0000x reference)
import numpy as np
import concourse.bass as bass
import concourse.mybir as mybir
from concourse.bass_utils import run_bass_kernel_spmd

F32 = mybir.dt.float32
BF16 = mybir.dt.bfloat16
AL = mybir.AluOpType
AF = mybir.ActivationFunctionType
AX = mybir.AxisListType

P = 128
D = 1024
ND = D // P
DI = 2816
NI = DI // P
T = 8192
GT = 256
NG = T // GT
NTT = GT // P
SC = 16
L = 64
NCHK = T // L
SEQ = 2048
CPS = SEQ // L
MAGIC = 12582912.0
QMAX = 127.0


class Sem:
    __slots__ = ("h", "n")

    def __init__(self, nc, name):
        self.h = nc.alloc_semaphore(name)
        self.n = 0


class Lanes:

    def __init__(self, nc, name, n=8):
        self.sems = [Sem(nc, f"{name}{i}") for i in range(n)]
        self.i = 0

    def next(self):
        s = self.sems[self.i % len(self.sems)]
        self.i += 1
        s.n += 16
        return (s, s.n)

    def all_tokens(self):
        return [(s, s.n) for s in self.sems]


class Prog:
    ENGS = ("sync", "vector", "scalar", "tensor", "gpsimd")

    def __init__(self, sems):
        self.ops = {e: [] for e in self.ENGS}
        self.sems = sems
        self.last_wait = {e: {} for e in self.ENGS}

    def raw(self, eng, fn):
        self.ops[eng].append(fn)

    def op(self, eng, fn, inc=1):
        s = self.sems[eng]
        s.n += inc
        def w(e, _fn=fn, _s=s, _k=inc):
            _fn(e).then_inc(_s.h, _k)
        self.ops[eng].append(w)
        return s.n

    def dma(self, eng, lanes, fn):
        s, v = lanes.next()
        def w(e, _fn=fn, _s=s):
            _fn(e).then_inc(_s.h, 16)
        self.ops[eng].append(w)
        return (s, v)

    def wait_tok(self, eng, tok):
        if tok == 0 or tok is None:
            return
        s, v = tok
        self.wait(eng, s, v)

    def wait_lanes(self, eng, lanes):
        for s, v in lanes.all_tokens():
            self.wait(eng, s, v)

    def wait(self, eng, sem, val):
        lw = self.last_wait[eng]
        if lw.get(id(sem), -1) >= val:
            return
        lw[id(sem)] = val
        self.ops[eng].append(lambda e, _s=sem, _v=val: e.wait_ge(_s.h, _v))

    def emit(self, nc):
        with nc.Block() as block:
            for name in self.ENGS:
                ops = self.ops[name]
                def body(eng, _ops=ops):
                    for f in _ops:
                        f(eng)
                getattr(block, name)(body)


def build_nc():
    nc = bass.Bass("TRN2", target_bir_lowering=False)

    d_xt = nc.dram_tensor("xt", [D, T], F32, kind="ExternalInput")
    d_x = nc.dram_tensor("x", [T, D], F32, kind="ExternalInput")
    d_w2t = nc.dram_tensor("w2t", [D, D], F32, kind="ExternalInput")
    d_wgt = nc.dram_tensor("wgt", [D, D], F32, kind="ExternalInput")
    d_bmt = nc.dram_tensor("bmt", [D, SC], F32, kind="ExternalInput")
    d_cm = nc.dram_tensor("cm", [D, SC], F32, kind="ExternalInput")
    d_wot = nc.dram_tensor("wot", [D, D], F32, kind="ExternalInput")
    d_anat = nc.dram_tensor("anat", [SC, SC], F32, kind="ExternalInput")
    d_at = nc.dram_tensor("at", [SC, SC], F32, kind="ExternalInput")
    d_b2c = nc.dram_tensor("b2c", [P, ND], F32, kind="ExternalInput")
    d_bgc = nc.dram_tensor("bgc", [P, ND], F32, kind="ExternalInput")
    d_bout = nc.dram_tensor("bout", [1, D], F32, kind="ExternalInput")
    d_clipx = nc.dram_tensor("clipx", [1, 1], F32, kind="ExternalInput")
    d_clipd = nc.dram_tensor("clipd", [1, 1], F32, kind="ExternalInput")
    d_fg_n = nc.dram_tensor("fg_n", [DI, D], F32, kind="ExternalInput")
    d_fu_n = nc.dram_tensor("fu_n", [DI, D], F32, kind="ExternalInput")
    d_fd_n = nc.dram_tensor("fd_n", [D, DI], F32, kind="ExternalInput")
    d_fg_t = nc.dram_tensor("fg_t", [D, DI], F32, kind="ExternalInput")
    d_fu_t = nc.dram_tensor("fu_t", [D, DI], F32, kind="ExternalInput")
    d_fd_t = nc.dram_tensor("fd_t", [DI, D], F32, kind="ExternalInput")
    d_eye = nc.dram_tensor("eye", [P, P], F32, kind="ExternalInput")
    d_ones_r = nc.dram_tensor("ones_r", [1, P], F32, kind="ExternalInput")
    d_ones_c = nc.dram_tensor("ones_c", [P, 1], F32, kind="ExternalInput")
    d_hones = nc.dram_tensor("hones", [1, T], F32, kind="ExternalInput")
    d_out = nc.dram_tensor("out", [T, D], F32, kind="ExternalOutput")
    d_hsp = nc.dram_tensor("hsp", [NI, P, T], F32)
    d_wdq = nc.dram_tensor("wdq", [NI, P, D], BF16)
    d_cc_in = nc.dram_tensor("cc_in", [1, 8], F32)
    d_cc_out = nc.dram_tensor("cc_out", [1, 8], F32)

    A = nc.alloc_sbuf_tensor
    eye = A("s_eye", [P, P], F32)
    ones_r = A("s_ones_r", [1, P], F32)
    ones_c = A("s_ones_c", [P, 1], F32)
    i16b = A("i16b", [SC, SC], BF16)
    anat_b = A("anat_b", [SC, SC], BF16)
    at_b = A("at_b", [SC, SC], BF16)
    bmt_b = A("bmt_b", [P, ND * SC], BF16)
    qstk = A("qstk", [SC, L * SC], BF16)
    wct = A("wct", [SC + 1, D], BF16)
    b2c = A("s_b2c", [P, ND], F32)
    bgc = A("s_bgc", [P, ND], F32)
    sc1 = A("sc1", [1, 16], F32)
    rowt = A("rowt", [1, 2 * P], F32)
    uB = A("uB", [SC, T], BF16)
    hs = A("hs", [SC + 1, T], BF16)
    fstk = A("fstk", [SC, NCHK], BF16)
    scr16 = A("scr16", [SC, NCHK], F32)
    m_cols = A("m_cols", [P, 2 * NG], F32)
    rstd_cols = A("rstd_cols", [P, 2 * NG], F32)
    mrstd_cols = A("mrstd_cols", [P, 2 * NG], F32)
    amax_g = A("amax_g", [P, NI], F32)
    amax_u = A("amax_u", [P, NI], F32)
    amax_d = A("amax_d", [P, ND], F32)
    sxg = A("sxg", [P, NI], F32)
    sxu = A("sxu", [P, NI], F32)
    cS = A("cS", [P, 8], F32)
    mxc = A("mxc", [P, 2], F32)
    mhc = A("mhc", [P, 2], F32)
    habs_c = A("habs_c", [P, NI], F32)
    nwt = A("nwt", [P, 32], F32)

    CONSTS = [0.0, 0.5, 1.0, MAGIC]
    csb = A("csb", [P, len(CONSTS)], F32)
    for i, v in enumerate(CONSTS):
        nc.const_aps.register(F32, v, csb[:, i:i + 1])

    ps = [nc.alloc_psum_tensor(f"ps{i}", [P, 512], F32) for i in range(8)]

    sems = {e: Sem(nc, "s_" + e) for e in Prog.ENGS}
    dL = Lanes(nc, "dL", 8)
    dS = Lanes(nc, "dS", 4)
    dG = Lanes(nc, "dG", 4)
    scc = Sem(nc, "scc")

    wq_g = A("wq_g", [P, ND * DI], BF16)
    wq_u = A("wq_u", [P, ND * DI], BF16)
    wq_d = wq_u
    from contextlib import ExitStack
    _wstack = ExitStack()
    w2tb = _wstack.enter_context(nc.sbuf_tensor("w2tb", [P, ND * D], BF16))
    wgtb = _wstack.enter_context(nc.sbuf_tensor("wgtb", [P, ND * D], BF16))

    V, Sc, Tn, Gp, Sy = "vector", "scalar", "tensor", "gpsimd", "sync"

    def seq_chain(pg, eng, fns):
        s = pg.sems[eng]
        toks = []
        for fn in fns:
            pg.wait(eng, s, s.n)
            toks.append(pg.op(eng, fn))
        return toks[-1] if toks else s.n

    def col2row(pg, bank, col_ap_fn, dst_ap_fn, dep_v=None, dep_t=None):
        if dep_v is not None:
            pg.wait(Tn, sems[V], dep_v)
        if dep_t is not None:
            pg.wait(Tn, sems[Tn], dep_t)
        t1 = pg.op(Tn, lambda t, _c=col_ap_fn, _b=bank: t.transpose(ps[_b][0:1, 0:P], _c(), eye[:, :]))
        pg.wait(V, sems[Tn], t1)
        return pg.op(V, lambda v, _d=dst_ap_fn, _b=bank: v.tensor_copy(_d(), ps[_b][0:1, 0:P]))

    def bcast_row(pg, bank, row_ap_fn, dst_ap_fn, n, dep_v):
        pg.wait(Tn, sems[V], dep_v)
        t1 = pg.op(Tn, lambda t, _r=row_ap_fn, _b=bank, _n=n: t.matmul(
            ps[_b][:, 0:_n], ones_r[:, :], _r(), start=True, stop=True))
        pg.wait(V, sems[Tn], t1)
        return pg.op(V, lambda v, _d=dst_ap_fn, _b=bank, _n=n: v.tensor_copy(_d(), ps[_b][:, 0:_n]))

    pg = Prog(sems)
    with (
        nc.sbuf_tensor("ldb", [P, 2, D], F32) as ldb,
        nc.sbuf_tensor("ltb", [P, 2, DI // 2], F32) as ltb,
        nc.sbuf_tensor("cmb", [P, ND, SC], F32) as cmb,
        nc.sbuf_tensor("recb", [P, DI], F32) as recb,
        nc.sbuf_tensor("qtmp", [P, DI // 2], F32) as qtmp,
        nc.sbuf_tensor("rrow", [1, 512], F32) as rrow,
        nc.sbuf_tensor("dstage", [P, 2, D], BF16) as dstage,
        nc.sbuf_tensor("s_bout", [1, D], F32) as bout,
    ):
        ld0_toks = []
        for o, i_ in [
            (eye[:, :], d_eye[:, :]), (ones_r[:, :], d_ones_r[:, :]),
            (ones_c[:, :], d_ones_c[:, :]), (b2c[:, :], d_b2c[:, :]),
            (bgc[:, :], d_bgc[:, :]), (bout[:, :], d_bout[:, :]),
            (sc1[0:1, 0:1], d_clipx[:, :]), (sc1[0:1, 1:2], d_clipd[:, :]),
            (ldb[0:SC, 0, 0:SC], d_anat[:, :]),
            (ldb[0:SC, 0, 64:64 + SC], d_at[:, :]),
            (cmb[:, :, :], d_cm.rearrange("(c p) n -> p c n", p=P)[:, :, :]),
            (ldb[:, 1, 0:ND * SC].rearrange("p (c n) -> p c n", n=SC), d_bmt.rearrange("(c p) n -> p c n", p=P)[:, :, :]),
        ]:
            ld0_toks.append(pg.dma(Sy, dL, lambda e, _o=o, _i=i_: e.dma_start(out=_o, in_=_i)))
        for t_ in ld0_toks:
            pg.wait_tok(V, t_)
        for i, v in enumerate(CONSTS):
            pg.op(V, lambda vv, _i=i, _v=v: vv.memset(csb[:, _i:_i + 1], _v))
        pg.op(V, lambda v: v.memset(mxc[:, :], 0.0))
        pg.op(V, lambda v: v.memset(mhc[:, :], 0.0))
        pg.op(V, lambda v: v.tensor_copy(anat_b[:, :], ldb[0:SC, 0, 0:SC]))
        pg.op(V, lambda v: v.tensor_copy(at_b[:, :], ldb[0:SC, 0, 64:64 + SC]))
        pg.op(V, lambda v: v.tensor_copy(bmt_b[:, :], ldb[:, 1, 0:ND * SC]))
        v_eye = pg.op(V, lambda v: v.tensor_copy(i16b[:, :], eye[0:SC, 0:SC]))
        pg.wait(V, sems[V], v_eye)
        v_q1 = pg.op(V, lambda v: v.tensor_copy(qstk[:, 0:SC], at_b[:, :]))

        cast_done = {}
        for wi, (wsrc, wdst) in enumerate(((d_w2t, w2tb), (d_wgt, wgtb))):
            for k in range(ND):
                idx = wi * ND + k
                if idx < 2:
                    pg.wait(Sy, sems[V], v_q1)
                else:
                    pg.wait(Sy, sems[V], cast_done[idx - 2])
                lw = pg.dma(Sy, dL, lambda e, _w=wsrc, _k=k, _s=idx % 2: e.dma_start(
                    out=ldb[:, _s, :], in_=_w[_k * P:(_k + 1) * P, :]))
                pg.wait_tok(V, lw)
                cast_done[idx] = pg.op(V, lambda v, _d=wdst, _k=k, _s=idx % 2: v.tensor_copy(
                    _d[:, _k * D:(_k + 1) * D], ldb[:, _s, :]))

        mm_done = {}
        for half in range(2):
            for k in range(ND):
                idx = half * ND + k
                if idx >= 2:
                    pg.wait(Sy, sems[Tn], mm_done[idx - 2])
                lw = pg.dma(Sy, dL, lambda e, _k=k, _h=half, _s=idx % 2: e.dma_start(
                    out=ltb[:, _s, 0:512], in_=d_wot[_k * P:(_k + 1) * P, _h * 512:(_h + 1) * 512]))
                pg.wait_tok(Tn, lw)
                if half == 0 and k == 0:
                    for t_ in ld0_toks:
                        pg.wait_tok(Tn, t_)
                mm_done[idx] = pg.op(Tn, lambda t, _k=k, _h=half, _s=idx % 2: t.matmul(
                    ps[0][0:SC, 0:512], cmb[:, _k, :], ltb[:, _s, 0:512],
                    start=(_k == 0), stop=(_k == ND - 1)))
            pg.wait(V, sems[Tn], mm_done[half * ND + ND - 1])
            ev = pg.op(V, lambda v, _h=half: v.tensor_copy(wct[0:SC, _h * 512:(_h + 1) * 512], ps[0][0:SC, 0:512]))
            pg.wait(Tn, sems[V], ev)

        pg.dma(Gp, dG, lambda e: e.dma_start(out=wct[SC:SC + 1, :], in_=d_bout[:, :]))
        pg.dma(Gp, dG, lambda e: e.dma_start(out=hs[SC:SC + 1, :], in_=d_hones[:, :]))
        pg.wait(Tn, sems[V], v_q1)
        for k in range(1, L):
            tq = pg.op(Tn, lambda t, _k=k: t.matmul(
                ps[1][0:SC, 0:SC], anat_b[:, :], qstk[:, (_k - 1) * SC:_k * SC], start=True, stop=True))
            pg.wait(V, sems[Tn], tq)
            vq = pg.op(V, lambda v, _k=k: v.tensor_copy(qstk[:, _k * SC:(_k + 1) * SC], ps[1][0:SC, 0:SC]))
            pg.wait(Tn, sems[V], vq)

        ldb_free = cast_done[2 * ND - 1]
        ltb_free_t = mm_done[2 * ND - 1]
        ltb_free_v = 0
        for acol, wt_src, wq_dst, wn_src in (
            (amax_g, d_fg_t, wq_g, d_fg_n),
            (amax_u, d_fu_t, wq_u, d_fu_n),
        ):
            red_done = {}
            for oc in range(NI):
                if oc >= 2:
                    pg.wait(Sy, sems[V], red_done[oc - 2])
                else:
                    pg.wait(Sy, sems[V], ldb_free)
                lw = pg.dma(Sy, dL, lambda e, _w=wn_src, _o=oc: e.dma_start(
                    out=ldb[:, _o % 2, :], in_=_w[_o * P:(_o + 1) * P, :]))
                pg.wait_tok(V, lw)
                red_done[oc] = pg.op(V, lambda v, _a=acol, _o=oc: v.tensor_reduce(
                    _a[:, _o:_o + 1], ldb[:, _o % 2, :], AX.X, AL.max, apply_absolute_value=True))
            ldb_free = red_done[NI - 1]
            v_rc = seq_chain(pg, V, [
                lambda v, _a=acol: v.reciprocal(nwt[:, 0:NI], _a[:, :]),
                lambda v: v.tensor_scalar(nwt[:, 0:NI], nwt[:, 0:NI], QMAX, None, AL.mult),
            ])
            e3 = v_rc
            for j in range(6):
                n = min(512, DI - j * 512)
                for jj in range(n // P):
                    oc = j * 4 + jj
                    e3 = col2row(pg, 2, (lambda _o=oc: lambda: nwt[:, _o:_o + 1])(),
                                 (lambda _jj=jj: lambda: rrow[0:1, _jj * P:(_jj + 1) * P])(), dep_v=e3)
                e3 = bcast_row(pg, 2, (lambda _n=n: lambda: rrow[0:1, 0:_n])(),
                               (lambda _j=j, _n=n: lambda: recb[:, _j * 512:_j * 512 + _n])(), n, e3)
            v_recb = e3
            q_done = {}
            HW = DI // 2
            for k2 in range(2 * ND):
                k, hf = divmod(k2, 2)
                if k2 >= 2:
                    pg.wait(Sy, sems[V], q_done[k2 - 2])
                else:
                    pg.wait(Sy, sems[Tn], ltb_free_t)
                    pg.wait(Sy, sems[V], ltb_free_v)
                lw = pg.dma(Sy, dL, lambda e, _w=wt_src, _k=k, _h=hf, _s=k2 % 2: e.dma_start(
                    out=ltb[:, _s, :], in_=_w[_k * P:(_k + 1) * P, _h * HW:(_h + 1) * HW]))
                pg.wait_tok(V, lw)
                pg.wait(V, sems[V], v_recb)
                t1 = pg.op(V, lambda v, _s=k2 % 2, _h=hf: v.tensor_tensor(
                    qtmp[:, :], ltb[:, _s, :], recb[:, _h * HW:(_h + 1) * HW], AL.mult))
                pg.wait(V, sems[V], t1)
                q_done[k2] = pg.op(V, lambda v, _d=wq_dst, _k=k, _h=hf: v.tensor_scalar(
                    _d[:, _k * DI + _h * HW:_k * DI + (_h + 1) * HW], qtmp[:, :], MAGIC, MAGIC, AL.add, AL.subtract))
            ltb_free_v = q_done[2 * ND - 1]
        red_done = {}
        HW = DI // 2
        for oc2 in range(2 * ND):
            oc, hf = divmod(oc2, 2)
            if oc2 >= 2:
                pg.wait(Sy, sems[V], red_done[oc2 - 2])
            else:
                pg.wait(Sy, sems[V], ltb_free_v)
            lw = pg.dma(Sy, dL, lambda e, _o=oc, _h=hf, _s=oc2 % 2: e.dma_start(
                out=ltb[:, _s, :], in_=d_fd_n[_o * P:(_o + 1) * P, _h * HW:(_h + 1) * HW]))
            pg.wait_tok(V, lw)
            if hf == 0:
                red_done[oc2] = pg.op(V, lambda v, _o=oc, _s=oc2 % 2: v.tensor_reduce(
                    nwt[:, 8:9], ltb[:, _s, :], AX.X, AL.max, apply_absolute_value=True))
            else:
                r1 = pg.op(V, lambda v, _s=oc2 % 2: v.tensor_reduce(nwt[:, 9:10], ltb[:, _s, :], AX.X, AL.max, apply_absolute_value=True))
                pg.wait(V, sems[V], r1)
                red_done[oc2] = pg.op(V, lambda v, _o=oc: v.tensor_tensor(amax_d[:, _o:_o + 1], nwt[:, 8:9], nwt[:, 9:10], AL.max))
        e0 = seq_chain(pg, V, [
            lambda v: v.reciprocal(nwt[:, 0:ND], amax_d[:, :]),
            lambda v: v.tensor_scalar(nwt[:, 0:ND], nwt[:, 0:ND], QMAX, None, AL.mult),
            lambda v: v.tensor_scalar(nwt[:, 8:8 + ND], amax_d[:, :], 1.0 / QMAX, None, AL.mult),
        ])
        for oc in range(ND):
            e0 = col2row(pg, 2, (lambda _o=oc: lambda: nwt[:, _o:_o + 1])(),
                         (lambda _o=oc: lambda: rrow[0:1, _o % 4 * P:(_o % 4 + 1) * P])(), dep_v=e0)
            if oc % 4 == 3:
                e0 = bcast_row(pg, 2, lambda: rrow[0:1, 0:512],
                               (lambda _j=oc // 4: lambda: recb[:, _j * 512:(_j + 1) * 512])(), 512, e0)
        for oc in range(ND):
            e0 = col2row(pg, 2, (lambda _o=oc: lambda: nwt[:, 8 + _o:9 + _o])(),
                         (lambda _o=oc: lambda: rrow[0:1, _o % 4 * P:(_o % 4 + 1) * P])(), dep_v=e0)
            if oc % 4 == 3:
                e0 = bcast_row(pg, 2, lambda: rrow[0:1, 0:512],
                               (lambda _j=oc // 4: lambda: recb[:, D + _j * 512:D + (_j + 1) * 512])(), 512, e0)
        wq_stage = {}
        st_done = {}
        for oc in range(NI):
            if oc >= 2:
                pg.wait(Sy, sems[V], wq_stage[oc - 2])
            lw = pg.dma(Sy, dL, lambda e, _o=oc: e.dma_start(out=ltb[:, _o % 2, 0:D], in_=d_fd_t[_o * P:(_o + 1) * P, :]))
            pg.wait_tok(V, lw)
            pg.wait(V, sems[V], e0)
            if oc >= 2:
                pg.wait_tok(V, st_done.get(oc - 2, 0))
            t1 = pg.op(V, lambda v, _o=oc: v.tensor_tensor(qtmp[:, 0:D], ltb[:, _o % 2, 0:D], recb[:, 0:D], AL.mult))
            pg.wait(V, sems[V], t1)
            t2 = pg.op(V, lambda v: v.tensor_scalar(qtmp[:, 0:D], qtmp[:, 0:D], MAGIC, MAGIC, AL.add, AL.subtract))
            pg.wait(V, sems[V], t2)
            wq_stage[oc] = pg.op(V, lambda v, _o=oc: v.tensor_tensor(
                dstage[:, _o % 2, :], qtmp[:, 0:D], recb[:, D:2 * D], AL.mult))
            pg.wait(Sc, sems[V], wq_stage[oc])
            st_done[oc] = pg.dma(Sc, dS, lambda s, _o=oc: s.dma_start(out=d_wdq[_o, :, :], in_=dstage[:, _o % 2, :]))
        pg.emit(nc)

    pg = Prog(sems)
    with (
        nc.sbuf_tensor("xtg", [P, 3, ND, GT], F32) as xtg,
        nc.sbuf_tensor("xnb", [P, 2, ND, GT], BF16) as xnb,
        nc.sbuf_tensor("ubf", [P, ND, GT], BF16) as ubf,
        nc.sbuf_tensor("xnf", [P, 1, GT], F32) as xnf,
        nc.sbuf_tensor("xsq", [P, 1, GT], F32) as xsq,
        nc.sbuf_tensor("sgb", [P, 2, GT], F32) as sgb,
        nc.sbuf_tensor("mbA", [P, 2, GT], F32) as mbA,
        nc.sbuf_tensor("rstdbA", [P, 2, GT], F32) as rstdbA,
        nc.sbuf_tensor("rowS", [1, 2, GT], F32) as rowS,
    ):
        ld_done = {}
        sq_done = {}
        stat_mm = {}
        bcast_done = {}
        xn_done = {}
        pool_done = {}
        u_done = {}
        sg_done = {}
        attn_done = {}
        ub_evac = {}

        def emit_loads(g):
            if g >= 3:
                pg.wait(Sy, sems[V], ub_evac[g - 3])
            ld_done[g] = [pg.dma(Sy, dL, lambda e, _g=g, _dc=dc: e.dma_start(
                out=xtg[:, _g % 3, _dc, :], in_=d_xt[_dc * P:(_dc + 1) * P, _g * GT:(_g + 1) * GT]))
                for dc in range(ND)]

        stat_evac = {}

        def emit_stat_mms(g):
            b = g % 3
            for t_ in ld_done[g]:
                pg.wait_tok(Sc, t_)
                pg.wait_tok(Tn, t_)
            if g >= 1:
                pg.wait(Tn, sems[V], stat_evac.get(g - 1, 0))
            for dc in range(ND):
                if (g, dc - 1) in stat_mm or (g - 1, ND + dc - 1) in stat_mm:
                    key = (g, dc - 1) if dc >= 1 else (g - 1, ND + dc - 1)
                    pg.wait(Sc, sems[Tn], stat_mm[key])
                sq_done[(g, dc)] = pg.op(Sc, lambda s, _b=b, _dc=dc: s.activation(
                    xsq[:, 0, :], xtg[:, _b, _dc, :], AF.Square, bias=0.0, scale=1.0))
                pg.op(Tn, lambda t, _b=b, _dc=dc: t.matmul(
                    ps[5][0:1, 0:GT], ones_c[:, :], xtg[:, _b, _dc, :],
                    start=(_dc == 0), stop=(_dc == ND - 1)))
                pg.wait(Tn, sems[Sc], sq_done[(g, dc)])
                stat_mm[(g, dc)] = pg.op(Tn, lambda t, _dc=dc: t.matmul(
                    ps[6][0:1, 0:GT], ones_c[:, :], xsq[:, 0, :],
                    start=(_dc == 0), stop=(_dc == ND - 1)))
            stat_mm[g] = stat_mm[(g, ND - 1)]

        def emit_stat_chain(g):
            b = g % 3
            br = g % 2
            bm = g % 2
            g2 = 2 * g
            pg.wait(V, sems[Tn], stat_mm[g])
            e1 = pg.op(V, lambda v, _br=br: v.tensor_copy(rowS[0:1, _br, :], ps[5][0:1, 0:GT]))
            for j in range(2):
                col2row_inv = pg
                pg.wait(Tn, sems[V], e1)
                tt = pg.op(Tn, lambda t, _br=br, _j=j: t.transpose(
                    ps[7][0:P, 0:1], rowS[0:1, _br, _j * P:(_j + 1) * P], eye[0:1, 0:1]))
                pg.wait(V, sems[Tn], tt)
                e1 = pg.op(V, lambda v, _j=j: v.tensor_copy(nwt[:, _j:_j + 1], ps[7][0:P, 0:1]))
            pg.wait(V, sems[Tn], sems[Tn].n)
            e2 = pg.op(V, lambda v, _br=br: v.tensor_copy(rowS[0:1, _br, :], ps[6][0:1, 0:GT]))
            stat_evac[g] = e2
            for j in range(2):
                pg.wait(Tn, sems[V], e2)
                tt = pg.op(Tn, lambda t, _br=br, _j=j: t.transpose(
                    ps[7][0:P, 0:1], rowS[0:1, _br, _j * P:(_j + 1) * P], eye[0:1, 0:1]))
                pg.wait(V, sems[Tn], tt)
                e2 = pg.op(V, lambda v, _j=j: v.tensor_copy(nwt[:, 2 + _j:3 + _j], ps[7][0:P, 0:1]))
            seq_chain(pg, V, [
                lambda v, _g2=g2: v.tensor_scalar(m_cols[:, _g2:_g2 + 2], nwt[:, 0:2], 1.0 / D, None, AL.mult),
                lambda v, _g2=g2: v.tensor_tensor(nwt[:, 4:6], m_cols[:, _g2:_g2 + 2], m_cols[:, _g2:_g2 + 2], AL.mult),
                lambda v: v.scalar_tensor_tensor(nwt[:, 6:8], nwt[:, 2:4], 1.0 / D, nwt[:, 4:6], AL.mult, AL.subtract),
                lambda v: v.tensor_scalar(nwt[:, 6:8], nwt[:, 6:8], 1e-5, None, AL.add),
                lambda v: v.tensor_scalar(nwt[:, 8:10], nwt[:, 6:8], 0.5, 0.5, AL.mult, AL.add),
                lambda v: v.reciprocal(nwt[:, 10:12], nwt[:, 8:10]),
                lambda v: v.tensor_tensor(nwt[:, 12:14], nwt[:, 10:12], nwt[:, 10:12], AL.mult),
                lambda v: v.tensor_tensor(nwt[:, 12:14], nwt[:, 12:14], nwt[:, 6:8], AL.mult),
                lambda v: v.tensor_scalar(nwt[:, 12:14], nwt[:, 12:14], -0.5, 1.5, AL.mult, AL.add),
                lambda v: v.tensor_tensor(nwt[:, 10:12], nwt[:, 10:12], nwt[:, 12:14], AL.mult),
                lambda v: v.tensor_tensor(nwt[:, 12:14], nwt[:, 10:12], nwt[:, 10:12], AL.mult),
                lambda v: v.tensor_tensor(nwt[:, 12:14], nwt[:, 12:14], nwt[:, 6:8], AL.mult),
                lambda v: v.tensor_scalar(nwt[:, 12:14], nwt[:, 12:14], -0.5, 1.5, AL.mult, AL.add),
                lambda v: v.tensor_tensor(nwt[:, 10:12], nwt[:, 10:12], nwt[:, 12:14], AL.mult),
                lambda v: v.tensor_tensor(nwt[:, 12:14], nwt[:, 10:12], nwt[:, 10:12], AL.mult),
                lambda v: v.tensor_tensor(nwt[:, 12:14], nwt[:, 12:14], nwt[:, 6:8], AL.mult),
                lambda v: v.tensor_scalar(nwt[:, 12:14], nwt[:, 12:14], -0.5, 1.5, AL.mult, AL.add),
                lambda v: v.tensor_tensor(nwt[:, 10:12], nwt[:, 10:12], nwt[:, 12:14], AL.mult),
                lambda v: v.tensor_tensor(nwt[:, 12:14], nwt[:, 10:12], nwt[:, 10:12], AL.mult),
                lambda v: v.tensor_tensor(nwt[:, 12:14], nwt[:, 12:14], nwt[:, 6:8], AL.mult),
                lambda v: v.tensor_scalar(nwt[:, 12:14], nwt[:, 12:14], -0.5, 1.5, AL.mult, AL.add),
                lambda v, _g2=g2: v.tensor_tensor(rstd_cols[:, _g2:_g2 + 2], nwt[:, 10:12], nwt[:, 12:14], AL.mult),
                lambda v, _g2=g2: v.tensor_tensor(mrstd_cols[:, _g2:_g2 + 2], m_cols[:, _g2:_g2 + 2], rstd_cols[:, _g2:_g2 + 2], AL.mult),
            ])
            v_stats = sems[V].n
            if g >= 2:
                pg.wait(V, sems[Gp], pool_done.get((g - 2, ND - 1), 0))
            e3 = v_stats
            for j in range(2):
                e3 = col2row(pg, 7, (lambda _g2=g2, _j=j: lambda: m_cols[:, _g2 + _j:_g2 + _j + 1])(),
                             (lambda _br=br, _j=j: lambda: rowS[0:1, _br, _j * P:(_j + 1) * P])(), dep_v=e3)
            e3 = bcast_row(pg, 7, (lambda _br=br: lambda: rowS[0:1, _br, :])(),
                           (lambda _bm=bm: lambda: mbA[:, _bm, :])(), GT, e3)
            for j in range(2):
                e3 = col2row(pg, 7, (lambda _g2=g2, _j=j: lambda: rstd_cols[:, _g2 + _j:_g2 + _j + 1])(),
                             (lambda _br=br, _j=j: lambda: rowS[0:1, _br, _j * P:(_j + 1) * P])(), dep_v=e3)
            e3 = bcast_row(pg, 7, (lambda _br=br: lambda: rowS[0:1, _br, :])(),
                           (lambda _bm=bm: lambda: rstdbA[:, _bm, :])(), GT, e3)
            bcast_done[g] = e3

        def emit_attn(g):
            b = g % 3
            bx = g % 2
            for dc in range(ND):
                e_ = dc % 2
                idx = g * ND + dc
                pg.wait(Gp, sems[V], bcast_done[g])
                for t_ in ld_done[g]:
                    pg.wait_tok(Gp, t_)
                if idx >= 1:
                    pg.wait(Gp, sems[V], xn_done.get(divmod(idx - 1, ND), 0))
                p1 = pg.op(Gp, lambda p_, _b=b, _bm=g % 2, _dc=dc, _e=e_: p_.tensor_tensor(
                    xnf[:, 0, :], xtg[:, _b, _dc, :], mbA[:, _bm, :], AL.subtract))
                pg.wait(Gp, sems[Gp], p1)
                pool_done[(g, dc)] = pg.op(Gp, lambda p_, _bm=g % 2, _e=e_: p_.tensor_tensor(
                    xnf[:, 0, :], xnf[:, 0, :], rstdbA[:, _bm, :], AL.mult))
                pg.wait(V, sems[Gp], pool_done[(g, dc)])
                if idx >= 2:
                    pg.wait(V, sems[Tn], attn_done.get(g - 2, 0))
                c1 = pg.op(V, lambda v, _bx=bx, _dc=dc, _e=e_: v.tensor_copy(xnb[:, _bx, _dc, :], xnf[:, 0, :]))
                r1 = pg.op(V, lambda v, _e=e_: v.tensor_reduce(nwt[:, 14:15], xnf[:, 0, :], AX.X, AL.max, apply_absolute_value=True))
                pg.wait(V, sems[V], r1)
                pp = idx % 2
                mx = pg.op(V, lambda v, _pp=pp: v.tensor_tensor(
                    mxc[:, 1 - _pp:2 - _pp], mxc[:, _pp:_pp + 1], nwt[:, 14:15], AL.max))
                xn_done[(g, dc)] = mx
            for dc in range(ND):
                e_ = dc % 2
                idx = g * ND + dc
                psA, psB = (0, 1) if e_ == 0 else (2, 3)
                pg.wait(Tn, sems[V], xn_done[(g, ND - 1)])
                if idx >= 2:
                    prev = divmod(idx - 2, ND)
                    pg.wait(Tn, sems[V], u_done.get(prev, 0))
                    pg.wait(Tn, sems[Sc], sg_done.get(prev, 0))
                for k in range(ND):
                    pg.op(Tn, lambda t, _pA=psA, _bx=bx, _dc=dc, _k=k: t.matmul(
                        ps[_pA][:, 0:GT], w2tb[:, _k * D + _dc * P:_k * D + (_dc + 1) * P], xnb[:, _bx, _k, :],
                        start=(_k == 0), stop=(_k == ND - 1)))
                for k in range(ND):
                    pg.op(Tn, lambda t, _pB=psB, _bx=bx, _dc=dc, _k=k: t.matmul(
                        ps[_pB][:, 0:GT], wgtb[:, _k * D + _dc * P:_k * D + (_dc + 1) * P], xnb[:, _bx, _k, :],
                        start=(_k == 0), stop=(_k == ND - 1)))
                t_mm = sems[Tn].n
                pg.wait(Sc, sems[Tn], t_mm)
                if idx >= 2:
                    prev = divmod(idx - 2, ND)
                    pg.wait(Sc, sems[V], u_done.get(prev, 0))
                sg_done[(g, dc)] = pg.op(Sc, lambda s, _pB=psB, _dc=dc, _e=e_: s.activation(
                    sgb[:, _e, :], ps[_pB][:, 0:GT], AF.Sigmoid, bias=bgc[:, _dc:_dc + 1], scale=1.0))
                pg.wait(V, sems[Sc], sg_done[(g, dc)])
                if dc == 0:
                    pg.wait(V, sems[Tn], attn_done.get(g - 1, 0))
                u_done[(g, dc)] = pg.op(V, lambda v, _pA=psA, _dc=dc, _e=e_: v.scalar_tensor_tensor(
                    ubf[:, _dc, :], ps[_pA][:, 0:GT], b2c[:, _dc:_dc + 1], sgb[:, _e, :], AL.add, AL.mult))
            pg.wait(Tn, sems[V], u_done[(g, ND - 1)])
            if g >= 1:
                pg.wait(Tn, sems[V], ub_evac.get(g - 1, 0))
            for k in range(ND):
                pg.op(Tn, lambda t, _k=k: t.matmul(
                    ps[4][0:SC, 0:GT], bmt_b[:, _k * SC:(_k + 1) * SC], ubf[:, _k, :],
                    start=(_k == 0), stop=(_k == ND - 1)))
            attn_done[g] = sems[Tn].n
            pg.wait(V, sems[Tn], attn_done[g])
            ub_evac[g] = pg.op(V, lambda v, _g=g: v.tensor_copy(uB[:, _g * GT:(_g + 1) * GT], ps[4][0:SC, 0:GT]))

        for gg in (0, 1):
            emit_loads(gg)
            emit_stat_mms(gg)
            emit_stat_chain(gg)
        for g in range(NG):
            if g + 2 < NG:
                emit_loads(g + 2)
                emit_stat_mms(g + 2)
            emit_attn(g)
            if g + 2 < NG:
                emit_stat_chain(g + 2)

        fin = (NG * ND) % 2
        fx = col2row(pg, 7, lambda: mxc[:, fin:fin + 1], lambda: rowt[0:1, 0:P], dep_v=xn_done[(NG - 1, ND - 1)])
        pg.wait(V, sems[V], fx)
        vmx = pg.op(V, lambda v: v.tensor_reduce(sc1[0:1, 2:3], rowt[0:1, 0:P], AX.X, AL.max))
        pg.wait(Sc, sems[V], vmx)
        stg = pg.op(Sc, lambda s: s.activation(rowt[0:1, P:P + 8], ones_r[0:1, 0:8], AF.Identity, bias=0.0, scale=sc1[0:1, 2:3]))
        pg.wait(Gp, sems[Sc], stg)
        tk1 = pg.dma(Gp, dG, lambda e: e.dma_start(out=d_cc_in[:, :], in_=rowt[0:1, P:P + 8]))
        pg.wait_tok(Gp, tk1)
        cc_tok = pg.op(Gp, lambda e: e.collective_compute(
            "AllReduce", AL.max, replica_groups=[list(range(8))],
            ins=[d_cc_in[:, :]], outs=[d_cc_out[:, :]]))
        pg.wait(Gp, sems[Gp], cc_tok)
        tk2 = pg.dma(Gp, dG, lambda e: e.dma_start(out=sc1[0:1, 2:3], in_=d_cc_out[0:1, 0:1]))
        pg.wait_tok(Sc, tk2)
        sgm = pg.op(Sc, lambda s: s.activation(sc1[0:1, 3:4], sc1[0:1, 0:1], AF.Sigmoid, bias=0.0, scale=1.0))
        pg.wait(V, sems[Sc], sgm)
        seq_chain(pg, V, [
            lambda v: v.tensor_tensor(sc1[0:1, 4:5], sc1[0:1, 3:4], sc1[0:1, 2:3], AL.mult),
            lambda v: v.tensor_scalar(sc1[0:1, 4:5], sc1[0:1, 4:5], 1.0 / QMAX, None, AL.mult),
            lambda v: v.reciprocal(sc1[0:1, 5:6], sc1[0:1, 4:5]),
        ])
        vsx = sems[V].n
        pg.wait(Tn, sems[V], vsx)
        tb = pg.op(Tn, lambda t: t.matmul(ps[7][:, 0:2], ones_r[:, :], sc1[0:1, 4:6], start=True, stop=True))
        pg.wait(V, sems[Tn], tb)
        seq_chain(pg, V, [
            lambda v: v.tensor_copy(cS[:, 2:3], ps[7][:, 0:1]),
            lambda v: v.tensor_copy(cS[:, 0:1], ps[7][:, 1:2]),
            lambda v: v.tensor_scalar(cS[:, 1:2], ps[7][:, 1:2], -1.0, None, AL.mult),
            lambda v: v.tensor_scalar(sxg[:, :], amax_g[:, :], 1.0 / QMAX, None, AL.mult),
            lambda v: v.tensor_scalar_mul(sxg[:, :], sxg[:, :], cS[:, 2:3]),
            lambda v: v.tensor_scalar(sxu[:, :], amax_u[:, :], 1.0 / QMAX, None, AL.mult),
            lambda v: v.tensor_scalar_mul(sxu[:, :], sxu[:, :], cS[:, 2:3]),
        ])
        pg.emit(nc)
    _wstack.close()

    pg = Prog(sems)
    with (
        nc.sbuf_tensor("xtg2", [P, 3, ND, GT], F32) as xtg2,
        nc.sbuf_tensor("zq", [P, 2, ND, GT], BF16) as zq,
        nc.sbuf_tensor("yt", [P, 2, GT], F32) as yt,
        nc.sbuf_tensor("sg2", [P, 2, GT], F32) as sg2,
        nc.sbuf_tensor("hb", [P, 4, GT], F32) as hb,
        nc.sbuf_tensor("c1b", [P, 3, GT], F32) as c1b,
        nc.sbuf_tensor("c0b", [P, 3, GT], F32) as c0b,
        nc.sbuf_tensor("rowB", [1, 3, GT], F32) as rowB,
    ):
        ldB = {}
        c_done = {}
        z_done = {}
        gu_done = {}
        h_done = {}
        sil_done = {}
        spill = {}
        habs_done = {}
        scan_v = {}
        scan_emitted = [0]

        def emit_loadsB(g):
            if g >= 3:
                pg.wait(Sy, sems[V], z_done[g - 3])
            ldB[g] = [pg.dma(Sy, dL, lambda e, _g=g, _dc=dc: e.dma_start(
                out=xtg2[:, _g % 3, _dc, :], in_=d_xt[_dc * P:(_dc + 1) * P, _g * GT:(_g + 1) * GT]))
                for dc in range(ND)]

        c_tn = {}

        def emit_cchain(g):
            b = g % 3
            g2 = 2 * g
            o = 16 + 4 * (g % 2)
            pg.wait(V, sems[Tn], c_tn.get(g - 2, 0))
            e0 = seq_chain(pg, V, [
                lambda v, _g2=g2, _o=o: v.tensor_scalar_mul(nwt[:, _o:_o + 2], rstd_cols[:, _g2:_g2 + 2], cS[:, 0:1]),
                lambda v, _g2=g2, _o=o: v.tensor_scalar_mul(nwt[:, _o + 2:_o + 4], mrstd_cols[:, _g2:_g2 + 2], cS[:, 1:2]),
            ])
            if g >= 3:
                pg.wait(V, sems[Gp], c_done.get(("pool", g - 3), 0))
            e3 = e0
            for j in range(2):
                e3 = col2row(pg, 5, (lambda _j=j, _o=o: lambda: nwt[:, _o + _j:_o + _j + 1])(),
                             (lambda _b=b, _j=j: lambda: rowB[0:1, _b, _j * P:(_j + 1) * P])(), dep_v=e3)
            e3 = bcast_row(pg, 5, (lambda _b=b: lambda: rowB[0:1, _b, :])(),
                           (lambda _b=b: lambda: c1b[:, _b, :])(), GT, e3)
            for j in range(2):
                e3 = col2row(pg, 5, (lambda _j=j, _o=o: lambda: nwt[:, _o + 2 + _j:_o + 3 + _j])(),
                             (lambda _b=b, _j=j: lambda: rowB[0:1, _b, _j * P:(_j + 1) * P])(), dep_v=e3)
            e3 = bcast_row(pg, 5, (lambda _b=b: lambda: rowB[0:1, _b, :])(),
                           (lambda _b=b: lambda: c0b[:, _b, :])(), GT, e3)
            c_done[g] = e3
            c_tn[g] = sems[Tn].n

        def emit_xq(g):
            b = g % 3
            bz = g % 2
            for t_ in ldB[g]:
                pg.wait_tok(Gp, t_)
            pg.wait(Gp, sems[V], c_done[g])
            for dc in range(ND):
                e_ = dc % 2
                idx = g * ND + dc
                if idx >= 2:
                    pg.wait(Gp, sems[V], z_done.get(divmod(idx - 2, ND), 0))
                p1 = pg.op(Gp, lambda p_, _b=b, _dc=dc, _e=e_: p_.tensor_tensor(
                    yt[:, _e, :], xtg2[:, _b, _dc, :], c1b[:, _b, :], AL.mult))
                pg.wait(Gp, sems[Gp], p1)
                p2 = pg.op(Gp, lambda p_, _b=b, _e=e_: p_.tensor_tensor(
                    yt[:, _e, :], yt[:, _e, :], c0b[:, _b, :], AL.add))
                c_done[("pool", g)] = p2
                pg.wait(V, sems[Gp], p2)
                if idx >= 2 and dc < 2:
                    pg.wait(V, sems[Tn], gu_done.get((g - 2, NI - 1), 0))
                t1 = pg.op(V, lambda v, _e=e_: v.tensor_scalar(yt[:, _e, :], yt[:, _e, :], MAGIC, MAGIC, AL.add, AL.subtract))
                pg.wait(V, sems[V], t1)
                z_done[divmod(idx, ND)] = pg.op(V, lambda v, _bz=bz, _dc=dc, _e=e_: v.tensor_scalar(
                    zq[:, _bz, _dc, :], yt[:, _e, :], -128.0, 127.0, AL.max, AL.min))
            z_done[g] = z_done[(g, ND - 1)]

        def emit_gateup(g):
            b = g % 2
            pg.wait(Tn, sems[V], z_done[g])
            for oc in range(NI):
                e_ = oc % 2
                idx = g * NI + oc
                psG, psU = (0, 1) if e_ == 0 else (2, 3)
                if idx >= 2:
                    prev = divmod(idx - 2, NI)
                    pg.wait(Tn, sems[Sc], sil_done.get(prev, 0))
                    pg.wait(Tn, sems[V], h_done.get(prev, 0))
                for k in range(ND):
                    pg.op(Tn, lambda t, _p=psG, _b=b, _oc=oc, _k=k: t.matmul(
                        ps[_p][:, 0:GT], wq_g[:, _k * DI + _oc * P:_k * DI + (_oc + 1) * P], zq[:, _b, _k, :],
                        start=(_k == 0), stop=(_k == ND - 1)))
                for k in range(ND):
                    pg.op(Tn, lambda t, _p=psU, _b=b, _oc=oc, _k=k: t.matmul(
                        ps[_p][:, 0:GT], wq_u[:, _k * DI + _oc * P:_k * DI + (_oc + 1) * P], zq[:, _b, _k, :],
                        start=(_k == 0), stop=(_k == ND - 1)))
                gu_done[(g, oc)] = sems[Tn].n
                pg.wait(Sc, sems[Tn], gu_done[(g, oc)])
                if idx >= 2:
                    pg.wait(Sc, sems[V], h_done.get(divmod(idx - 2, NI), 0))
                sil_done[(g, oc)] = pg.op(Sc, lambda s, _p=psG, _oc=oc, _e=e_: s.activation(
                    sg2[:, _e, :], ps[_p][:, 0:GT], AF.Silu, bias=0.0, scale=sxg[:, _oc:_oc + 1]))
                hslot = idx % 4
                pg.wait(V, sems[Sc], sil_done[(g, oc)])
                if idx >= 4:
                    pg.wait_tok(V, spill.get(divmod(idx - 4, NI), 0))
                hv = pg.op(V, lambda v, _p=psU, _oc=oc, _e=e_, _h=hslot: v.scalar_tensor_tensor(
                    hb[:, _h, :], ps[_p][:, 0:GT], sxu[:, _oc:_oc + 1], sg2[:, _e, :], AL.mult, AL.mult))
                pg.wait(V, sems[V], hv)
                h_done[(g, oc)] = pg.op(V, lambda v, _oc=oc, _h=hslot: v.tensor_reduce(
                    habs_c[:, _oc:_oc + 1], hb[:, _h, :], AX.X, AL.max, apply_absolute_value=True))
                pg.wait(Sc, sems[V], h_done[(g, oc)])
                spill[(g, oc)] = pg.dma(Sc, dS, lambda s, _g=g, _oc=oc, _h=hslot: s.dma_start(
                    out=d_hsp[_oc, :, _g * GT:(_g + 1) * GT], in_=hb[:, _h, :]))
            pp = g % 2
            pg.wait(V, sems[V], h_done[(g, NI - 1)])
            r1 = pg.op(V, lambda v: v.tensor_reduce(nwt[:, 14:15], habs_c[:, :], AX.X, AL.max))
            pg.wait(V, sems[V], r1)
            habs_done[g] = pg.op(V, lambda v, _pp=pp: v.tensor_tensor(
                mhc[:, 1 - _pp:2 - _pp], mhc[:, _pp:_pp + 1], nwt[:, 14:15], AL.max))

        def emit_scan_steps(n):
            for _ in range(n):
                tstep = scan_emitted[0] + 1
                if tstep >= L:
                    return
                scan_emitted[0] = tstep
                pg.wait(Tn, sems[V], scan_v[tstep - 1])
                pg.op(Tn, lambda t, _t=tstep: t.matmul(
                    ps[4][0:SC, 0:NCHK], at_b[:, :], hs[0:SC, (_t - 1):T:L], start=True, stop=False))
                tq = pg.op(Tn, lambda t, _t=tstep: t.matmul(
                    ps[4][0:SC, 0:NCHK], i16b[:, :], uB[:, _t:T:L], start=False, stop=True))
                pg.wait(V, sems[Tn], tq)
                scan_v[tstep] = pg.op(V, lambda v, _t=tstep: v.tensor_copy(hs[0:SC, _t:T:L], ps[4][0:SC, 0:NCHK]))

        scan_v[0] = pg.op(V, lambda v: v.tensor_copy(hs[0:SC, 0:T:L], uB[:, 0:T:L]))

        for gg in (0, 1):
            emit_loadsB(gg)
            emit_cchain(gg)
        emit_xq(0)
        for g in range(NG):
            if g + 2 < NG:
                emit_loadsB(g + 2)
                emit_cchain(g + 2)
            emit_gateup(g)
            if g + 1 < NG:
                emit_xq(g + 1)
            emit_scan_steps(2)
        emit_scan_steps(L)
        pg.wait(V, sems[V], scan_v[L - 1])
        f0 = pg.op(V, lambda v: v.memset(fstk[:, 0:NCHK:CPS], 0.0))
        fprev = f0
        q64 = (L - 1) * SC
        for c in range(1, CPS):
            pg.wait(Tn, sems[V], fprev)
            pg.op(Tn, lambda t, _c=c: t.matmul(
                ps[4][0:SC, 0:4], qstk[:, q64:q64 + SC], fstk[:, (_c - 1):NCHK:CPS], start=True, stop=False))
            tq = pg.op(Tn, lambda t, _c=c: t.matmul(
                ps[4][0:SC, 0:4], i16b[:, :], hs[0:SC, ((_c - 1) * L + L - 1):T:SEQ], start=False, stop=True))
            pg.wait(V, sems[Tn], tq)
            fprev = pg.op(V, lambda v, _c=c: v.tensor_copy(fstk[:, _c:NCHK:CPS], ps[4][0:SC, 0:4]))
        corr_prev = fprev
        for tstep in range(L):
            pg.wait(Tn, sems[V], corr_prev)
            tq = pg.op(Tn, lambda t, _t=tstep: t.matmul(
                ps[4][0:SC, 0:NCHK], qstk[:, _t * SC:(_t + 1) * SC], fstk[:, :], start=True, stop=True))
            pg.wait(V, sems[Tn], tq)
            a1 = pg.op(V, lambda v, _t=tstep: v.tensor_tensor(scr16[:, :], hs[0:SC, _t:T:L], ps[4][0:SC, 0:NCHK], AL.add))
            pg.wait(V, sems[V], a1)
            corr_prev = pg.op(V, lambda v, _t=tstep: v.tensor_copy(hs[0:SC, _t:T:L], scr16[:, :]))

        finh = NG % 2
        fh = col2row(pg, 7, lambda: mhc[:, finh:finh + 1], lambda: rowt[0:1, 0:P], dep_v=habs_done[NG - 1])
        pg.wait(V, sems[V], fh)
        vmh = pg.op(V, lambda v: v.tensor_reduce(sc1[0:1, 2:3], rowt[0:1, 0:P], AX.X, AL.max))
        pg.wait(Sc, sems[V], vmh)
        stg = pg.op(Sc, lambda s: s.activation(rowt[0:1, P:P + 8], ones_r[0:1, 0:8], AF.Identity, bias=0.0, scale=sc1[0:1, 2:3]))
        pg.wait(Gp, sems[Sc], stg)
        tk1 = pg.dma(Gp, dG, lambda e: e.dma_start(out=d_cc_in[:, :], in_=rowt[0:1, P:P + 8]))
        pg.wait_tok(Gp, tk1)
        cc_tok = pg.op(Gp, lambda e: e.collective_compute(
            "AllReduce", AL.max, replica_groups=[list(range(8))],
            ins=[d_cc_in[:, :]], outs=[d_cc_out[:, :]]))
        pg.wait(Gp, sems[Gp], cc_tok)
        tk2 = pg.dma(Gp, dG, lambda e: e.dma_start(out=sc1[0:1, 2:3], in_=d_cc_out[0:1, 0:1]))
        pg.wait_tok(Sc, tk2)
        sgm = pg.op(Sc, lambda s: s.activation(sc1[0:1, 3:4], sc1[0:1, 1:2], AF.Sigmoid, bias=0.0, scale=1.0))
        pg.wait(V, sems[Sc], sgm)
        seq_chain(pg, V, [
            lambda v: v.tensor_tensor(sc1[0:1, 4:5], sc1[0:1, 3:4], sc1[0:1, 2:3], AL.mult),
            lambda v: v.tensor_scalar(sc1[0:1, 4:5], sc1[0:1, 4:5], 1.0 / QMAX, None, AL.mult),
            lambda v: v.reciprocal(sc1[0:1, 5:6], sc1[0:1, 4:5]),
        ])
        vsd = sems[V].n
        pg.wait(Tn, sems[V], vsd)
        tb = pg.op(Tn, lambda t: t.matmul(ps[7][:, 0:2], ones_r[:, :], sc1[0:1, 4:6], start=True, stop=True))
        pg.wait(V, sems[Tn], tb)
        seq_chain(pg, V, [
            lambda v: v.tensor_copy(cS[:, 6:7], ps[7][:, 0:1]),
            lambda v: v.tensor_copy(cS[:, 3:4], ps[7][:, 1:2]),
        ])
        pg.emit(nc)

    pg = Prog(sems)
    with (
        nc.sbuf_tensor("hre", [P, 2, 4, GT], F32) as hre,
        nc.sbuf_tensor("yq", [P, 2, 4 * GT], F32) as yq,
        nc.sbuf_tensor("zh", [P, 2, NI, GT], BF16) as zh,
        nc.sbuf_tensor("xres", [P, 2, D], F32) as xres,
        nc.sbuf_tensor("outb", [P, 2, D], F32) as outb,
    ):
        wdl = [pg.dma(Sy, dL, lambda e, _o=oc: e.dma_start(out=wq_d[:, _o * D:(_o + 1) * D], in_=d_wdq[_o, :, :]))
               for oc in range(NI)]
        for t_ in wdl:
            pg.wait_tok(Tn, t_)
        ldh = {}
        sA_done = {}
        zq_done = {}
        acc_done = {}
        ev_done = {}
        outd = {}

        BW = 4
        NB = (NI + BW - 1) // BW

        def emit_hq(g):
            b = g % 2
            for ib in range(NB):
                o0 = ib * BW
                nch = min(BW, NI - o0)
                idx = g * NB + ib
                e_ = idx % 2
                if idx >= 2:
                    pg.wait(Sy, sems[V], sA_done[idx - 2])
                ldh[idx] = pg.dma(Sy, dL, lambda e, _g=g, _o=o0, _n=nch, _s=idx % 2: e.dma_start(
                    out=hre[:, _s, 0:_n, :], in_=d_hsp[_o:_o + _n, :, _g * GT:(_g + 1) * GT].rearrange("c p t -> p c t")))
                Qe = V
                pg.wait_tok(Qe, ldh[idx])
                if idx >= 2:
                    pg.wait(Qe, sems[V], zq_done.get(idx - 2, (None, 0))[1] if False else 0)
                if idx >= 2:
                    pt = zq_done[idx - 2]
                    pg.wait(Qe, pt[0], pt[1])
                sA_done[idx] = pg.op(Qe, lambda v, _s=idx % 2, _e=e_, _n=nch: v.tensor_scalar(
                    yq[:, _e, 0:_n * GT], hre[:, _s, 0:_n, :].rearrange("p c t -> p (c t)"),
                    cS[:, 3:4], MAGIC, AL.mult, AL.add))
                pg.wait(Qe, pg.sems[Qe], sA_done[idx])
                if g >= 2 and ib == 0:
                    pg.wait(Qe, sems[Tn], acc_done.get((g - 2, NTT - 1), 0))
                t0 = pg.op(Qe, lambda v, _e=e_, _n=nch: v.tensor_scalar(
                    yq[:, _e, 0:_n * GT], yq[:, _e, 0:_n * GT], MAGIC, -128.0, AL.subtract, AL.max))
                pg.wait(Qe, pg.sems[Qe], t0)
                zq_done[idx] = (pg.sems[Qe], pg.op(Qe, lambda v, _b=b, _o=o0, _n=nch, _e=e_: v.tensor_scalar(
                    zh[:, _b, _o:_o + _n, :].rearrange("p c t -> p (c t)"), yq[:, _e, 0:_n * GT],
                    127.0, cS[:, 6:7], AL.min, AL.mult)))
            zq_done[g] = zq_done[g * NB + NB - 1]

        def emit_tiles(g):
            b = g % 2
            pg.wait(Tn, zq_done[g][0], zq_done[g][1])
            if g >= 1:
                pg.wait(Sy, sems[V], ev_done.get((g - 1, NTT - 1), 0))
            ldx = [pg.dma(Sy, dL, lambda e, _g=g, _b=b, _tt=tt: e.dma_start(
                out=xres[:, (_g * NTT + _tt) % 2, :], in_=d_x[(_g * NTT + _tt) * P:(_g * NTT + _tt + 1) * P, :]))
                for tt in range(NTT)]
            for tt in range(NTT):
                ti = g * NTT + tt
                pA, pB = (0, 1) if tt == 0 else (2, 3)
                if ti >= 2:
                    pg.wait(Tn, sems[V], ev_done.get((g - 1, tt), 0))
                for half, bk in ((0, pA), (1, pB)):
                    pg.op(Tn, lambda t, _bk=bk, _h=half, _ti=ti: t.matmul(
                        ps[_bk][:, 0:512], hs[0:SC + 1, _ti * P:(_ti + 1) * P], wct[0:SC + 1, _h * 512:(_h + 1) * 512],
                        start=True, stop=False))
                    for oc in range(NI):
                        pg.op(Tn, lambda t, _bk=bk, _h=half, _b=b, _o=oc, _tt=tt: t.matmul(
                            ps[_bk][:, 0:512], zh[:, _b, _o, _tt * P:(_tt + 1) * P],
                            wq_d[:, _o * D + _h * 512:_o * D + _h * 512 + 512],
                            start=False, stop=(_o == NI - 1)))
                acc_done[(g, tt)] = sems[Tn].n
                pg.wait(V, sems[Tn], acc_done[(g, tt)])
                pg.wait_tok(V, ldx[tt])
                if ti >= 2:
                    pg.wait_tok(V, outd.get(ti - 2, 0))
                pg.op(V, lambda v, _pA=pA, _s=ti % 2: v.tensor_tensor(
                    outb[:, _s, 0:512], ps[_pA][:, 0:512], xres[:, _s, 0:512], AL.add))
                ev_done[(g, tt)] = pg.op(V, lambda v, _pB=pB, _s=ti % 2: v.tensor_tensor(
                    outb[:, _s, 512:1024], ps[_pB][:, 0:512], xres[:, _s, 512:1024], AL.add))
                pg.wait(Gp, sems[V], ev_done[(g, tt)])
                outd[ti] = pg.dma(Gp, dG, lambda e, _ti=ti, _s=ti % 2: e.dma_start(
                    out=d_out[_ti * P:(_ti + 1) * P, :], in_=outb[:, _s, :]))

        emit_hq(0)
        for g in range(NG):
            emit_tiles(g)
            if g + 1 < NG:
                emit_hq(g + 1)
        pg.wait_lanes(Gp, dG)
        pg.emit(nc)
    return nc


_NC_CACHE = {}


def kernel(L0_attn_x, L0_attn_W_in, L0_attn_b_in, L0_attn_A, L0_attn_Bm, L0_attn_Cm,
           L0_attn_W_gate, L0_attn_b_gate, L0_attn_W_out, L0_attn_b_out,
           L0_ffn_clip_x, L0_ffn_clip_down, L0_ffn_w_gate, L0_ffn_w_up, L0_ffn_w_down):
    x = np.asarray(L0_attn_x)
    B, S, Dm = x.shape
    xr = np.ascontiguousarray(x.reshape(B * S, Dm))
    f32 = np.float32

    w_in = np.asarray(L0_attn_W_in)
    shared = {
        "w2t": np.ascontiguousarray(w_in[Dm:2 * Dm, :].T.astype(f32)),
        "wgt": np.ascontiguousarray(np.asarray(L0_attn_W_gate).T.astype(f32)),
        "bmt": np.ascontiguousarray(np.asarray(L0_attn_Bm).T.astype(f32)),
        "cm": np.ascontiguousarray(np.asarray(L0_attn_Cm).astype(f32)),
        "wot": np.ascontiguousarray(np.asarray(L0_attn_W_out).T.astype(f32)),
        "anat": np.ascontiguousarray(np.asarray(L0_attn_A).astype(f32)),
        "at": np.ascontiguousarray(np.asarray(L0_attn_A).T.astype(f32)),
        "b2c": np.ascontiguousarray(np.asarray(L0_attn_b_in)[Dm:2 * Dm].astype(f32).reshape(ND, P).T),
        "bgc": np.ascontiguousarray(np.asarray(L0_attn_b_gate).astype(f32).reshape(ND, P).T),
        "bout": np.ascontiguousarray(np.asarray(L0_attn_b_out).astype(f32).reshape(1, Dm)),
        "clipx": np.asarray(L0_ffn_clip_x, f32).reshape(1, 1),
        "clipd": np.asarray(L0_ffn_clip_down, f32).reshape(1, 1),
        "fg_n": np.ascontiguousarray(np.asarray(L0_ffn_w_gate).astype(f32)),
        "fu_n": np.ascontiguousarray(np.asarray(L0_ffn_w_up).astype(f32)),
        "fd_n": np.ascontiguousarray(np.asarray(L0_ffn_w_down).astype(f32)),
        "fg_t": np.ascontiguousarray(np.asarray(L0_ffn_w_gate).astype(f32).T),
        "fu_t": np.ascontiguousarray(np.asarray(L0_ffn_w_up).astype(f32).T),
        "fd_t": np.ascontiguousarray(np.asarray(L0_ffn_w_down).astype(f32).T),
        "eye": np.eye(P, dtype=f32),
        "hones": np.ones((1, T), f32),
        "ones_r": np.ones((1, P), f32),
        "ones_c": np.ones((P, 1), f32),
    }

    if "nc" not in _NC_CACHE:
        _NC_CACHE["nc"] = build_nc()
    nc = _NC_CACHE["nc"]

    in_maps = []
    for c in range(8):
        sl = slice(c * T, (c + 1) * T)
        m = dict(shared)
        m["x"] = np.ascontiguousarray(xr[sl])
        m["xt"] = np.ascontiguousarray(xr[sl].T)
        in_maps.append(m)

    res = run_bass_kernel_spmd(nc, in_maps, core_ids=list(range(8)))
    out = np.concatenate([np.asarray(r["out"]) for r in res.results], axis=0)
    return out.reshape(B, S, Dm).astype(np.float32)


# version 10
# speedup vs baseline: 1.0405x; 1.0405x over previous
import numpy as np
import concourse.bass as bass
import concourse.mybir as mybir
from concourse.bass_utils import run_bass_kernel_spmd

F32 = mybir.dt.float32
BF16 = mybir.dt.bfloat16
AL = mybir.AluOpType
AF = mybir.ActivationFunctionType
AX = mybir.AxisListType

P = 128
D = 1024
ND = D // P
DI = 2816
NI = DI // P
T = 8192
GT = 256
NG = T // GT
NTT = GT // P
SC = 16
L = 64
NCHK = T // L
SEQ = 2048
CPS = SEQ // L
MAGIC = 12582912.0
QMAX = 127.0


class Sem:
    __slots__ = ("h", "n")

    def __init__(self, nc, name):
        self.h = nc.alloc_semaphore(name)
        self.n = 0


class Lanes:

    def __init__(self, nc, name, n=8):
        self.sems = [Sem(nc, f"{name}{i}") for i in range(n)]
        self.i = 0

    def next(self):
        s = self.sems[self.i % len(self.sems)]
        self.i += 1
        s.n += 16
        return (s, s.n)

    def all_tokens(self):
        return [(s, s.n) for s in self.sems]


class Prog:
    ENGS = ("sync", "vector", "scalar", "tensor", "gpsimd")

    def __init__(self, sems):
        self.ops = {e: [] for e in self.ENGS}
        self.sems = sems
        self.last_wait = {e: {} for e in self.ENGS}

    def raw(self, eng, fn):
        self.ops[eng].append(fn)

    def op(self, eng, fn, inc=1):
        s = self.sems[eng]
        s.n += inc
        def w(e, _fn=fn, _s=s, _k=inc):
            _fn(e).then_inc(_s.h, _k)
        self.ops[eng].append(w)
        return s.n

    def dma(self, eng, lanes, fn):
        s, v = lanes.next()
        def w(e, _fn=fn, _s=s):
            _fn(e).then_inc(_s.h, 16)
        self.ops[eng].append(w)
        return (s, v)

    def wait_tok(self, eng, tok):
        if tok == 0 or tok is None:
            return
        s, v = tok
        self.wait(eng, s, v)

    def wait_lanes(self, eng, lanes):
        for s, v in lanes.all_tokens():
            self.wait(eng, s, v)

    def wait(self, eng, sem, val):
        lw = self.last_wait[eng]
        if lw.get(id(sem), -1) >= val:
            return
        lw[id(sem)] = val
        self.ops[eng].append(lambda e, _s=sem, _v=val: e.wait_ge(_s.h, _v))

    def emit(self, nc):
        with nc.Block() as block:
            for name in self.ENGS:
                ops = self.ops[name]
                def body(eng, _ops=ops):
                    for f in _ops:
                        f(eng)
                getattr(block, name)(body)


def build_nc():
    nc = bass.Bass("TRN2", target_bir_lowering=False)

    d_xt = nc.dram_tensor("xt", [D, T], F32, kind="ExternalInput")
    d_x = nc.dram_tensor("x", [T, D], F32, kind="ExternalInput")
    d_w2t = nc.dram_tensor("w2t", [D, D], F32, kind="ExternalInput")
    d_wgt = nc.dram_tensor("wgt", [D, D], F32, kind="ExternalInput")
    d_bmt = nc.dram_tensor("bmt", [D, SC], F32, kind="ExternalInput")
    d_cm = nc.dram_tensor("cm", [D, SC], F32, kind="ExternalInput")
    d_wot = nc.dram_tensor("wot", [D, D], F32, kind="ExternalInput")
    d_anat = nc.dram_tensor("anat", [SC, SC], F32, kind="ExternalInput")
    d_at = nc.dram_tensor("at", [SC, SC], F32, kind="ExternalInput")
    d_b2c = nc.dram_tensor("b2c", [P, ND], F32, kind="ExternalInput")
    d_bgc = nc.dram_tensor("bgc", [P, ND], F32, kind="ExternalInput")
    d_bout = nc.dram_tensor("bout", [1, D], F32, kind="ExternalInput")
    d_clipx = nc.dram_tensor("clipx", [1, 1], F32, kind="ExternalInput")
    d_clipd = nc.dram_tensor("clipd", [1, 1], F32, kind="ExternalInput")
    d_fg_n = nc.dram_tensor("fg_n", [DI, D], F32, kind="ExternalInput")
    d_fu_n = nc.dram_tensor("fu_n", [DI, D], F32, kind="ExternalInput")
    d_fd_n = nc.dram_tensor("fd_n", [D, DI], F32, kind="ExternalInput")
    d_fg_t = nc.dram_tensor("fg_t", [D, DI], F32, kind="ExternalInput")
    d_fu_t = nc.dram_tensor("fu_t", [D, DI], F32, kind="ExternalInput")
    d_fd_t = nc.dram_tensor("fd_t", [DI, D], F32, kind="ExternalInput")
    d_eye = nc.dram_tensor("eye", [P, P], F32, kind="ExternalInput")
    d_ones_r = nc.dram_tensor("ones_r", [1, P], F32, kind="ExternalInput")
    d_ones_c = nc.dram_tensor("ones_c", [P, 1], F32, kind="ExternalInput")
    d_hones = nc.dram_tensor("hones", [1, T], F32, kind="ExternalInput")
    d_out = nc.dram_tensor("out", [T, D], F32, kind="ExternalOutput")
    d_hsp = nc.dram_tensor("hsp", [NI, P, T], F32)
    d_wdq = nc.dram_tensor("wdq", [NI, P, D], BF16)
    d_cc_in = nc.dram_tensor("cc_in", [1, 8], F32)
    d_cc_out = nc.dram_tensor("cc_out", [1, 8], F32)

    A = nc.alloc_sbuf_tensor
    eye = A("s_eye", [P, P], F32)
    ones_r = A("s_ones_r", [1, P], F32)
    ones_c = A("s_ones_c", [P, 1], F32)
    i16b = A("i16b", [SC, SC], BF16)
    anat_b = A("anat_b", [SC, SC], BF16)
    at_b = A("at_b", [SC, SC], BF16)
    bmt_b = A("bmt_b", [P, ND * SC], BF16)
    qstk = A("qstk", [SC, L * SC], BF16)
    wct = A("wct", [SC + 1, D], BF16)
    b2c = A("s_b2c", [P, ND], F32)
    bgc = A("s_bgc", [P, ND], F32)
    sc1 = A("sc1", [1, 16], F32)
    rowt = A("rowt", [1, 2 * P], F32)
    uB = A("uB", [SC, T], BF16)
    hs = A("hs", [SC + 1, T], BF16)
    fstk = A("fstk", [SC, NCHK], BF16)
    scr16 = A("scr16", [SC, NCHK], F32)
    m_cols = A("m_cols", [P, 2 * NG], F32)
    rstd_cols = A("rstd_cols", [P, 2 * NG], F32)
    mrstd_cols = A("mrstd_cols", [P, 2 * NG], F32)
    amax_g = A("amax_g", [P, NI], F32)
    amax_u = A("amax_u", [P, NI], F32)
    amax_d = A("amax_d", [P, ND], F32)
    sxg = A("sxg", [P, NI], F32)
    sxu = A("sxu", [P, NI], F32)
    cS = A("cS", [P, 8], F32)
    mxc = A("mxc", [P, 2], F32)
    mhc = A("mhc", [P, 2], F32)
    habs_c = A("habs_c", [P, NI], F32)
    nwt = A("nwt", [P, 32], F32)

    CONSTS = [0.0, 0.5, 1.0, MAGIC]
    csb = A("csb", [P, len(CONSTS)], F32)
    for i, v in enumerate(CONSTS):
        nc.const_aps.register(F32, v, csb[:, i:i + 1])

    ps = [nc.alloc_psum_tensor(f"ps{i}", [P, 512], F32) for i in range(8)]

    sems = {e: Sem(nc, "s_" + e) for e in Prog.ENGS}
    dL = Lanes(nc, "dL", 8)
    dS = Lanes(nc, "dS", 4)
    dG = Lanes(nc, "dG", 4)
    scc = Sem(nc, "scc")

    wq_g = A("wq_g", [P, ND * DI], BF16)
    wq_u = A("wq_u", [P, ND * DI], BF16)
    wq_d = wq_u
    from contextlib import ExitStack
    _wstack = ExitStack()
    w2tb = _wstack.enter_context(nc.sbuf_tensor("w2tb", [P, ND * D], BF16))
    wgtb = _wstack.enter_context(nc.sbuf_tensor("wgtb", [P, ND * D], BF16))

    V, Sc, Tn, Gp, Sy = "vector", "scalar", "tensor", "gpsimd", "sync"

    def seq_chain(pg, eng, fns):
        s = pg.sems[eng]
        toks = []
        for fn in fns:
            pg.wait(eng, s, s.n)
            toks.append(pg.op(eng, fn))
        return toks[-1] if toks else s.n

    def col2row(pg, bank, col_ap_fn, dst_ap_fn, dep_v=None, dep_t=None):
        if dep_v is not None:
            pg.wait(Tn, sems[V], dep_v)
        if dep_t is not None:
            pg.wait(Tn, sems[Tn], dep_t)
        t1 = pg.op(Tn, lambda t, _c=col_ap_fn, _b=bank: t.transpose(ps[_b][0:1, 0:P], _c(), eye[:, :]))
        pg.wait(V, sems[Tn], t1)
        return pg.op(V, lambda v, _d=dst_ap_fn, _b=bank: v.tensor_copy(_d(), ps[_b][0:1, 0:P]))

    def bcast_row(pg, bank, row_ap_fn, dst_ap_fn, n, dep_v):
        pg.wait(Tn, sems[V], dep_v)
        t1 = pg.op(Tn, lambda t, _r=row_ap_fn, _b=bank, _n=n: t.matmul(
            ps[_b][:, 0:_n], ones_r[:, :], _r(), start=True, stop=True))
        pg.wait(V, sems[Tn], t1)
        return pg.op(V, lambda v, _d=dst_ap_fn, _b=bank, _n=n: v.tensor_copy(_d(), ps[_b][:, 0:_n]))

    pg = Prog(sems)
    with (
        nc.sbuf_tensor("ldb", [P, 2, D], F32) as ldb,
        nc.sbuf_tensor("ltb", [P, 2, DI // 2], F32) as ltb,
        nc.sbuf_tensor("cmb", [P, ND, SC], F32) as cmb,
        nc.sbuf_tensor("recb", [P, DI], F32) as recb,
        nc.sbuf_tensor("qtmp", [P, DI // 2], F32) as qtmp,
        nc.sbuf_tensor("rrow", [1, 512], F32) as rrow,
        nc.sbuf_tensor("dstage", [P, 2, D], BF16) as dstage,
        nc.sbuf_tensor("s_bout", [1, D], F32) as bout,
    ):
        ld0_toks = []
        for o, i_ in [
            (eye[:, :], d_eye[:, :]), (ones_r[:, :], d_ones_r[:, :]),
            (ones_c[:, :], d_ones_c[:, :]), (b2c[:, :], d_b2c[:, :]),
            (bgc[:, :], d_bgc[:, :]), (bout[:, :], d_bout[:, :]),
            (sc1[0:1, 0:1], d_clipx[:, :]), (sc1[0:1, 1:2], d_clipd[:, :]),
            (ldb[0:SC, 0, 0:SC], d_anat[:, :]),
            (ldb[0:SC, 0, 64:64 + SC], d_at[:, :]),
            (cmb[:, :, :], d_cm.rearrange("(c p) n -> p c n", p=P)[:, :, :]),
            (ldb[:, 1, 0:ND * SC].rearrange("p (c n) -> p c n", n=SC), d_bmt.rearrange("(c p) n -> p c n", p=P)[:, :, :]),
        ]:
            ld0_toks.append(pg.dma(Sy, dL, lambda e, _o=o, _i=i_: e.dma_start(out=_o, in_=_i)))
        for t_ in ld0_toks:
            pg.wait_tok(V, t_)
        for i, v in enumerate(CONSTS):
            pg.op(V, lambda vv, _i=i, _v=v: vv.memset(csb[:, _i:_i + 1], _v))
        pg.op(V, lambda v: v.memset(mxc[:, :], 0.0))
        pg.op(V, lambda v: v.memset(mhc[:, :], 0.0))
        pg.op(V, lambda v: v.tensor_copy(anat_b[:, :], ldb[0:SC, 0, 0:SC]))
        pg.op(V, lambda v: v.tensor_copy(at_b[:, :], ldb[0:SC, 0, 64:64 + SC]))
        pg.op(V, lambda v: v.tensor_copy(bmt_b[:, :], ldb[:, 1, 0:ND * SC]))
        v_eye = pg.op(V, lambda v: v.tensor_copy(i16b[:, :], eye[0:SC, 0:SC]))
        pg.wait(V, sems[V], v_eye)
        v_q1 = pg.op(V, lambda v: v.tensor_copy(qstk[:, 0:SC], at_b[:, :]))

        cast_done = {}
        for wi, (wsrc, wdst) in enumerate(((d_w2t, w2tb), (d_wgt, wgtb))):
            for k in range(ND):
                idx = wi * ND + k
                if idx < 2:
                    pg.wait(Sy, sems[V], v_q1)
                else:
                    pg.wait(Sy, sems[V], cast_done[idx - 2])
                lw = pg.dma(Sy, dL, lambda e, _w=wsrc, _k=k, _s=idx % 2: e.dma_start(
                    out=ldb[:, _s, :], in_=_w[_k * P:(_k + 1) * P, :]))
                pg.wait_tok(V, lw)
                cast_done[idx] = pg.op(V, lambda v, _d=wdst, _k=k, _s=idx % 2: v.tensor_copy(
                    _d[:, _k * D:(_k + 1) * D], ldb[:, _s, :]))

        mm_done = {}
        for half in range(2):
            for k in range(ND):
                idx = half * ND + k
                if idx >= 2:
                    pg.wait(Sy, sems[Tn], mm_done[idx - 2])
                lw = pg.dma(Sy, dL, lambda e, _k=k, _h=half, _s=idx % 2: e.dma_start(
                    out=ltb[:, _s, 0:512], in_=d_wot[_k * P:(_k + 1) * P, _h * 512:(_h + 1) * 512]))
                pg.wait_tok(Tn, lw)
                if half == 0 and k == 0:
                    for t_ in ld0_toks:
                        pg.wait_tok(Tn, t_)
                mm_done[idx] = pg.op(Tn, lambda t, _k=k, _h=half, _s=idx % 2: t.matmul(
                    ps[0][0:SC, 0:512], cmb[:, _k, :], ltb[:, _s, 0:512],
                    start=(_k == 0), stop=(_k == ND - 1)))
            pg.wait(V, sems[Tn], mm_done[half * ND + ND - 1])
            ev = pg.op(V, lambda v, _h=half: v.tensor_copy(wct[0:SC, _h * 512:(_h + 1) * 512], ps[0][0:SC, 0:512]))
            pg.wait(Tn, sems[V], ev)

        pg.dma(Gp, dG, lambda e: e.dma_start(out=wct[SC:SC + 1, :], in_=d_bout[:, :]))
        pg.dma(Gp, dG, lambda e: e.dma_start(out=hs[SC:SC + 1, :], in_=d_hones[:, :]))
        pg.wait(Tn, sems[V], v_q1)
        for k in range(1, L):
            tq = pg.op(Tn, lambda t, _k=k: t.matmul(
                ps[1][0:SC, 0:SC], anat_b[:, :], qstk[:, (_k - 1) * SC:_k * SC], start=True, stop=True))
            pg.wait(V, sems[Tn], tq)
            vq = pg.op(V, lambda v, _k=k: v.tensor_copy(qstk[:, _k * SC:(_k + 1) * SC], ps[1][0:SC, 0:SC]))
            pg.wait(Tn, sems[V], vq)

        ldb_free = cast_done[2 * ND - 1]
        ltb_free_t = mm_done[2 * ND - 1]
        ltb_free_v = 0
        for acol, wt_src, wq_dst, wn_src in (
            (amax_g, d_fg_t, wq_g, d_fg_n),
            (amax_u, d_fu_t, wq_u, d_fu_n),
        ):
            red_done = {}
            for oc in range(NI):
                if oc >= 2:
                    pg.wait(Sy, sems[V], red_done[oc - 2])
                else:
                    pg.wait(Sy, sems[V], ldb_free)
                lw = pg.dma(Sy, dL, lambda e, _w=wn_src, _o=oc: e.dma_start(
                    out=ldb[:, _o % 2, :], in_=_w[_o * P:(_o + 1) * P, :]))
                pg.wait_tok(V, lw)
                red_done[oc] = pg.op(V, lambda v, _a=acol, _o=oc: v.tensor_reduce(
                    _a[:, _o:_o + 1], ldb[:, _o % 2, :], AX.X, AL.max, apply_absolute_value=True))
            ldb_free = red_done[NI - 1]
            v_rc = seq_chain(pg, V, [
                lambda v, _a=acol: v.reciprocal(nwt[:, 0:NI], _a[:, :]),
                lambda v: v.tensor_scalar(nwt[:, 0:NI], nwt[:, 0:NI], QMAX, None, AL.mult),
            ])
            e3 = v_rc
            for j in range(6):
                n = min(512, DI - j * 512)
                for jj in range(n // P):
                    oc = j * 4 + jj
                    e3 = col2row(pg, 2, (lambda _o=oc: lambda: nwt[:, _o:_o + 1])(),
                                 (lambda _jj=jj: lambda: rrow[0:1, _jj * P:(_jj + 1) * P])(), dep_v=e3)
                e3 = bcast_row(pg, 2, (lambda _n=n: lambda: rrow[0:1, 0:_n])(),
                               (lambda _j=j, _n=n: lambda: recb[:, _j * 512:_j * 512 + _n])(), n, e3)
            v_recb = e3
            q_done = {}
            HW = DI // 2
            for k2 in range(2 * ND):
                k, hf = divmod(k2, 2)
                if k2 >= 2:
                    pg.wait(Sy, sems[V], q_done[k2 - 2])
                else:
                    pg.wait(Sy, sems[Tn], ltb_free_t)
                    pg.wait(Sy, sems[V], ltb_free_v)
                lw = pg.dma(Sy, dL, lambda e, _w=wt_src, _k=k, _h=hf, _s=k2 % 2: e.dma_start(
                    out=ltb[:, _s, :], in_=_w[_k * P:(_k + 1) * P, _h * HW:(_h + 1) * HW]))
                pg.wait_tok(V, lw)
                pg.wait(V, sems[V], v_recb)
                t1 = pg.op(V, lambda v, _s=k2 % 2, _h=hf: v.tensor_tensor(
                    qtmp[:, :], ltb[:, _s, :], recb[:, _h * HW:(_h + 1) * HW], AL.mult))
                pg.wait(V, sems[V], t1)
                q_done[k2] = pg.op(V, lambda v, _d=wq_dst, _k=k, _h=hf: v.tensor_scalar(
                    _d[:, _k * DI + _h * HW:_k * DI + (_h + 1) * HW], qtmp[:, :], MAGIC, MAGIC, AL.add, AL.subtract))
            ltb_free_v = q_done[2 * ND - 1]
        red_done = {}
        HW = DI // 2
        for oc2 in range(2 * ND):
            oc, hf = divmod(oc2, 2)
            if oc2 >= 2:
                pg.wait(Sy, sems[V], red_done[oc2 - 2])
            else:
                pg.wait(Sy, sems[V], ltb_free_v)
            lw = pg.dma(Sy, dL, lambda e, _o=oc, _h=hf, _s=oc2 % 2: e.dma_start(
                out=ltb[:, _s, :], in_=d_fd_n[_o * P:(_o + 1) * P, _h * HW:(_h + 1) * HW]))
            pg.wait_tok(V, lw)
            if hf == 0:
                red_done[oc2] = pg.op(V, lambda v, _o=oc, _s=oc2 % 2: v.tensor_reduce(
                    nwt[:, 8:9], ltb[:, _s, :], AX.X, AL.max, apply_absolute_value=True))
            else:
                r1 = pg.op(V, lambda v, _s=oc2 % 2: v.tensor_reduce(nwt[:, 9:10], ltb[:, _s, :], AX.X, AL.max, apply_absolute_value=True))
                pg.wait(V, sems[V], r1)
                red_done[oc2] = pg.op(V, lambda v, _o=oc: v.tensor_tensor(amax_d[:, _o:_o + 1], nwt[:, 8:9], nwt[:, 9:10], AL.max))
        e0 = seq_chain(pg, V, [
            lambda v: v.reciprocal(nwt[:, 0:ND], amax_d[:, :]),
            lambda v: v.tensor_scalar(nwt[:, 0:ND], nwt[:, 0:ND], QMAX, None, AL.mult),
            lambda v: v.tensor_scalar(nwt[:, 8:8 + ND], amax_d[:, :], 1.0 / QMAX, None, AL.mult),
        ])
        for oc in range(ND):
            e0 = col2row(pg, 2, (lambda _o=oc: lambda: nwt[:, _o:_o + 1])(),
                         (lambda _o=oc: lambda: rrow[0:1, _o % 4 * P:(_o % 4 + 1) * P])(), dep_v=e0)
            if oc % 4 == 3:
                e0 = bcast_row(pg, 2, lambda: rrow[0:1, 0:512],
                               (lambda _j=oc // 4: lambda: recb[:, _j * 512:(_j + 1) * 512])(), 512, e0)
        for oc in range(ND):
            e0 = col2row(pg, 2, (lambda _o=oc: lambda: nwt[:, 8 + _o:9 + _o])(),
                         (lambda _o=oc: lambda: rrow[0:1, _o % 4 * P:(_o % 4 + 1) * P])(), dep_v=e0)
            if oc % 4 == 3:
                e0 = bcast_row(pg, 2, lambda: rrow[0:1, 0:512],
                               (lambda _j=oc // 4: lambda: recb[:, D + _j * 512:D + (_j + 1) * 512])(), 512, e0)
        wq_stage = {}
        st_done = {}
        for oc in range(NI):
            if oc >= 2:
                pg.wait(Sy, sems[V], wq_stage[oc - 2])
            lw = pg.dma(Sy, dL, lambda e, _o=oc: e.dma_start(out=ltb[:, _o % 2, 0:D], in_=d_fd_t[_o * P:(_o + 1) * P, :]))
            pg.wait_tok(V, lw)
            pg.wait(V, sems[V], e0)
            if oc >= 2:
                pg.wait_tok(V, st_done.get(oc - 2, 0))
            t1 = pg.op(V, lambda v, _o=oc: v.tensor_tensor(qtmp[:, 0:D], ltb[:, _o % 2, 0:D], recb[:, 0:D], AL.mult))
            pg.wait(V, sems[V], t1)
            t2 = pg.op(V, lambda v: v.tensor_scalar(qtmp[:, 0:D], qtmp[:, 0:D], MAGIC, MAGIC, AL.add, AL.subtract))
            pg.wait(V, sems[V], t2)
            wq_stage[oc] = pg.op(V, lambda v, _o=oc: v.tensor_tensor(
                dstage[:, _o % 2, :], qtmp[:, 0:D], recb[:, D:2 * D], AL.mult))
            pg.wait(Sc, sems[V], wq_stage[oc])
            st_done[oc] = pg.dma(Sc, dS, lambda s, _o=oc: s.dma_start(out=d_wdq[_o, :, :], in_=dstage[:, _o % 2, :]))
        pg.emit(nc)

    pg = Prog(sems)
    with (
        nc.sbuf_tensor("xtg", [P, 3, ND, GT], F32) as xtg,
        nc.sbuf_tensor("xnb", [P, 2, ND, GT], BF16) as xnb,
        nc.sbuf_tensor("ubf", [P, ND, GT], BF16) as ubf,
        nc.sbuf_tensor("xnf", [P, 1, GT], F32) as xnf,
        nc.sbuf_tensor("xsq", [P, 1, GT], F32) as xsq,
        nc.sbuf_tensor("sgb", [P, 2, GT], F32) as sgb,
        nc.sbuf_tensor("mbA", [P, 2, GT], F32) as mbA,
        nc.sbuf_tensor("rstdbA", [P, 2, GT], F32) as rstdbA,
        nc.sbuf_tensor("rowS", [1, 2, GT], F32) as rowS,
    ):
        ld_done = {}
        sq_done = {}
        stat_mm = {}
        bcast_done = {}
        xn_done = {}
        pool_done = {}
        u_done = {}
        sg_done = {}
        attn_done = {}
        ub_evac = {}

        def emit_loads(g):
            if g >= 3:
                pg.wait(Sy, sems[V], ub_evac[g - 3])
            ld_done[g] = [pg.dma(Sy, dL, lambda e, _g=g, _dc=dc: e.dma_start(
                out=xtg[:, _g % 3, _dc, :], in_=d_xt[_dc * P:(_dc + 1) * P, _g * GT:(_g + 1) * GT]))
                for dc in range(ND)]

        stat_evac = {}

        def emit_stat_mms(g):
            b = g % 3
            for t_ in ld_done[g]:
                pg.wait_tok(Sc, t_)
                pg.wait_tok(Tn, t_)
            if g >= 1:
                pg.wait(Tn, sems[V], stat_evac.get(g - 1, 0))
            for dc in range(ND):
                if (g, dc - 1) in stat_mm or (g - 1, ND + dc - 1) in stat_mm:
                    key = (g, dc - 1) if dc >= 1 else (g - 1, ND + dc - 1)
                    pg.wait(Sc, sems[Tn], stat_mm[key])
                sq_done[(g, dc)] = pg.op(Sc, lambda s, _b=b, _dc=dc: s.activation(
                    xsq[:, 0, :], xtg[:, _b, _dc, :], AF.Square, bias=0.0, scale=1.0))
                pg.op(Tn, lambda t, _b=b, _dc=dc: t.matmul(
                    ps[5][0:1, 0:GT], ones_c[:, :], xtg[:, _b, _dc, :],
                    start=(_dc == 0), stop=(_dc == ND - 1)))
                pg.wait(Tn, sems[Sc], sq_done[(g, dc)])
                stat_mm[(g, dc)] = pg.op(Tn, lambda t, _dc=dc: t.matmul(
                    ps[6][0:1, 0:GT], ones_c[:, :], xsq[:, 0, :],
                    start=(_dc == 0), stop=(_dc == ND - 1)))
            stat_mm[g] = stat_mm[(g, ND - 1)]

        def emit_stat_chain(g):
            b = g % 3
            br = g % 2
            bm = g % 2
            g2 = 2 * g
            pg.wait(V, sems[Tn], stat_mm[g])
            e1 = pg.op(V, lambda v, _br=br: v.tensor_copy(rowS[0:1, _br, :], ps[5][0:1, 0:GT]))
            for j in range(2):
                col2row_inv = pg
                pg.wait(Tn, sems[V], e1)
                tt = pg.op(Tn, lambda t, _br=br, _j=j: t.transpose(
                    ps[7][0:P, 0:1], rowS[0:1, _br, _j * P:(_j + 1) * P], eye[0:1, 0:1]))
                pg.wait(V, sems[Tn], tt)
                e1 = pg.op(V, lambda v, _j=j: v.tensor_copy(nwt[:, _j:_j + 1], ps[7][0:P, 0:1]))
            pg.wait(V, sems[Tn], sems[Tn].n)
            e2 = pg.op(V, lambda v, _br=br: v.tensor_copy(rowS[0:1, _br, :], ps[6][0:1, 0:GT]))
            stat_evac[g] = e2
            for j in range(2):
                pg.wait(Tn, sems[V], e2)
                tt = pg.op(Tn, lambda t, _br=br, _j=j: t.transpose(
                    ps[7][0:P, 0:1], rowS[0:1, _br, _j * P:(_j + 1) * P], eye[0:1, 0:1]))
                pg.wait(V, sems[Tn], tt)
                e2 = pg.op(V, lambda v, _j=j: v.tensor_copy(nwt[:, 2 + _j:3 + _j], ps[7][0:P, 0:1]))
            seq_chain(pg, V, [
                lambda v, _g2=g2: v.tensor_scalar(m_cols[:, _g2:_g2 + 2], nwt[:, 0:2], 1.0 / D, None, AL.mult),
                lambda v, _g2=g2: v.tensor_tensor(nwt[:, 4:6], m_cols[:, _g2:_g2 + 2], m_cols[:, _g2:_g2 + 2], AL.mult),
                lambda v: v.scalar_tensor_tensor(nwt[:, 6:8], nwt[:, 2:4], 1.0 / D, nwt[:, 4:6], AL.mult, AL.subtract),
                lambda v: v.tensor_scalar(nwt[:, 6:8], nwt[:, 6:8], 1e-5, None, AL.add),
                lambda v: v.tensor_scalar(nwt[:, 8:10], nwt[:, 6:8], 0.5, 0.5, AL.mult, AL.add),
                lambda v: v.reciprocal(nwt[:, 10:12], nwt[:, 8:10]),
                lambda v: v.tensor_tensor(nwt[:, 12:14], nwt[:, 10:12], nwt[:, 10:12], AL.mult),
                lambda v: v.tensor_tensor(nwt[:, 12:14], nwt[:, 12:14], nwt[:, 6:8], AL.mult),
                lambda v: v.tensor_scalar(nwt[:, 12:14], nwt[:, 12:14], -0.5, 1.5, AL.mult, AL.add),
                lambda v: v.tensor_tensor(nwt[:, 10:12], nwt[:, 10:12], nwt[:, 12:14], AL.mult),
                lambda v: v.tensor_tensor(nwt[:, 12:14], nwt[:, 10:12], nwt[:, 10:12], AL.mult),
                lambda v: v.tensor_tensor(nwt[:, 12:14], nwt[:, 12:14], nwt[:, 6:8], AL.mult),
                lambda v: v.tensor_scalar(nwt[:, 12:14], nwt[:, 12:14], -0.5, 1.5, AL.mult, AL.add),
                lambda v: v.tensor_tensor(nwt[:, 10:12], nwt[:, 10:12], nwt[:, 12:14], AL.mult),
                lambda v: v.tensor_tensor(nwt[:, 12:14], nwt[:, 10:12], nwt[:, 10:12], AL.mult),
                lambda v: v.tensor_tensor(nwt[:, 12:14], nwt[:, 12:14], nwt[:, 6:8], AL.mult),
                lambda v: v.tensor_scalar(nwt[:, 12:14], nwt[:, 12:14], -0.5, 1.5, AL.mult, AL.add),
                lambda v: v.tensor_tensor(nwt[:, 10:12], nwt[:, 10:12], nwt[:, 12:14], AL.mult),
                lambda v: v.tensor_tensor(nwt[:, 12:14], nwt[:, 10:12], nwt[:, 10:12], AL.mult),
                lambda v: v.tensor_tensor(nwt[:, 12:14], nwt[:, 12:14], nwt[:, 6:8], AL.mult),
                lambda v: v.tensor_scalar(nwt[:, 12:14], nwt[:, 12:14], -0.5, 1.5, AL.mult, AL.add),
                lambda v, _g2=g2: v.tensor_tensor(rstd_cols[:, _g2:_g2 + 2], nwt[:, 10:12], nwt[:, 12:14], AL.mult),
                lambda v, _g2=g2: v.tensor_tensor(mrstd_cols[:, _g2:_g2 + 2], m_cols[:, _g2:_g2 + 2], rstd_cols[:, _g2:_g2 + 2], AL.mult),
            ])
            v_stats = sems[V].n
            if g >= 2:
                pg.wait(V, sems[Gp], pool_done.get((g - 2, ND - 1), 0))
            e3 = v_stats
            for j in range(2):
                e3 = col2row(pg, 7, (lambda _g2=g2, _j=j: lambda: m_cols[:, _g2 + _j:_g2 + _j + 1])(),
                             (lambda _br=br, _j=j: lambda: rowS[0:1, _br, _j * P:(_j + 1) * P])(), dep_v=e3)
            e3 = bcast_row(pg, 7, (lambda _br=br: lambda: rowS[0:1, _br, :])(),
                           (lambda _bm=bm: lambda: mbA[:, _bm, :])(), GT, e3)
            for j in range(2):
                e3 = col2row(pg, 7, (lambda _g2=g2, _j=j: lambda: rstd_cols[:, _g2 + _j:_g2 + _j + 1])(),
                             (lambda _br=br, _j=j: lambda: rowS[0:1, _br, _j * P:(_j + 1) * P])(), dep_v=e3)
            e3 = bcast_row(pg, 7, (lambda _br=br: lambda: rowS[0:1, _br, :])(),
                           (lambda _bm=bm: lambda: rstdbA[:, _bm, :])(), GT, e3)
            bcast_done[g] = e3

        def emit_attn(g):
            b = g % 3
            bx = g % 2
            for dc in range(ND):
                e_ = dc % 2
                idx = g * ND + dc
                pg.wait(Gp, sems[V], bcast_done[g])
                for t_ in ld_done[g]:
                    pg.wait_tok(Gp, t_)
                if idx >= 1:
                    pg.wait(Gp, sems[V], xn_done.get(divmod(idx - 1, ND), 0))
                p1 = pg.op(Gp, lambda p_, _b=b, _bm=g % 2, _dc=dc, _e=e_: p_.tensor_tensor(
                    xnf[:, 0, :], xtg[:, _b, _dc, :], mbA[:, _bm, :], AL.subtract))
                pg.wait(Gp, sems[Gp], p1)
                pool_done[(g, dc)] = pg.op(Gp, lambda p_, _bm=g % 2, _e=e_: p_.tensor_tensor(
                    xnf[:, 0, :], xnf[:, 0, :], rstdbA[:, _bm, :], AL.mult))
                pg.wait(V, sems[Gp], pool_done[(g, dc)])
                if idx >= 2:
                    pg.wait(V, sems[Tn], attn_done.get(g - 2, 0))
                c1 = pg.op(V, lambda v, _bx=bx, _dc=dc, _e=e_: v.tensor_copy(xnb[:, _bx, _dc, :], xnf[:, 0, :]))
                r1 = pg.op(V, lambda v, _e=e_: v.tensor_reduce(nwt[:, 14:15], xnf[:, 0, :], AX.X, AL.max, apply_absolute_value=True))
                pg.wait(V, sems[V], r1)
                pp = idx % 2
                mx = pg.op(V, lambda v, _pp=pp: v.tensor_tensor(
                    mxc[:, 1 - _pp:2 - _pp], mxc[:, _pp:_pp + 1], nwt[:, 14:15], AL.max))
                xn_done[(g, dc)] = mx
            for dc in range(ND):
                e_ = dc % 2
                idx = g * ND + dc
                psA, psB = (0, 1) if e_ == 0 else (2, 3)
                pg.wait(Tn, sems[V], xn_done[(g, ND - 1)])
                if idx >= 2:
                    prev = divmod(idx - 2, ND)
                    pg.wait(Tn, sems[V], u_done.get(prev, 0))
                    pg.wait(Tn, sems[Sc], sg_done.get(prev, 0))
                for k in range(ND):
                    pg.op(Tn, lambda t, _pA=psA, _bx=bx, _dc=dc, _k=k: t.matmul(
                        ps[_pA][:, 0:GT], w2tb[:, _k * D + _dc * P:_k * D + (_dc + 1) * P], xnb[:, _bx, _k, :],
                        start=(_k == 0), stop=(_k == ND - 1)))
                for k in range(ND):
                    pg.op(Tn, lambda t, _pB=psB, _bx=bx, _dc=dc, _k=k: t.matmul(
                        ps[_pB][:, 0:GT], wgtb[:, _k * D + _dc * P:_k * D + (_dc + 1) * P], xnb[:, _bx, _k, :],
                        start=(_k == 0), stop=(_k == ND - 1)))
                t_mm = sems[Tn].n
                pg.wait(Sc, sems[Tn], t_mm)
                if idx >= 2:
                    prev = divmod(idx - 2, ND)
                    pg.wait(Sc, sems[V], u_done.get(prev, 0))
                sg_done[(g, dc)] = pg.op(Sc, lambda s, _pB=psB, _dc=dc, _e=e_: s.activation(
                    sgb[:, _e, :], ps[_pB][:, 0:GT], AF.Sigmoid, bias=bgc[:, _dc:_dc + 1], scale=1.0))
                pg.wait(V, sems[Sc], sg_done[(g, dc)])
                if dc == 0:
                    pg.wait(V, sems[Tn], attn_done.get(g - 1, 0))
                u_done[(g, dc)] = pg.op(V, lambda v, _pA=psA, _dc=dc, _e=e_: v.scalar_tensor_tensor(
                    ubf[:, _dc, :], ps[_pA][:, 0:GT], b2c[:, _dc:_dc + 1], sgb[:, _e, :], AL.add, AL.mult))
            pg.wait(Tn, sems[V], u_done[(g, ND - 1)])
            if g >= 1:
                pg.wait(Tn, sems[V], ub_evac.get(g - 1, 0))
            for k in range(ND):
                pg.op(Tn, lambda t, _k=k: t.matmul(
                    ps[4][0:SC, 0:GT], bmt_b[:, _k * SC:(_k + 1) * SC], ubf[:, _k, :],
                    start=(_k == 0), stop=(_k == ND - 1)))
            attn_done[g] = sems[Tn].n
            pg.wait(V, sems[Tn], attn_done[g])
            ub_evac[g] = pg.op(V, lambda v, _g=g: v.tensor_copy(uB[:, _g * GT:(_g + 1) * GT], ps[4][0:SC, 0:GT]))

        for gg in (0, 1):
            emit_loads(gg)
            emit_stat_mms(gg)
            emit_stat_chain(gg)
        for g in range(NG):
            emit_attn(g)
            if g + 2 < NG:
                emit_loads(g + 2)
                emit_stat_mms(g + 2)
                emit_stat_chain(g + 2)

        fin = (NG * ND) % 2
        fx = col2row(pg, 7, lambda: mxc[:, fin:fin + 1], lambda: rowt[0:1, 0:P], dep_v=xn_done[(NG - 1, ND - 1)])
        pg.wait(V, sems[V], fx)
        vmx = pg.op(V, lambda v: v.tensor_reduce(sc1[0:1, 2:3], rowt[0:1, 0:P], AX.X, AL.max))
        pg.wait(Sc, sems[V], vmx)
        stg = pg.op(Sc, lambda s: s.activation(rowt[0:1, P:P + 8], ones_r[0:1, 0:8], AF.Identity, bias=0.0, scale=sc1[0:1, 2:3]))
        pg.wait(Gp, sems[Sc], stg)
        tk1 = pg.dma(Gp, dG, lambda e: e.dma_start(out=d_cc_in[:, :], in_=rowt[0:1, P:P + 8]))
        pg.wait_tok(Gp, tk1)
        cc_tok = pg.op(Gp, lambda e: e.collective_compute(
            "AllReduce", AL.max, replica_groups=[list(range(8))],
            ins=[d_cc_in[:, :]], outs=[d_cc_out[:, :]]))
        pg.wait(Gp, sems[Gp], cc_tok)
        tk2 = pg.dma(Gp, dG, lambda e: e.dma_start(out=sc1[0:1, 2:3], in_=d_cc_out[0:1, 0:1]))
        pg.wait_tok(Sc, tk2)
        sgm = pg.op(Sc, lambda s: s.activation(sc1[0:1, 3:4], sc1[0:1, 0:1], AF.Sigmoid, bias=0.0, scale=1.0))
        pg.wait(V, sems[Sc], sgm)
        seq_chain(pg, V, [
            lambda v: v.tensor_tensor(sc1[0:1, 4:5], sc1[0:1, 3:4], sc1[0:1, 2:3], AL.mult),
            lambda v: v.tensor_scalar(sc1[0:1, 4:5], sc1[0:1, 4:5], 1.0 / QMAX, None, AL.mult),
            lambda v: v.reciprocal(sc1[0:1, 5:6], sc1[0:1, 4:5]),
        ])
        vsx = sems[V].n
        pg.wait(Tn, sems[V], vsx)
        tb = pg.op(Tn, lambda t: t.matmul(ps[7][:, 0:2], ones_r[:, :], sc1[0:1, 4:6], start=True, stop=True))
        pg.wait(V, sems[Tn], tb)
        seq_chain(pg, V, [
            lambda v: v.tensor_copy(cS[:, 2:3], ps[7][:, 0:1]),
            lambda v: v.tensor_copy(cS[:, 0:1], ps[7][:, 1:2]),
            lambda v: v.tensor_scalar(cS[:, 1:2], ps[7][:, 1:2], -1.0, None, AL.mult),
            lambda v: v.tensor_scalar(sxg[:, :], amax_g[:, :], 1.0 / QMAX, None, AL.mult),
            lambda v: v.tensor_scalar_mul(sxg[:, :], sxg[:, :], cS[:, 2:3]),
            lambda v: v.tensor_scalar(sxu[:, :], amax_u[:, :], 1.0 / QMAX, None, AL.mult),
            lambda v: v.tensor_scalar_mul(sxu[:, :], sxu[:, :], cS[:, 2:3]),
        ])
        pg.emit(nc)
    _wstack.close()

    pg = Prog(sems)
    with (
        nc.sbuf_tensor("xtg2", [P, 3, ND, GT], F32) as xtg2,
        nc.sbuf_tensor("zq", [P, 2, ND, GT], BF16) as zq,
        nc.sbuf_tensor("yt", [P, 2, GT], F32) as yt,
        nc.sbuf_tensor("sg2", [P, 2, GT], F32) as sg2,
        nc.sbuf_tensor("hb", [P, 4, GT], F32) as hb,
        nc.sbuf_tensor("c1b", [P, 3, GT], F32) as c1b,
        nc.sbuf_tensor("c0b", [P, 3, GT], F32) as c0b,
        nc.sbuf_tensor("rowB", [1, 3, GT], F32) as rowB,
    ):
        ldB = {}
        c_done = {}
        z_done = {}
        gu_done = {}
        h_done = {}
        sil_done = {}
        spill = {}
        habs_done = {}
        scan_v = {}
        scan_emitted = [0]

        def emit_loadsB(g):
            if g >= 3:
                pg.wait(Sy, sems[V], z_done[g - 3])
            ldB[g] = [pg.dma(Sy, dL, lambda e, _g=g, _dc=dc: e.dma_start(
                out=xtg2[:, _g % 3, _dc, :], in_=d_xt[_dc * P:(_dc + 1) * P, _g * GT:(_g + 1) * GT]))
                for dc in range(ND)]

        c_tn = {}

        def emit_cchain(g):
            b = g % 3
            g2 = 2 * g
            o = 16 + 4 * (g % 2)
            pg.wait(V, sems[Tn], c_tn.get(g - 2, 0))
            e0 = seq_chain(pg, V, [
                lambda v, _g2=g2, _o=o: v.tensor_scalar_mul(nwt[:, _o:_o + 2], rstd_cols[:, _g2:_g2 + 2], cS[:, 0:1]),
                lambda v, _g2=g2, _o=o: v.tensor_scalar_mul(nwt[:, _o + 2:_o + 4], mrstd_cols[:, _g2:_g2 + 2], cS[:, 1:2]),
            ])
            if g >= 3:
                pg.wait(V, sems[Gp], c_done.get(("pool", g - 3), 0))
            e3 = e0
            for j in range(2):
                e3 = col2row(pg, 5, (lambda _j=j, _o=o: lambda: nwt[:, _o + _j:_o + _j + 1])(),
                             (lambda _b=b, _j=j: lambda: rowB[0:1, _b, _j * P:(_j + 1) * P])(), dep_v=e3)
            e3 = bcast_row(pg, 5, (lambda _b=b: lambda: rowB[0:1, _b, :])(),
                           (lambda _b=b: lambda: c1b[:, _b, :])(), GT, e3)
            for j in range(2):
                e3 = col2row(pg, 5, (lambda _j=j, _o=o: lambda: nwt[:, _o + 2 + _j:_o + 3 + _j])(),
                             (lambda _b=b, _j=j: lambda: rowB[0:1, _b, _j * P:(_j + 1) * P])(), dep_v=e3)
            e3 = bcast_row(pg, 5, (lambda _b=b: lambda: rowB[0:1, _b, :])(),
                           (lambda _b=b: lambda: c0b[:, _b, :])(), GT, e3)
            c_done[g] = e3
            c_tn[g] = sems[Tn].n

        def emit_xq(g):
            b = g % 3
            bz = g % 2
            for t_ in ldB[g]:
                pg.wait_tok(Gp, t_)
            pg.wait(Gp, sems[V], c_done[g])
            for dc in range(ND):
                e_ = dc % 2
                idx = g * ND + dc
                if idx >= 2:
                    pg.wait(Gp, sems[V], z_done.get(divmod(idx - 2, ND), 0))
                p1 = pg.op(Gp, lambda p_, _b=b, _dc=dc, _e=e_: p_.tensor_tensor(
                    yt[:, _e, :], xtg2[:, _b, _dc, :], c1b[:, _b, :], AL.mult))
                pg.wait(Gp, sems[Gp], p1)
                p2 = pg.op(Gp, lambda p_, _b=b, _e=e_: p_.tensor_tensor(
                    yt[:, _e, :], yt[:, _e, :], c0b[:, _b, :], AL.add))
                c_done[("pool", g)] = p2
                pg.wait(V, sems[Gp], p2)
                if idx >= 2 and dc < 2:
                    pg.wait(V, sems[Tn], gu_done.get((g - 2, NI - 1), 0))
                t1 = pg.op(V, lambda v, _e=e_: v.tensor_scalar(yt[:, _e, :], yt[:, _e, :], MAGIC, MAGIC, AL.add, AL.subtract))
                pg.wait(V, sems[V], t1)
                z_done[divmod(idx, ND)] = pg.op(V, lambda v, _bz=bz, _dc=dc, _e=e_: v.tensor_scalar(
                    zq[:, _bz, _dc, :], yt[:, _e, :], -128.0, 127.0, AL.max, AL.min))
            z_done[g] = z_done[(g, ND - 1)]

        def emit_gateup(g):
            b = g % 2
            pg.wait(Tn, sems[V], z_done[g])
            for oc in range(NI):
                e_ = oc % 2
                idx = g * NI + oc
                psG, psU = (0, 1) if e_ == 0 else (2, 3)
                if idx >= 2:
                    prev = divmod(idx - 2, NI)
                    pg.wait(Tn, sems[Sc], sil_done.get(prev, 0))
                    pg.wait(Tn, sems[V], h_done.get(prev, 0))
                for k in range(ND):
                    pg.op(Tn, lambda t, _p=psG, _b=b, _oc=oc, _k=k: t.matmul(
                        ps[_p][:, 0:GT], wq_g[:, _k * DI + _oc * P:_k * DI + (_oc + 1) * P], zq[:, _b, _k, :],
                        start=(_k == 0), stop=(_k == ND - 1)))
                for k in range(ND):
                    pg.op(Tn, lambda t, _p=psU, _b=b, _oc=oc, _k=k: t.matmul(
                        ps[_p][:, 0:GT], wq_u[:, _k * DI + _oc * P:_k * DI + (_oc + 1) * P], zq[:, _b, _k, :],
                        start=(_k == 0), stop=(_k == ND - 1)))
                gu_done[(g, oc)] = sems[Tn].n
                pg.wait(Sc, sems[Tn], gu_done[(g, oc)])
                if idx >= 2:
                    pg.wait(Sc, sems[V], h_done.get(divmod(idx - 2, NI), 0))
                sil_done[(g, oc)] = pg.op(Sc, lambda s, _p=psG, _oc=oc, _e=e_: s.activation(
                    sg2[:, _e, :], ps[_p][:, 0:GT], AF.Silu, bias=0.0, scale=sxg[:, _oc:_oc + 1]))
                hslot = idx % 4
                pg.wait(V, sems[Sc], sil_done[(g, oc)])
                if idx >= 4:
                    pg.wait_tok(V, spill.get(divmod(idx - 4, NI), 0))
                hv = pg.op(V, lambda v, _p=psU, _oc=oc, _e=e_, _h=hslot: v.scalar_tensor_tensor(
                    hb[:, _h, :], ps[_p][:, 0:GT], sxu[:, _oc:_oc + 1], sg2[:, _e, :], AL.mult, AL.mult))
                pg.wait(V, sems[V], hv)
                h_done[(g, oc)] = pg.op(V, lambda v, _oc=oc, _h=hslot: v.tensor_reduce(
                    habs_c[:, _oc:_oc + 1], hb[:, _h, :], AX.X, AL.max, apply_absolute_value=True))
                pg.wait(Sc, sems[V], h_done[(g, oc)])
                spill[(g, oc)] = pg.dma(Sc, dS, lambda s, _g=g, _oc=oc, _h=hslot: s.dma_start(
                    out=d_hsp[_oc, :, _g * GT:(_g + 1) * GT], in_=hb[:, _h, :]))
            pp = g % 2
            pg.wait(V, sems[V], h_done[(g, NI - 1)])
            r1 = pg.op(V, lambda v: v.tensor_reduce(nwt[:, 14:15], habs_c[:, :], AX.X, AL.max))
            pg.wait(V, sems[V], r1)
            habs_done[g] = pg.op(V, lambda v, _pp=pp: v.tensor_tensor(
                mhc[:, 1 - _pp:2 - _pp], mhc[:, _pp:_pp + 1], nwt[:, 14:15], AL.max))

        def emit_scan_steps(n):
            for _ in range(n):
                tstep = scan_emitted[0] + 1
                if tstep >= L:
                    return
                scan_emitted[0] = tstep
                pg.wait(Tn, sems[V], scan_v[tstep - 1])
                pg.op(Tn, lambda t, _t=tstep: t.matmul(
                    ps[4][0:SC, 0:NCHK], at_b[:, :], hs[0:SC, (_t - 1):T:L], start=True, stop=False))
                tq = pg.op(Tn, lambda t, _t=tstep: t.matmul(
                    ps[4][0:SC, 0:NCHK], i16b[:, :], uB[:, _t:T:L], start=False, stop=True))
                pg.wait(V, sems[Tn], tq)
                scan_v[tstep] = pg.op(V, lambda v, _t=tstep: v.tensor_copy(hs[0:SC, _t:T:L], ps[4][0:SC, 0:NCHK]))

        scan_v[0] = pg.op(V, lambda v: v.tensor_copy(hs[0:SC, 0:T:L], uB[:, 0:T:L]))

        for gg in (0, 1):
            emit_loadsB(gg)
            emit_cchain(gg)
        emit_xq(0)
        for g in range(NG):
            emit_gateup(g)
            if g + 1 < NG:
                emit_xq(g + 1)
            if g + 2 < NG:
                emit_loadsB(g + 2)
                emit_cchain(g + 2)
            emit_scan_steps(2)
        emit_scan_steps(L)
        pg.wait(V, sems[V], scan_v[L - 1])
        f0 = pg.op(V, lambda v: v.memset(fstk[:, 0:NCHK:CPS], 0.0))
        fprev = f0
        q64 = (L - 1) * SC
        for c in range(1, CPS):
            pg.wait(Tn, sems[V], fprev)
            pg.op(Tn, lambda t, _c=c: t.matmul(
                ps[4][0:SC, 0:4], qstk[:, q64:q64 + SC], fstk[:, (_c - 1):NCHK:CPS], start=True, stop=False))
            tq = pg.op(Tn, lambda t, _c=c: t.matmul(
                ps[4][0:SC, 0:4], i16b[:, :], hs[0:SC, ((_c - 1) * L + L - 1):T:SEQ], start=False, stop=True))
            pg.wait(V, sems[Tn], tq)
            fprev = pg.op(V, lambda v, _c=c: v.tensor_copy(fstk[:, _c:NCHK:CPS], ps[4][0:SC, 0:4]))
        corr_prev = fprev
        for tstep in range(L):
            pg.wait(Tn, sems[V], corr_prev)
            tq = pg.op(Tn, lambda t, _t=tstep: t.matmul(
                ps[4][0:SC, 0:NCHK], qstk[:, _t * SC:(_t + 1) * SC], fstk[:, :], start=True, stop=True))
            pg.wait(V, sems[Tn], tq)
            a1 = pg.op(V, lambda v, _t=tstep: v.tensor_tensor(scr16[:, :], hs[0:SC, _t:T:L], ps[4][0:SC, 0:NCHK], AL.add))
            pg.wait(V, sems[V], a1)
            corr_prev = pg.op(V, lambda v, _t=tstep: v.tensor_copy(hs[0:SC, _t:T:L], scr16[:, :]))

        finh = NG % 2
        fh = col2row(pg, 7, lambda: mhc[:, finh:finh + 1], lambda: rowt[0:1, 0:P], dep_v=habs_done[NG - 1])
        pg.wait(V, sems[V], fh)
        vmh = pg.op(V, lambda v: v.tensor_reduce(sc1[0:1, 2:3], rowt[0:1, 0:P], AX.X, AL.max))
        pg.wait(Sc, sems[V], vmh)
        stg = pg.op(Sc, lambda s: s.activation(rowt[0:1, P:P + 8], ones_r[0:1, 0:8], AF.Identity, bias=0.0, scale=sc1[0:1, 2:3]))
        pg.wait(Gp, sems[Sc], stg)
        tk1 = pg.dma(Gp, dG, lambda e: e.dma_start(out=d_cc_in[:, :], in_=rowt[0:1, P:P + 8]))
        pg.wait_tok(Gp, tk1)
        cc_tok = pg.op(Gp, lambda e: e.collective_compute(
            "AllReduce", AL.max, replica_groups=[list(range(8))],
            ins=[d_cc_in[:, :]], outs=[d_cc_out[:, :]]))
        pg.wait(Gp, sems[Gp], cc_tok)
        tk2 = pg.dma(Gp, dG, lambda e: e.dma_start(out=sc1[0:1, 2:3], in_=d_cc_out[0:1, 0:1]))
        pg.wait_tok(Sc, tk2)
        sgm = pg.op(Sc, lambda s: s.activation(sc1[0:1, 3:4], sc1[0:1, 1:2], AF.Sigmoid, bias=0.0, scale=1.0))
        pg.wait(V, sems[Sc], sgm)
        seq_chain(pg, V, [
            lambda v: v.tensor_tensor(sc1[0:1, 4:5], sc1[0:1, 3:4], sc1[0:1, 2:3], AL.mult),
            lambda v: v.tensor_scalar(sc1[0:1, 4:5], sc1[0:1, 4:5], 1.0 / QMAX, None, AL.mult),
            lambda v: v.reciprocal(sc1[0:1, 5:6], sc1[0:1, 4:5]),
        ])
        vsd = sems[V].n
        pg.wait(Tn, sems[V], vsd)
        tb = pg.op(Tn, lambda t: t.matmul(ps[7][:, 0:2], ones_r[:, :], sc1[0:1, 4:6], start=True, stop=True))
        pg.wait(V, sems[Tn], tb)
        seq_chain(pg, V, [
            lambda v: v.tensor_copy(cS[:, 6:7], ps[7][:, 0:1]),
            lambda v: v.tensor_copy(cS[:, 3:4], ps[7][:, 1:2]),
        ])
        pg.emit(nc)

    pg = Prog(sems)
    with (
        nc.sbuf_tensor("hre", [P, 2, 4, GT], F32) as hre,
        nc.sbuf_tensor("yq", [P, 2, 4 * GT], F32) as yq,
        nc.sbuf_tensor("zh", [P, 2, NI, GT], BF16) as zh,
        nc.sbuf_tensor("xres", [P, 2, D], F32) as xres,
        nc.sbuf_tensor("outb", [P, 2, D], F32) as outb,
    ):
        wdl = [pg.dma(Sy, dL, lambda e, _o=oc: e.dma_start(out=wq_d[:, _o * D:(_o + 1) * D], in_=d_wdq[_o, :, :]))
               for oc in range(NI)]
        for t_ in wdl:
            pg.wait_tok(Tn, t_)
        ldh = {}
        sA_done = {}
        zq_done = {}
        acc_done = {}
        ev_done = {}
        outd = {}

        BW = 4
        NB = (NI + BW - 1) // BW

        def emit_hq(g):
            b = g % 2
            for ib in range(NB):
                o0 = ib * BW
                nch = min(BW, NI - o0)
                idx = g * NB + ib
                e_ = idx % 2
                if idx >= 2:
                    pg.wait(Sy, sems[V], sA_done[idx - 2])
                ldh[idx] = pg.dma(Sy, dL, lambda e, _g=g, _o=o0, _n=nch, _s=idx % 2: e.dma_start(
                    out=hre[:, _s, 0:_n, :], in_=d_hsp[_o:_o + _n, :, _g * GT:(_g + 1) * GT].rearrange("c p t -> p c t")))
                Qe = V
                pg.wait_tok(Qe, ldh[idx])
                if idx >= 2:
                    pg.wait(Qe, sems[V], zq_done.get(idx - 2, (None, 0))[1] if False else 0)
                if idx >= 2:
                    pt = zq_done[idx - 2]
                    pg.wait(Qe, pt[0], pt[1])
                sA_done[idx] = pg.op(Qe, lambda v, _s=idx % 2, _e=e_, _n=nch: v.tensor_scalar(
                    yq[:, _e, 0:_n * GT], hre[:, _s, 0:_n, :].rearrange("p c t -> p (c t)"),
                    cS[:, 3:4], MAGIC, AL.mult, AL.add))
                pg.wait(Qe, pg.sems[Qe], sA_done[idx])
                if g >= 2 and ib == 0:
                    pg.wait(Qe, sems[Tn], acc_done.get((g - 2, NTT - 1), 0))
                t0 = pg.op(Qe, lambda v, _e=e_, _n=nch: v.tensor_scalar(
                    yq[:, _e, 0:_n * GT], yq[:, _e, 0:_n * GT], MAGIC, -128.0, AL.subtract, AL.max))
                pg.wait(Qe, pg.sems[Qe], t0)
                zq_done[idx] = (pg.sems[Qe], pg.op(Qe, lambda v, _b=b, _o=o0, _n=nch, _e=e_: v.tensor_scalar(
                    zh[:, _b, _o:_o + _n, :].rearrange("p c t -> p (c t)"), yq[:, _e, 0:_n * GT],
                    127.0, cS[:, 6:7], AL.min, AL.mult)))
            zq_done[g] = zq_done[g * NB + NB - 1]

        def emit_tiles(g):
            b = g % 2
            pg.wait(Tn, zq_done[g][0], zq_done[g][1])
            if g >= 1:
                pg.wait(Sy, sems[V], ev_done.get((g - 1, NTT - 1), 0))
            ldx = [pg.dma(Sy, dL, lambda e, _g=g, _b=b, _tt=tt: e.dma_start(
                out=xres[:, (_g * NTT + _tt) % 2, :], in_=d_x[(_g * NTT + _tt) * P:(_g * NTT + _tt + 1) * P, :]))
                for tt in range(NTT)]
            for tt in range(NTT):
                ti = g * NTT + tt
                pA, pB = (0, 1) if tt == 0 else (2, 3)
                if ti >= 2:
                    pg.wait(Tn, sems[V], ev_done.get((g - 1, tt), 0))
                for half, bk in ((0, pA), (1, pB)):
                    pg.op(Tn, lambda t, _bk=bk, _h=half, _ti=ti: t.matmul(
                        ps[_bk][:, 0:512], hs[0:SC + 1, _ti * P:(_ti + 1) * P], wct[0:SC + 1, _h * 512:(_h + 1) * 512],
                        start=True, stop=False))
                    for oc in range(NI):
                        pg.op(Tn, lambda t, _bk=bk, _h=half, _b=b, _o=oc, _tt=tt: t.matmul(
                            ps[_bk][:, 0:512], zh[:, _b, _o, _tt * P:(_tt + 1) * P],
                            wq_d[:, _o * D + _h * 512:_o * D + _h * 512 + 512],
                            start=False, stop=(_o == NI - 1)))
                acc_done[(g, tt)] = sems[Tn].n
                pg.wait(V, sems[Tn], acc_done[(g, tt)])
                pg.wait_tok(V, ldx[tt])
                if ti >= 2:
                    pg.wait_tok(V, outd.get(ti - 2, 0))
                pg.op(V, lambda v, _pA=pA, _s=ti % 2: v.tensor_tensor(
                    outb[:, _s, 0:512], ps[_pA][:, 0:512], xres[:, _s, 0:512], AL.add))
                ev_done[(g, tt)] = pg.op(V, lambda v, _pB=pB, _s=ti % 2: v.tensor_tensor(
                    outb[:, _s, 512:1024], ps[_pB][:, 0:512], xres[:, _s, 512:1024], AL.add))
                pg.wait(Gp, sems[V], ev_done[(g, tt)])
                outd[ti] = pg.dma(Gp, dG, lambda e, _ti=ti, _s=ti % 2: e.dma_start(
                    out=d_out[_ti * P:(_ti + 1) * P, :], in_=outb[:, _s, :]))

        emit_hq(0)
        for g in range(NG):
            if g + 1 < NG:
                emit_hq(g + 1)
            emit_tiles(g)
        pg.wait_lanes(Gp, dG)
        pg.emit(nc)
    return nc


_NC_CACHE = {}


def kernel(L0_attn_x, L0_attn_W_in, L0_attn_b_in, L0_attn_A, L0_attn_Bm, L0_attn_Cm,
           L0_attn_W_gate, L0_attn_b_gate, L0_attn_W_out, L0_attn_b_out,
           L0_ffn_clip_x, L0_ffn_clip_down, L0_ffn_w_gate, L0_ffn_w_up, L0_ffn_w_down):
    x = np.asarray(L0_attn_x)
    B, S, Dm = x.shape
    xr = np.ascontiguousarray(x.reshape(B * S, Dm))
    f32 = np.float32

    w_in = np.asarray(L0_attn_W_in)
    shared = {
        "w2t": np.ascontiguousarray(w_in[Dm:2 * Dm, :].T.astype(f32)),
        "wgt": np.ascontiguousarray(np.asarray(L0_attn_W_gate).T.astype(f32)),
        "bmt": np.ascontiguousarray(np.asarray(L0_attn_Bm).T.astype(f32)),
        "cm": np.ascontiguousarray(np.asarray(L0_attn_Cm).astype(f32)),
        "wot": np.ascontiguousarray(np.asarray(L0_attn_W_out).T.astype(f32)),
        "anat": np.ascontiguousarray(np.asarray(L0_attn_A).astype(f32)),
        "at": np.ascontiguousarray(np.asarray(L0_attn_A).T.astype(f32)),
        "b2c": np.ascontiguousarray(np.asarray(L0_attn_b_in)[Dm:2 * Dm].astype(f32).reshape(ND, P).T),
        "bgc": np.ascontiguousarray(np.asarray(L0_attn_b_gate).astype(f32).reshape(ND, P).T),
        "bout": np.ascontiguousarray(np.asarray(L0_attn_b_out).astype(f32).reshape(1, Dm)),
        "clipx": np.asarray(L0_ffn_clip_x, f32).reshape(1, 1),
        "clipd": np.asarray(L0_ffn_clip_down, f32).reshape(1, 1),
        "fg_n": np.ascontiguousarray(np.asarray(L0_ffn_w_gate).astype(f32)),
        "fu_n": np.ascontiguousarray(np.asarray(L0_ffn_w_up).astype(f32)),
        "fd_n": np.ascontiguousarray(np.asarray(L0_ffn_w_down).astype(f32)),
        "fg_t": np.ascontiguousarray(np.asarray(L0_ffn_w_gate).astype(f32).T),
        "fu_t": np.ascontiguousarray(np.asarray(L0_ffn_w_up).astype(f32).T),
        "fd_t": np.ascontiguousarray(np.asarray(L0_ffn_w_down).astype(f32).T),
        "eye": np.eye(P, dtype=f32),
        "hones": np.ones((1, T), f32),
        "ones_r": np.ones((1, P), f32),
        "ones_c": np.ones((P, 1), f32),
    }

    if "nc" not in _NC_CACHE:
        _NC_CACHE["nc"] = build_nc()
    nc = _NC_CACHE["nc"]

    in_maps = []
    for c in range(8):
        sl = slice(c * T, (c + 1) * T)
        m = dict(shared)
        m["x"] = np.ascontiguousarray(xr[sl])
        m["xt"] = np.ascontiguousarray(xr[sl].T)
        in_maps.append(m)

    res = run_bass_kernel_spmd(nc, in_maps, core_ids=list(range(8)))
    out = np.concatenate([np.asarray(r["out"]) for r in res.results], axis=0)
    return out.reshape(B, S, Dm).astype(np.float32)


# version 12
# speedup vs baseline: 1.0765x; 1.0347x over previous
import numpy as np
import concourse.bass as bass
import concourse.mybir as mybir
from concourse.bass_utils import run_bass_kernel_spmd

F32 = mybir.dt.float32
BF16 = mybir.dt.bfloat16
AL = mybir.AluOpType
AF = mybir.ActivationFunctionType
AX = mybir.AxisListType

P = 128
D = 1024
ND = D // P
DI = 2816
NI = DI // P
T = 8192
GT = 256
NG = T // GT
NTT = GT // P
SC = 16
L = 64
NCHK = T // L
SEQ = 2048
CPS = SEQ // L
MAGIC = 12582912.0
QMAX = 127.0


class Sem:
    __slots__ = ("h", "n")

    def __init__(self, nc, name):
        self.h = nc.alloc_semaphore(name)
        self.n = 0


class Lanes:

    def __init__(self, nc, name, n=8):
        self.sems = [Sem(nc, f"{name}{i}") for i in range(n)]
        self.i = 0

    def next(self):
        s = self.sems[self.i % len(self.sems)]
        self.i += 1
        s.n += 16
        return (s, s.n)

    def all_tokens(self):
        return [(s, s.n) for s in self.sems]


class Prog:
    ENGS = ("sync", "vector", "scalar", "tensor", "gpsimd")

    def __init__(self, sems):
        self.ops = {e: [] for e in self.ENGS}
        self.sems = sems
        self.last_wait = {e: {} for e in self.ENGS}

    def raw(self, eng, fn):
        self.ops[eng].append(fn)

    def op(self, eng, fn, inc=1):
        s = self.sems[eng]
        s.n += inc
        def w(e, _fn=fn, _s=s, _k=inc):
            _fn(e).then_inc(_s.h, _k)
        self.ops[eng].append(w)
        return s.n

    def dma(self, eng, lanes, fn):
        s, v = lanes.next()
        def w(e, _fn=fn, _s=s):
            _fn(e).then_inc(_s.h, 16)
        self.ops[eng].append(w)
        return (s, v)

    def wait_tok(self, eng, tok):
        if tok == 0 or tok is None:
            return
        s, v = tok
        self.wait(eng, s, v)

    def wait_lanes(self, eng, lanes):
        for s, v in lanes.all_tokens():
            self.wait(eng, s, v)

    def wait(self, eng, sem, val):
        lw = self.last_wait[eng]
        if lw.get(id(sem), -1) >= val:
            return
        lw[id(sem)] = val
        self.ops[eng].append(lambda e, _s=sem, _v=val: e.wait_ge(_s.h, _v))

    def emit(self, nc):
        with nc.Block() as block:
            for name in self.ENGS:
                ops = self.ops[name]
                def body(eng, _ops=ops):
                    for f in _ops:
                        f(eng)
                getattr(block, name)(body)


def build_nc():
    nc = bass.Bass("TRN2", target_bir_lowering=False)

    d_xt = nc.dram_tensor("xt", [D, T], F32, kind="ExternalInput")
    d_x = nc.dram_tensor("x", [T, D], F32, kind="ExternalInput")
    d_w2t = nc.dram_tensor("w2t", [D, D], F32, kind="ExternalInput")
    d_wgt = nc.dram_tensor("wgt", [D, D], F32, kind="ExternalInput")
    d_bmt = nc.dram_tensor("bmt", [D, SC], F32, kind="ExternalInput")
    d_cm = nc.dram_tensor("cm", [D, SC], F32, kind="ExternalInput")
    d_wot = nc.dram_tensor("wot", [D, D], F32, kind="ExternalInput")
    d_anat = nc.dram_tensor("anat", [SC, SC], F32, kind="ExternalInput")
    d_at = nc.dram_tensor("at", [SC, SC], F32, kind="ExternalInput")
    d_b2c = nc.dram_tensor("b2c", [P, ND], F32, kind="ExternalInput")
    d_bgc = nc.dram_tensor("bgc", [P, ND], F32, kind="ExternalInput")
    d_bout = nc.dram_tensor("bout", [1, D], F32, kind="ExternalInput")
    d_clipx = nc.dram_tensor("clipx", [1, 1], F32, kind="ExternalInput")
    d_clipd = nc.dram_tensor("clipd", [1, 1], F32, kind="ExternalInput")
    d_fg_n = nc.dram_tensor("fg_n", [DI, D], F32, kind="ExternalInput")
    d_fu_n = nc.dram_tensor("fu_n", [DI, D], F32, kind="ExternalInput")
    d_fd_n = nc.dram_tensor("fd_n", [D, DI], F32, kind="ExternalInput")
    d_fg_t = nc.dram_tensor("fg_t", [D, DI], F32, kind="ExternalInput")
    d_fu_t = nc.dram_tensor("fu_t", [D, DI], F32, kind="ExternalInput")
    d_fd_t = nc.dram_tensor("fd_t", [DI, D], F32, kind="ExternalInput")
    d_eye = nc.dram_tensor("eye", [P, P], F32, kind="ExternalInput")
    d_ones_r = nc.dram_tensor("ones_r", [1, P], F32, kind="ExternalInput")
    d_ones_c = nc.dram_tensor("ones_c", [P, 1], F32, kind="ExternalInput")
    d_hones = nc.dram_tensor("hones", [1, T], F32, kind="ExternalInput")
    d_out = nc.dram_tensor("out", [T, D], F32, kind="ExternalOutput")
    d_hsp = nc.dram_tensor("hsp", [NI, P, T], F32)
    d_wdq = nc.dram_tensor("wdq", [NI, P, D], BF16)
    d_cc_in = nc.dram_tensor("cc_in", [1, 8], F32)
    d_cc_out = nc.dram_tensor("cc_out", [1, 8], F32)

    A = nc.alloc_sbuf_tensor
    eye = A("s_eye", [P, P], F32)
    ones_r = A("s_ones_r", [1, P], F32)
    ones_c = A("s_ones_c", [P, 1], F32)
    i16b = A("i16b", [SC, SC], BF16)
    anat_b = A("anat_b", [SC, SC], BF16)
    at_b = A("at_b", [SC, SC], BF16)
    bmt_b = A("bmt_b", [P, ND * SC], BF16)
    qstk = A("qstk", [SC, L * SC], BF16)
    wct = A("wct", [SC + 1, D], BF16)
    b2c = A("s_b2c", [P, ND], F32)
    bgc = A("s_bgc", [P, ND], F32)
    sc1 = A("sc1", [1, 16], F32)
    rowt = A("rowt", [1, P + 8], F32)
    uB = A("uB", [SC, T], BF16)
    hs = A("hs", [SC + 1, T], BF16)
    m_cols = A("m_cols", [P, 2 * NG], F32)
    rstd_cols = A("rstd_cols", [P, 2 * NG], F32)
    mrstd_cols = A("mrstd_cols", [P, 2 * NG], F32)
    amax_g = A("amax_g", [P, NI], F32)
    amax_u = A("amax_u", [P, NI], F32)
    amax_d = A("amax_d", [P, ND], F32)
    sxg = A("sxg", [P, NI], F32)
    sxu = A("sxu", [P, NI], F32)
    cS = A("cS", [P, 8], F32)
    mxc = A("mxc", [P, 2], F32)
    mhc = A("mhc", [P, 2], F32)
    habs_c = A("habs_c", [P, NI], F32)
    nwt = A("nwt", [P, 32], F32)

    CONSTS = [0.0, 0.5, 1.0, MAGIC]
    csb = A("csb", [P, len(CONSTS)], F32)
    for i, v in enumerate(CONSTS):
        nc.const_aps.register(F32, v, csb[:, i:i + 1])

    ps = [nc.alloc_psum_tensor(f"ps{i}", [P, 512], F32) for i in range(8)]

    sems = {e: Sem(nc, "s_" + e) for e in Prog.ENGS}
    dL = Lanes(nc, "dL", 8)
    dS = Lanes(nc, "dS", 4)
    dG = Lanes(nc, "dG", 4)
    scc = Sem(nc, "scc")

    wq_g = A("wq_g", [P, ND * DI], BF16)
    wq_u = A("wq_u", [P, ND * DI], BF16)
    wq_d = wq_u
    from contextlib import ExitStack
    _wstack = ExitStack()
    w2tb = _wstack.enter_context(nc.sbuf_tensor("w2tb", [P, ND * D], BF16))
    wgtb = _wstack.enter_context(nc.sbuf_tensor("wgtb", [P, ND * D], BF16))

    V, Sc, Tn, Gp, Sy = "vector", "scalar", "tensor", "gpsimd", "sync"

    def seq_chain(pg, eng, fns):
        s = pg.sems[eng]
        toks = []
        for fn in fns:
            pg.wait(eng, s, s.n)
            toks.append(pg.op(eng, fn))
        return toks[-1] if toks else s.n

    def col2row(pg, bank, col_ap_fn, dst_ap_fn, dep_v=None, dep_t=None):
        if dep_v is not None:
            pg.wait(Tn, sems[V], dep_v)
        if dep_t is not None:
            pg.wait(Tn, sems[Tn], dep_t)
        t1 = pg.op(Tn, lambda t, _c=col_ap_fn, _b=bank: t.transpose(ps[_b][0:1, 0:P], _c(), eye[:, :]))
        pg.wait(V, sems[Tn], t1)
        return pg.op(V, lambda v, _d=dst_ap_fn, _b=bank: v.tensor_copy(_d(), ps[_b][0:1, 0:P]))

    def bcast_row(pg, bank, row_ap_fn, dst_ap_fn, n, dep_v):
        pg.wait(Tn, sems[V], dep_v)
        t1 = pg.op(Tn, lambda t, _r=row_ap_fn, _b=bank, _n=n: t.matmul(
            ps[_b][:, 0:_n], ones_r[:, :], _r(), start=True, stop=True))
        pg.wait(V, sems[Tn], t1)
        return pg.op(V, lambda v, _d=dst_ap_fn, _b=bank, _n=n: v.tensor_copy(_d(), ps[_b][:, 0:_n]))

    pg = Prog(sems)
    with (
        nc.sbuf_tensor("ldb", [P, 2, D], F32) as ldb,
        nc.sbuf_tensor("ltb", [P, 2, DI // 2], F32) as ltb,
        nc.sbuf_tensor("cmb", [P, ND, SC], F32) as cmb,
        nc.sbuf_tensor("recb", [P, DI], F32) as recb,
        nc.sbuf_tensor("qtmp", [P, DI // 2], F32) as qtmp,
        nc.sbuf_tensor("rrow", [1, 512], F32) as rrow,
        nc.sbuf_tensor("dstage", [P, 2, D], BF16) as dstage,
        nc.sbuf_tensor("s_bout", [1, D], F32) as bout,
    ):
        ld0_toks = []
        for o, i_ in [
            (eye[:, :], d_eye[:, :]), (ones_r[:, :], d_ones_r[:, :]),
            (ones_c[:, :], d_ones_c[:, :]), (b2c[:, :], d_b2c[:, :]),
            (bgc[:, :], d_bgc[:, :]), (bout[:, :], d_bout[:, :]),
            (sc1[0:1, 0:1], d_clipx[:, :]), (sc1[0:1, 1:2], d_clipd[:, :]),
            (ldb[0:SC, 0, 0:SC], d_anat[:, :]),
            (ldb[0:SC, 0, 64:64 + SC], d_at[:, :]),
            (cmb[:, :, :], d_cm.rearrange("(c p) n -> p c n", p=P)[:, :, :]),
            (ldb[:, 1, 0:ND * SC].rearrange("p (c n) -> p c n", n=SC), d_bmt.rearrange("(c p) n -> p c n", p=P)[:, :, :]),
        ]:
            ld0_toks.append(pg.dma(Sy, dL, lambda e, _o=o, _i=i_: e.dma_start(out=_o, in_=_i)))
        for t_ in ld0_toks:
            pg.wait_tok(V, t_)
        for i, v in enumerate(CONSTS):
            pg.op(V, lambda vv, _i=i, _v=v: vv.memset(csb[:, _i:_i + 1], _v))
        pg.op(V, lambda v: v.memset(mxc[:, :], 0.0))
        pg.op(V, lambda v: v.memset(mhc[:, :], 0.0))
        pg.op(V, lambda v: v.tensor_copy(anat_b[:, :], ldb[0:SC, 0, 0:SC]))
        pg.op(V, lambda v: v.tensor_copy(at_b[:, :], ldb[0:SC, 0, 64:64 + SC]))
        pg.op(V, lambda v: v.tensor_copy(bmt_b[:, :], ldb[:, 1, 0:ND * SC]))
        v_eye = pg.op(V, lambda v: v.tensor_copy(i16b[:, :], eye[0:SC, 0:SC]))
        pg.wait(V, sems[V], v_eye)
        v_q1 = pg.op(V, lambda v: v.tensor_copy(qstk[:, 0:SC], at_b[:, :]))

        cast_done = {}
        for wi, (wsrc, wdst) in enumerate(((d_w2t, w2tb), (d_wgt, wgtb))):
            for k in range(ND):
                idx = wi * ND + k
                if idx < 2:
                    pg.wait(Sy, sems[V], v_q1)
                else:
                    pg.wait(Sy, sems[V], cast_done[idx - 2])
                lw = pg.dma(Sy, dL, lambda e, _w=wsrc, _k=k, _s=idx % 2: e.dma_start(
                    out=ldb[:, _s, :], in_=_w[_k * P:(_k + 1) * P, :]))
                pg.wait_tok(V, lw)
                cast_done[idx] = pg.op(V, lambda v, _d=wdst, _k=k, _s=idx % 2: v.tensor_copy(
                    _d[:, _k * D:(_k + 1) * D], ldb[:, _s, :]))

        mm_done = {}
        for half in range(2):
            for k in range(ND):
                idx = half * ND + k
                if idx >= 2:
                    pg.wait(Sy, sems[Tn], mm_done[idx - 2])
                lw = pg.dma(Sy, dL, lambda e, _k=k, _h=half, _s=idx % 2: e.dma_start(
                    out=ltb[:, _s, 0:512], in_=d_wot[_k * P:(_k + 1) * P, _h * 512:(_h + 1) * 512]))
                pg.wait_tok(Tn, lw)
                if half == 0 and k == 0:
                    for t_ in ld0_toks:
                        pg.wait_tok(Tn, t_)
                mm_done[idx] = pg.op(Tn, lambda t, _k=k, _h=half, _s=idx % 2: t.matmul(
                    ps[0][0:SC, 0:512], cmb[:, _k, :], ltb[:, _s, 0:512],
                    start=(_k == 0), stop=(_k == ND - 1)))
            pg.wait(V, sems[Tn], mm_done[half * ND + ND - 1])
            ev = pg.op(V, lambda v, _h=half: v.tensor_copy(wct[0:SC, _h * 512:(_h + 1) * 512], ps[0][0:SC, 0:512]))
            pg.wait(Tn, sems[V], ev)

        pg.dma(Gp, dG, lambda e: e.dma_start(out=wct[SC:SC + 1, :], in_=d_bout[:, :]))
        pg.dma(Gp, dG, lambda e: e.dma_start(out=hs[SC:SC + 1, :], in_=d_hones[:, :]))
        pg.wait(Tn, sems[V], v_q1)
        for k in range(1, L):
            tq = pg.op(Tn, lambda t, _k=k: t.matmul(
                ps[1][0:SC, 0:SC], anat_b[:, :], qstk[:, (_k - 1) * SC:_k * SC], start=True, stop=True))
            pg.wait(V, sems[Tn], tq)
            vq = pg.op(V, lambda v, _k=k: v.tensor_copy(qstk[:, _k * SC:(_k + 1) * SC], ps[1][0:SC, 0:SC]))
            pg.wait(Tn, sems[V], vq)

        ldb_free = cast_done[2 * ND - 1]
        ltb_free_t = mm_done[2 * ND - 1]
        ltb_free_v = 0
        for acol, wt_src, wq_dst, wn_src in (
            (amax_g, d_fg_t, wq_g, d_fg_n),
            (amax_u, d_fu_t, wq_u, d_fu_n),
        ):
            red_done = {}
            for oc in range(NI):
                if oc >= 2:
                    pg.wait(Sy, sems[V], red_done[oc - 2])
                else:
                    pg.wait(Sy, sems[V], ldb_free)
                lw = pg.dma(Sy, dL, lambda e, _w=wn_src, _o=oc: e.dma_start(
                    out=ldb[:, _o % 2, :], in_=_w[_o * P:(_o + 1) * P, :]))
                pg.wait_tok(V, lw)
                red_done[oc] = pg.op(V, lambda v, _a=acol, _o=oc: v.tensor_reduce(
                    _a[:, _o:_o + 1], ldb[:, _o % 2, :], AX.X, AL.max, apply_absolute_value=True))
            ldb_free = red_done[NI - 1]
            v_rc = seq_chain(pg, V, [
                lambda v, _a=acol: v.reciprocal(nwt[:, 0:NI], _a[:, :]),
                lambda v: v.tensor_scalar(nwt[:, 0:NI], nwt[:, 0:NI], QMAX, None, AL.mult),
            ])
            e3 = v_rc
            for j in range(6):
                n = min(512, DI - j * 512)
                for jj in range(n // P):
                    oc = j * 4 + jj
                    e3 = col2row(pg, 2, (lambda _o=oc: lambda: nwt[:, _o:_o + 1])(),
                                 (lambda _jj=jj: lambda: rrow[0:1, _jj * P:(_jj + 1) * P])(), dep_v=e3)
                e3 = bcast_row(pg, 2, (lambda _n=n: lambda: rrow[0:1, 0:_n])(),
                               (lambda _j=j, _n=n: lambda: recb[:, _j * 512:_j * 512 + _n])(), n, e3)
            v_recb = e3
            q_done = {}
            HW = DI // 2
            for k2 in range(2 * ND):
                k, hf = divmod(k2, 2)
                if k2 >= 2:
                    pg.wait(Sy, sems[V], q_done[k2 - 2])
                else:
                    pg.wait(Sy, sems[Tn], ltb_free_t)
                    pg.wait(Sy, sems[V], ltb_free_v)
                lw = pg.dma(Sy, dL, lambda e, _w=wt_src, _k=k, _h=hf, _s=k2 % 2: e.dma_start(
                    out=ltb[:, _s, :], in_=_w[_k * P:(_k + 1) * P, _h * HW:(_h + 1) * HW]))
                pg.wait_tok(V, lw)
                pg.wait(V, sems[V], v_recb)
                t1 = pg.op(V, lambda v, _s=k2 % 2, _h=hf: v.tensor_tensor(
                    qtmp[:, :], ltb[:, _s, :], recb[:, _h * HW:(_h + 1) * HW], AL.mult))
                pg.wait(V, sems[V], t1)
                q_done[k2] = pg.op(V, lambda v, _d=wq_dst, _k=k, _h=hf: v.tensor_scalar(
                    _d[:, _k * DI + _h * HW:_k * DI + (_h + 1) * HW], qtmp[:, :], MAGIC, MAGIC, AL.add, AL.subtract))
            ltb_free_v = q_done[2 * ND - 1]
        red_done = {}
        HW = DI // 2
        for oc2 in range(2 * ND):
            oc, hf = divmod(oc2, 2)
            if oc2 >= 2:
                pg.wait(Sy, sems[V], red_done[oc2 - 2])
            else:
                pg.wait(Sy, sems[V], ltb_free_v)
            lw = pg.dma(Sy, dL, lambda e, _o=oc, _h=hf, _s=oc2 % 2: e.dma_start(
                out=ltb[:, _s, :], in_=d_fd_n[_o * P:(_o + 1) * P, _h * HW:(_h + 1) * HW]))
            pg.wait_tok(V, lw)
            if hf == 0:
                red_done[oc2] = pg.op(V, lambda v, _o=oc, _s=oc2 % 2: v.tensor_reduce(
                    nwt[:, 8:9], ltb[:, _s, :], AX.X, AL.max, apply_absolute_value=True))
            else:
                r1 = pg.op(V, lambda v, _s=oc2 % 2: v.tensor_reduce(nwt[:, 9:10], ltb[:, _s, :], AX.X, AL.max, apply_absolute_value=True))
                pg.wait(V, sems[V], r1)
                red_done[oc2] = pg.op(V, lambda v, _o=oc: v.tensor_tensor(amax_d[:, _o:_o + 1], nwt[:, 8:9], nwt[:, 9:10], AL.max))
        e0 = seq_chain(pg, V, [
            lambda v: v.reciprocal(nwt[:, 0:ND], amax_d[:, :]),
            lambda v: v.tensor_scalar(nwt[:, 0:ND], nwt[:, 0:ND], QMAX, None, AL.mult),
            lambda v: v.tensor_scalar(nwt[:, 8:8 + ND], amax_d[:, :], 1.0 / QMAX, None, AL.mult),
        ])
        for oc in range(ND):
            e0 = col2row(pg, 2, (lambda _o=oc: lambda: nwt[:, _o:_o + 1])(),
                         (lambda _o=oc: lambda: rrow[0:1, _o % 4 * P:(_o % 4 + 1) * P])(), dep_v=e0)
            if oc % 4 == 3:
                e0 = bcast_row(pg, 2, lambda: rrow[0:1, 0:512],
                               (lambda _j=oc // 4: lambda: recb[:, _j * 512:(_j + 1) * 512])(), 512, e0)
        for oc in range(ND):
            e0 = col2row(pg, 2, (lambda _o=oc: lambda: nwt[:, 8 + _o:9 + _o])(),
                         (lambda _o=oc: lambda: rrow[0:1, _o % 4 * P:(_o % 4 + 1) * P])(), dep_v=e0)
            if oc % 4 == 3:
                e0 = bcast_row(pg, 2, lambda: rrow[0:1, 0:512],
                               (lambda _j=oc // 4: lambda: recb[:, D + _j * 512:D + (_j + 1) * 512])(), 512, e0)
        wq_stage = {}
        st_done = {}
        for oc in range(NI):
            if oc >= 2:
                pg.wait(Sy, sems[V], wq_stage[oc - 2])
            lw = pg.dma(Sy, dL, lambda e, _o=oc: e.dma_start(out=ltb[:, _o % 2, 0:D], in_=d_fd_t[_o * P:(_o + 1) * P, :]))
            pg.wait_tok(V, lw)
            pg.wait(V, sems[V], e0)
            if oc >= 2:
                pg.wait_tok(V, st_done.get(oc - 2, 0))
            t1 = pg.op(V, lambda v, _o=oc: v.tensor_tensor(qtmp[:, 0:D], ltb[:, _o % 2, 0:D], recb[:, 0:D], AL.mult))
            pg.wait(V, sems[V], t1)
            t2 = pg.op(V, lambda v: v.tensor_scalar(qtmp[:, 0:D], qtmp[:, 0:D], MAGIC, MAGIC, AL.add, AL.subtract))
            pg.wait(V, sems[V], t2)
            wq_stage[oc] = pg.op(V, lambda v, _o=oc: v.tensor_tensor(
                dstage[:, _o % 2, :], qtmp[:, 0:D], recb[:, D:2 * D], AL.mult))
            pg.wait(Sc, sems[V], wq_stage[oc])
            st_done[oc] = pg.dma(Sc, dS, lambda s, _o=oc: s.dma_start(out=d_wdq[_o, :, :], in_=dstage[:, _o % 2, :]))
        pg.emit(nc)

    pg = Prog(sems)
    with (
        nc.sbuf_tensor("xtg", [P, 3, ND, GT], F32) as xtg,
        nc.sbuf_tensor("xnb", [P, 2, ND, GT], BF16) as xnb,
        nc.sbuf_tensor("ubf", [P, ND, GT], BF16) as ubf,
        nc.sbuf_tensor("xnf", [P, 1, GT], F32) as xnf,
        nc.sbuf_tensor("xsq", [P, 2, GT], F32) as xsq,
        nc.sbuf_tensor("sgb", [P, 2, GT], F32) as sgb,
        nc.sbuf_tensor("mbA", [P, 2, GT], F32) as mbA,
        nc.sbuf_tensor("rstdbA", [P, 2, GT], F32) as rstdbA,
        nc.sbuf_tensor("rowS", [1, 2, GT], F32) as rowS,
    ):
        ld_done = {}
        sq_done = {}
        stat_mm = {}
        bcast_done = {}
        xn_done = {}
        pool_done = {}
        u_done = {}
        sg_done = {}
        attn_done = {}
        ub_evac = {}

        def emit_loads(g):
            if g >= 3:
                pg.wait(Sy, sems[V], ub_evac[g - 3])
            ld_done[g] = [pg.dma(Sy, dL, lambda e, _g=g, _dc=dc: e.dma_start(
                out=xtg[:, _g % 3, _dc, :], in_=d_xt[_dc * P:(_dc + 1) * P, _g * GT:(_g + 1) * GT]))
                for dc in range(ND)]

        stat_evac = {}

        def emit_stat_mms(g):
            b = g % 3
            for t_ in ld_done[g]:
                pg.wait_tok(V, t_)
                pg.wait_tok(Tn, t_)
            if g >= 1:
                pg.wait(Tn, sems[V], stat_evac.get(g - 1, 0))
            for dc in range(ND):
                if (g, dc - 2) in stat_mm or (g - 1, ND + dc - 2) in stat_mm:
                    key = (g, dc - 2) if dc >= 2 else (g - 1, ND + dc - 2)
                    pg.wait(V, sems[Tn], stat_mm[key])
                sq_done[(g, dc)] = pg.op(V, lambda v, _b=b, _dc=dc: v.tensor_tensor(
                    xsq[:, _dc % 2, :], xtg[:, _b, _dc, :], xtg[:, _b, _dc, :], AL.mult))
                pg.op(Tn, lambda t, _b=b, _dc=dc: t.matmul(
                    ps[5][0:1, 0:GT], ones_c[:, :], xtg[:, _b, _dc, :],
                    start=(_dc == 0), stop=(_dc == ND - 1)))
                pg.wait(Tn, sems[V], sq_done[(g, dc)])
                stat_mm[(g, dc)] = pg.op(Tn, lambda t, _dc=dc: t.matmul(
                    ps[6][0:1, 0:GT], ones_c[:, :], xsq[:, _dc % 2, :],
                    start=(_dc == 0), stop=(_dc == ND - 1)))
            stat_mm[g] = stat_mm[(g, ND - 1)]

        def emit_stat_chain(g):
            b = g % 3
            br = g % 2
            bm = g % 2
            g2 = 2 * g
            pg.wait(V, sems[Tn], stat_mm[g])
            e1 = pg.op(V, lambda v, _br=br: v.tensor_copy(rowS[0:1, _br, :], ps[5][0:1, 0:GT]))
            for j in range(2):
                col2row_inv = pg
                pg.wait(Tn, sems[V], e1)
                tt = pg.op(Tn, lambda t, _br=br, _j=j: t.transpose(
                    ps[7][0:P, 0:1], rowS[0:1, _br, _j * P:(_j + 1) * P], eye[0:1, 0:1]))
                pg.wait(V, sems[Tn], tt)
                e1 = pg.op(V, lambda v, _j=j: v.tensor_copy(nwt[:, _j:_j + 1], ps[7][0:P, 0:1]))
            pg.wait(V, sems[Tn], sems[Tn].n)
            e2 = pg.op(V, lambda v, _br=br: v.tensor_copy(rowS[0:1, _br, :], ps[6][0:1, 0:GT]))
            stat_evac[g] = e2
            for j in range(2):
                pg.wait(Tn, sems[V], e2)
                tt = pg.op(Tn, lambda t, _br=br, _j=j: t.transpose(
                    ps[7][0:P, 0:1], rowS[0:1, _br, _j * P:(_j + 1) * P], eye[0:1, 0:1]))
                pg.wait(V, sems[Tn], tt)
                e2 = pg.op(V, lambda v, _j=j: v.tensor_copy(nwt[:, 2 + _j:3 + _j], ps[7][0:P, 0:1]))
            seq_chain(pg, V, [
                lambda v, _g2=g2: v.tensor_scalar(m_cols[:, _g2:_g2 + 2], nwt[:, 0:2], 1.0 / D, None, AL.mult),
                lambda v, _g2=g2: v.tensor_tensor(nwt[:, 4:6], m_cols[:, _g2:_g2 + 2], m_cols[:, _g2:_g2 + 2], AL.mult),
                lambda v: v.scalar_tensor_tensor(nwt[:, 6:8], nwt[:, 2:4], 1.0 / D, nwt[:, 4:6], AL.mult, AL.subtract),
                lambda v: v.tensor_scalar(nwt[:, 6:8], nwt[:, 6:8], 1e-5, None, AL.add),
                lambda v: v.tensor_scalar(nwt[:, 8:10], nwt[:, 6:8], 0.5, 0.5, AL.mult, AL.add),
                lambda v: v.reciprocal(nwt[:, 10:12], nwt[:, 8:10]),
                lambda v: v.tensor_tensor(nwt[:, 12:14], nwt[:, 10:12], nwt[:, 10:12], AL.mult),
                lambda v: v.tensor_tensor(nwt[:, 12:14], nwt[:, 12:14], nwt[:, 6:8], AL.mult),
                lambda v: v.tensor_scalar(nwt[:, 12:14], nwt[:, 12:14], -0.5, 1.5, AL.mult, AL.add),
                lambda v: v.tensor_tensor(nwt[:, 10:12], nwt[:, 10:12], nwt[:, 12:14], AL.mult),
                lambda v: v.tensor_tensor(nwt[:, 12:14], nwt[:, 10:12], nwt[:, 10:12], AL.mult),
                lambda v: v.tensor_tensor(nwt[:, 12:14], nwt[:, 12:14], nwt[:, 6:8], AL.mult),
                lambda v: v.tensor_scalar(nwt[:, 12:14], nwt[:, 12:14], -0.5, 1.5, AL.mult, AL.add),
                lambda v: v.tensor_tensor(nwt[:, 10:12], nwt[:, 10:12], nwt[:, 12:14], AL.mult),
                lambda v: v.tensor_tensor(nwt[:, 12:14], nwt[:, 10:12], nwt[:, 10:12], AL.mult),
                lambda v: v.tensor_tensor(nwt[:, 12:14], nwt[:, 12:14], nwt[:, 6:8], AL.mult),
                lambda v: v.tensor_scalar(nwt[:, 12:14], nwt[:, 12:14], -0.5, 1.5, AL.mult, AL.add),
                lambda v: v.tensor_tensor(nwt[:, 10:12], nwt[:, 10:12], nwt[:, 12:14], AL.mult),
                lambda v: v.tensor_tensor(nwt[:, 12:14], nwt[:, 10:12], nwt[:, 10:12], AL.mult),
                lambda v: v.tensor_tensor(nwt[:, 12:14], nwt[:, 12:14], nwt[:, 6:8], AL.mult),
                lambda v: v.tensor_scalar(nwt[:, 12:14], nwt[:, 12:14], -0.5, 1.5, AL.mult, AL.add),
                lambda v, _g2=g2: v.tensor_tensor(rstd_cols[:, _g2:_g2 + 2], nwt[:, 10:12], nwt[:, 12:14], AL.mult),
                lambda v, _g2=g2: v.tensor_tensor(mrstd_cols[:, _g2:_g2 + 2], m_cols[:, _g2:_g2 + 2], rstd_cols[:, _g2:_g2 + 2], AL.mult),
            ])
            v_stats = sems[V].n
            if g >= 2:
                pg.wait(V, sems[Gp], pool_done.get((g - 2, ND - 1), 0))
            e3 = v_stats
            for j in range(2):
                e3 = col2row(pg, 7, (lambda _g2=g2, _j=j: lambda: m_cols[:, _g2 + _j:_g2 + _j + 1])(),
                             (lambda _br=br, _j=j: lambda: rowS[0:1, _br, _j * P:(_j + 1) * P])(), dep_v=e3)
            e3 = bcast_row(pg, 7, (lambda _br=br: lambda: rowS[0:1, _br, :])(),
                           (lambda _bm=bm: lambda: mbA[:, _bm, :])(), GT, e3)
            for j in range(2):
                e3 = col2row(pg, 7, (lambda _g2=g2, _j=j: lambda: rstd_cols[:, _g2 + _j:_g2 + _j + 1])(),
                             (lambda _br=br, _j=j: lambda: rowS[0:1, _br, _j * P:(_j + 1) * P])(), dep_v=e3)
            e3 = bcast_row(pg, 7, (lambda _br=br: lambda: rowS[0:1, _br, :])(),
                           (lambda _bm=bm: lambda: rstdbA[:, _bm, :])(), GT, e3)
            bcast_done[g] = e3

        def emit_attn(g):
            b = g % 3
            bx = g % 2
            for dc in range(ND):
                e_ = dc % 2
                idx = g * ND + dc
                pg.wait(Gp, sems[V], bcast_done[g])
                for t_ in ld_done[g]:
                    pg.wait_tok(Gp, t_)
                if idx >= 1:
                    pg.wait(Gp, sems[V], xn_done.get(divmod(idx - 1, ND), 0))
                p1 = pg.op(Gp, lambda p_, _b=b, _bm=g % 2, _dc=dc, _e=e_: p_.tensor_tensor(
                    xnf[:, 0, :], xtg[:, _b, _dc, :], mbA[:, _bm, :], AL.subtract))
                pg.wait(Gp, sems[Gp], p1)
                pool_done[(g, dc)] = pg.op(Gp, lambda p_, _bm=g % 2, _e=e_: p_.tensor_tensor(
                    xnf[:, 0, :], xnf[:, 0, :], rstdbA[:, _bm, :], AL.mult))
                pg.wait(V, sems[Gp], pool_done[(g, dc)])
                if idx >= 2:
                    pg.wait(V, sems[Tn], attn_done.get(g - 2, 0))
                c1 = pg.op(V, lambda v, _bx=bx, _dc=dc, _e=e_: v.tensor_copy(xnb[:, _bx, _dc, :], xnf[:, 0, :]))
                r1 = pg.op(V, lambda v, _e=e_: v.tensor_reduce(nwt[:, 14:15], xnf[:, 0, :], AX.X, AL.max, apply_absolute_value=True))
                pg.wait(V, sems[V], r1)
                pp = idx % 2
                mx = pg.op(V, lambda v, _pp=pp: v.tensor_tensor(
                    mxc[:, 1 - _pp:2 - _pp], mxc[:, _pp:_pp + 1], nwt[:, 14:15], AL.max))
                xn_done[(g, dc)] = mx
            for dc in range(ND):
                e_ = dc % 2
                idx = g * ND + dc
                psA, psB = (0, 1) if e_ == 0 else (2, 3)
                pg.wait(Tn, sems[V], xn_done[(g, ND - 1)])
                if idx >= 2:
                    prev = divmod(idx - 2, ND)
                    pg.wait(Tn, sems[V], u_done.get(prev, 0))
                    pg.wait(Tn, sems[Sc], sg_done.get(prev, 0))
                for k in range(ND):
                    pg.op(Tn, lambda t, _pA=psA, _bx=bx, _dc=dc, _k=k: t.matmul(
                        ps[_pA][:, 0:GT], w2tb[:, _k * D + _dc * P:_k * D + (_dc + 1) * P], xnb[:, _bx, _k, :],
                        start=(_k == 0), stop=(_k == ND - 1)))
                for k in range(ND):
                    pg.op(Tn, lambda t, _pB=psB, _bx=bx, _dc=dc, _k=k: t.matmul(
                        ps[_pB][:, 0:GT], wgtb[:, _k * D + _dc * P:_k * D + (_dc + 1) * P], xnb[:, _bx, _k, :],
                        start=(_k == 0), stop=(_k == ND - 1)))
                t_mm = sems[Tn].n
                pg.wait(Sc, sems[Tn], t_mm)
                if idx >= 2:
                    prev = divmod(idx - 2, ND)
                    pg.wait(Sc, sems[V], u_done.get(prev, 0))
                sg_done[(g, dc)] = pg.op(Sc, lambda s, _pB=psB, _dc=dc, _e=e_: s.activation(
                    sgb[:, _e, :], ps[_pB][:, 0:GT], AF.Sigmoid, bias=bgc[:, _dc:_dc + 1], scale=1.0))
                pg.wait(V, sems[Sc], sg_done[(g, dc)])
                if dc == 0:
                    pg.wait(V, sems[Tn], attn_done.get(g - 1, 0))
                u_done[(g, dc)] = pg.op(V, lambda v, _pA=psA, _dc=dc, _e=e_: v.scalar_tensor_tensor(
                    ubf[:, _dc, :], ps[_pA][:, 0:GT], b2c[:, _dc:_dc + 1], sgb[:, _e, :], AL.add, AL.mult))
            pg.wait(Tn, sems[V], u_done[(g, ND - 1)])
            if g >= 1:
                pg.wait(Tn, sems[V], ub_evac.get(g - 1, 0))
            for k in range(ND):
                pg.op(Tn, lambda t, _k=k: t.matmul(
                    ps[4][0:SC, 0:GT], bmt_b[:, _k * SC:(_k + 1) * SC], ubf[:, _k, :],
                    start=(_k == 0), stop=(_k == ND - 1)))
            attn_done[g] = sems[Tn].n
            pg.wait(V, sems[Tn], attn_done[g])
            ub_evac[g] = pg.op(V, lambda v, _g=g: v.tensor_copy(uB[:, _g * GT:(_g + 1) * GT], ps[4][0:SC, 0:GT]))

        for gg in (0, 1):
            emit_loads(gg)
            emit_stat_mms(gg)
            emit_stat_chain(gg)
        for g in range(NG):
            emit_attn(g)
            if g + 2 < NG:
                emit_loads(g + 2)
                emit_stat_mms(g + 2)
                emit_stat_chain(g + 2)

        fin = (NG * ND) % 2
        fx = col2row(pg, 7, lambda: mxc[:, fin:fin + 1], lambda: rowt[0:1, 0:P], dep_v=xn_done[(NG - 1, ND - 1)])
        pg.wait(V, sems[V], fx)
        vmx = pg.op(V, lambda v: v.tensor_reduce(sc1[0:1, 2:3], rowt[0:1, 0:P], AX.X, AL.max))
        pg.wait(Sc, sems[V], vmx)
        stg = pg.op(Sc, lambda s: s.activation(rowt[0:1, P:P + 8], ones_r[0:1, 0:8], AF.Identity, bias=0.0, scale=sc1[0:1, 2:3]))
        pg.wait(Gp, sems[Sc], stg)
        tk1 = pg.dma(Gp, dG, lambda e: e.dma_start(out=d_cc_in[:, :], in_=rowt[0:1, P:P + 8]))
        pg.wait_tok(Gp, tk1)
        cc_tok = pg.op(Gp, lambda e: e.collective_compute(
            "AllReduce", AL.max, replica_groups=[list(range(8))],
            ins=[d_cc_in[:, :]], outs=[d_cc_out[:, :]]))
        pg.wait(Gp, sems[Gp], cc_tok)
        tk2 = pg.dma(Gp, dG, lambda e: e.dma_start(out=sc1[0:1, 2:3], in_=d_cc_out[0:1, 0:1]))
        pg.wait_tok(Sc, tk2)
        sgm = pg.op(Sc, lambda s: s.activation(sc1[0:1, 3:4], sc1[0:1, 0:1], AF.Sigmoid, bias=0.0, scale=1.0))
        pg.wait(V, sems[Sc], sgm)
        seq_chain(pg, V, [
            lambda v: v.tensor_tensor(sc1[0:1, 4:5], sc1[0:1, 3:4], sc1[0:1, 2:3], AL.mult),
            lambda v: v.tensor_scalar(sc1[0:1, 4:5], sc1[0:1, 4:5], 1.0 / QMAX, None, AL.mult),
            lambda v: v.reciprocal(sc1[0:1, 5:6], sc1[0:1, 4:5]),
        ])
        vsx = sems[V].n
        pg.wait(Tn, sems[V], vsx)
        tb = pg.op(Tn, lambda t: t.matmul(ps[7][:, 0:2], ones_r[:, :], sc1[0:1, 4:6], start=True, stop=True))
        pg.wait(V, sems[Tn], tb)
        seq_chain(pg, V, [
            lambda v: v.tensor_copy(cS[:, 2:3], ps[7][:, 0:1]),
            lambda v: v.tensor_copy(cS[:, 0:1], ps[7][:, 1:2]),
            lambda v: v.tensor_scalar(cS[:, 1:2], ps[7][:, 1:2], -1.0, None, AL.mult),
            lambda v: v.tensor_scalar(sxg[:, :], amax_g[:, :], 1.0 / QMAX, None, AL.mult),
            lambda v: v.tensor_scalar_mul(sxg[:, :], sxg[:, :], cS[:, 2:3]),
            lambda v: v.tensor_scalar(sxu[:, :], amax_u[:, :], 1.0 / QMAX, None, AL.mult),
            lambda v: v.tensor_scalar_mul(sxu[:, :], sxu[:, :], cS[:, 2:3]),
        ])
        pg.emit(nc)
    _wstack.close()

    pg = Prog(sems)
    with (
        nc.sbuf_tensor("xtg2", [P, 3, ND, GT], F32) as xtg2,
        nc.sbuf_tensor("zq", [P, 2, ND, GT], BF16) as zq,
        nc.sbuf_tensor("yt", [P, 2, GT], F32) as yt,
        nc.sbuf_tensor("sg2", [P, 2, GT], F32) as sg2,
        nc.sbuf_tensor("hb", [P, 4, GT], F32) as hb,
        nc.sbuf_tensor("c1b", [P, 3, GT], F32) as c1b,
        nc.sbuf_tensor("c0b", [P, 3, GT], F32) as c0b,
        nc.sbuf_tensor("rowB", [1, 3, GT], F32) as rowB,
        nc.sbuf_tensor("fstk", [SC, NCHK], BF16) as fstk,
        nc.sbuf_tensor("scr16", [SC, NCHK], F32) as scr16,
    ):
        ldB = {}
        c_done = {}
        z_done = {}
        gu_done = {}
        h_done = {}
        sil_done = {}
        spill = {}
        habs_done = {}
        scan_v = {}
        scan_emitted = [0]

        def emit_loadsB(g):
            if g >= 3:
                pg.wait(Sy, sems[Gp], z_done[g - 3])
            ldB[g] = [pg.dma(Sy, dL, lambda e, _g=g, _dc=dc: e.dma_start(
                out=xtg2[:, _g % 3, _dc, :], in_=d_xt[_dc * P:(_dc + 1) * P, _g * GT:(_g + 1) * GT]))
                for dc in range(ND)]

        c_tn = {}

        def emit_cchain(g):
            b = g % 3
            g2 = 2 * g
            o = 16 + 4 * (g % 2)
            pg.wait(V, sems[Tn], c_tn.get(g - 2, 0))
            e0 = seq_chain(pg, V, [
                lambda v, _g2=g2, _o=o: v.tensor_scalar_mul(nwt[:, _o:_o + 2], rstd_cols[:, _g2:_g2 + 2], cS[:, 0:1]),
                lambda v, _g2=g2, _o=o: v.tensor_scalar_mul(nwt[:, _o + 2:_o + 4], mrstd_cols[:, _g2:_g2 + 2], cS[:, 1:2]),
            ])
            if g >= 3:
                pg.wait(V, sems[Gp], c_done.get(("pool", g - 3), 0))
            e3 = e0
            for j in range(2):
                e3 = col2row(pg, 5, (lambda _j=j, _o=o: lambda: nwt[:, _o + _j:_o + _j + 1])(),
                             (lambda _b=b, _j=j: lambda: rowB[0:1, _b, _j * P:(_j + 1) * P])(), dep_v=e3)
            e3 = bcast_row(pg, 5, (lambda _b=b: lambda: rowB[0:1, _b, :])(),
                           (lambda _b=b: lambda: c1b[:, _b, :])(), GT, e3)
            for j in range(2):
                e3 = col2row(pg, 5, (lambda _j=j, _o=o: lambda: nwt[:, _o + 2 + _j:_o + 3 + _j])(),
                             (lambda _b=b, _j=j: lambda: rowB[0:1, _b, _j * P:(_j + 1) * P])(), dep_v=e3)
            e3 = bcast_row(pg, 5, (lambda _b=b: lambda: rowB[0:1, _b, :])(),
                           (lambda _b=b: lambda: c0b[:, _b, :])(), GT, e3)
            c_done[g] = e3
            c_tn[g] = sems[Tn].n

        def emit_xq(g):
            b = g % 3
            bz = g % 2
            for t_ in ldB[g]:
                pg.wait_tok(Gp, t_)
            pg.wait(Gp, sems[V], c_done[g])
            for dc in range(ND):
                e_ = dc % 2
                idx = g * ND + dc
                if idx >= 2:
                    pg.wait(Gp, sems[Gp], z_done.get(divmod(idx - 2, ND), 0))
                p1 = pg.op(Gp, lambda p_, _b=b, _dc=dc, _e=e_: p_.tensor_tensor(
                    yt[:, _e, :], xtg2[:, _b, _dc, :], c1b[:, _b, :], AL.mult))
                pg.wait(Gp, sems[Gp], p1)
                p2 = pg.op(Gp, lambda p_, _b=b, _e=e_: p_.tensor_tensor(
                    yt[:, _e, :], yt[:, _e, :], c0b[:, _b, :], AL.add))
                c_done[("pool", g)] = p2
                pg.wait(Gp, sems[Gp], p2)
                if idx >= 2 and dc < 2:
                    pg.wait(Gp, sems[Tn], gu_done.get((g - 2, NI - 1), 0))
                t1 = pg.op(Gp, lambda v, _e=e_: v.tensor_scalar(yt[:, _e, :], yt[:, _e, :], MAGIC, MAGIC, AL.add, AL.subtract))
                pg.wait(Gp, sems[Gp], t1)
                z_done[divmod(idx, ND)] = pg.op(Gp, lambda v, _bz=bz, _dc=dc, _e=e_: v.tensor_scalar(
                    zq[:, _bz, _dc, :], yt[:, _e, :], -128.0, 127.0, AL.max, AL.min))
            z_done[g] = z_done[(g, ND - 1)]

        def emit_gateup(g):
            b = g % 2
            pg.wait(Tn, sems[Gp], z_done[g])
            for oc in range(NI):
                e_ = oc % 2
                idx = g * NI + oc
                psG, psU = (0, 1) if e_ == 0 else (2, 3)
                if idx >= 2:
                    prev = divmod(idx - 2, NI)
                    pg.wait(Tn, sems[Sc], sil_done.get(prev, 0))
                    pg.wait(Tn, sems[V], h_done.get(prev, 0))
                for k in range(ND):
                    pg.op(Tn, lambda t, _p=psG, _b=b, _oc=oc, _k=k: t.matmul(
                        ps[_p][:, 0:GT], wq_g[:, _k * DI + _oc * P:_k * DI + (_oc + 1) * P], zq[:, _b, _k, :],
                        start=(_k == 0), stop=(_k == ND - 1)))
                for k in range(ND):
                    pg.op(Tn, lambda t, _p=psU, _b=b, _oc=oc, _k=k: t.matmul(
                        ps[_p][:, 0:GT], wq_u[:, _k * DI + _oc * P:_k * DI + (_oc + 1) * P], zq[:, _b, _k, :],
                        start=(_k == 0), stop=(_k == ND - 1)))
                gu_done[(g, oc)] = sems[Tn].n
                pg.wait(Sc, sems[Tn], gu_done[(g, oc)])
                if idx >= 2:
                    pg.wait(Sc, sems[V], h_done.get(divmod(idx - 2, NI), 0))
                sil_done[(g, oc)] = pg.op(Sc, lambda s, _p=psG, _oc=oc, _e=e_: s.activation(
                    sg2[:, _e, :], ps[_p][:, 0:GT], AF.Silu, bias=0.0, scale=sxg[:, _oc:_oc + 1]))
                hslot = idx % 4
                pg.wait(V, sems[Sc], sil_done[(g, oc)])
                if idx >= 4:
                    pg.wait_tok(V, spill.get(divmod(idx - 4, NI), 0))
                hv = pg.op(V, lambda v, _p=psU, _oc=oc, _e=e_, _h=hslot: v.scalar_tensor_tensor(
                    hb[:, _h, :], ps[_p][:, 0:GT], sxu[:, _oc:_oc + 1], sg2[:, _e, :], AL.mult, AL.mult))
                pg.wait(V, sems[V], hv)
                h_done[(g, oc)] = pg.op(V, lambda v, _oc=oc, _h=hslot: v.tensor_reduce(
                    habs_c[:, _oc:_oc + 1], hb[:, _h, :], AX.X, AL.max, apply_absolute_value=True))
                pg.wait(Sc, sems[V], h_done[(g, oc)])
                spill[(g, oc)] = pg.dma(Sc, dS, lambda s, _g=g, _oc=oc, _h=hslot: s.dma_start(
                    out=d_hsp[_oc, :, _g * GT:(_g + 1) * GT], in_=hb[:, _h, :]))
            pp = g % 2
            pg.wait(V, sems[V], h_done[(g, NI - 1)])
            r1 = pg.op(V, lambda v: v.tensor_reduce(nwt[:, 14:15], habs_c[:, :], AX.X, AL.max))
            pg.wait(V, sems[V], r1)
            habs_done[g] = pg.op(V, lambda v, _pp=pp: v.tensor_tensor(
                mhc[:, 1 - _pp:2 - _pp], mhc[:, _pp:_pp + 1], nwt[:, 14:15], AL.max))

        def emit_scan_steps(n):
            for _ in range(n):
                tstep = scan_emitted[0] + 1
                if tstep >= L:
                    return
                scan_emitted[0] = tstep
                pg.wait(Tn, sems[V], scan_v[tstep - 1])
                pg.op(Tn, lambda t, _t=tstep: t.matmul(
                    ps[4][0:SC, 0:NCHK], at_b[:, :], hs[0:SC, (_t - 1):T:L], start=True, stop=False))
                tq = pg.op(Tn, lambda t, _t=tstep: t.matmul(
                    ps[4][0:SC, 0:NCHK], i16b[:, :], uB[:, _t:T:L], start=False, stop=True))
                pg.wait(V, sems[Tn], tq)
                scan_v[tstep] = pg.op(V, lambda v, _t=tstep: v.tensor_copy(hs[0:SC, _t:T:L], ps[4][0:SC, 0:NCHK]))

        scan_v[0] = pg.op(V, lambda v: v.tensor_copy(hs[0:SC, 0:T:L], uB[:, 0:T:L]))

        for gg in (0, 1):
            emit_loadsB(gg)
            emit_cchain(gg)
        emit_xq(0)
        for g in range(NG):
            emit_gateup(g)
            if g + 1 < NG:
                emit_xq(g + 1)
            if g + 2 < NG:
                emit_loadsB(g + 2)
                emit_cchain(g + 2)
            emit_scan_steps(2)
        emit_scan_steps(L)
        pg.wait(V, sems[V], scan_v[L - 1])
        f0 = pg.op(V, lambda v: v.memset(fstk[:, 0:NCHK:CPS], 0.0))
        fprev = f0
        q64 = (L - 1) * SC
        for c in range(1, CPS):
            pg.wait(Tn, sems[V], fprev)
            pg.op(Tn, lambda t, _c=c: t.matmul(
                ps[4][0:SC, 0:4], qstk[:, q64:q64 + SC], fstk[:, (_c - 1):NCHK:CPS], start=True, stop=False))
            tq = pg.op(Tn, lambda t, _c=c: t.matmul(
                ps[4][0:SC, 0:4], i16b[:, :], hs[0:SC, ((_c - 1) * L + L - 1):T:SEQ], start=False, stop=True))
            pg.wait(V, sems[Tn], tq)
            fprev = pg.op(V, lambda v, _c=c: v.tensor_copy(fstk[:, _c:NCHK:CPS], ps[4][0:SC, 0:4]))
        corr_prev = fprev
        for tstep in range(L):
            pg.wait(Tn, sems[V], corr_prev)
            tq = pg.op(Tn, lambda t, _t=tstep: t.matmul(
                ps[4][0:SC, 0:NCHK], qstk[:, _t * SC:(_t + 1) * SC], fstk[:, :], start=True, stop=True))
            pg.wait(V, sems[Tn], tq)
            a1 = pg.op(V, lambda v, _t=tstep: v.tensor_tensor(scr16[:, :], hs[0:SC, _t:T:L], ps[4][0:SC, 0:NCHK], AL.add))
            pg.wait(V, sems[V], a1)
            corr_prev = pg.op(V, lambda v, _t=tstep: v.tensor_copy(hs[0:SC, _t:T:L], scr16[:, :]))

        finh = NG % 2
        fh = col2row(pg, 7, lambda: mhc[:, finh:finh + 1], lambda: rowt[0:1, 0:P], dep_v=habs_done[NG - 1])
        pg.wait(V, sems[V], fh)
        vmh = pg.op(V, lambda v: v.tensor_reduce(sc1[0:1, 2:3], rowt[0:1, 0:P], AX.X, AL.max))
        pg.wait(Sc, sems[V], vmh)
        stg = pg.op(Sc, lambda s: s.activation(rowt[0:1, P:P + 8], ones_r[0:1, 0:8], AF.Identity, bias=0.0, scale=sc1[0:1, 2:3]))
        pg.wait(Gp, sems[Sc], stg)
        tk1 = pg.dma(Gp, dG, lambda e: e.dma_start(out=d_cc_in[:, :], in_=rowt[0:1, P:P + 8]))
        pg.wait_tok(Gp, tk1)
        cc_tok = pg.op(Gp, lambda e: e.collective_compute(
            "AllReduce", AL.max, replica_groups=[list(range(8))],
            ins=[d_cc_in[:, :]], outs=[d_cc_out[:, :]]))
        pg.wait(Gp, sems[Gp], cc_tok)
        tk2 = pg.dma(Gp, dG, lambda e: e.dma_start(out=sc1[0:1, 2:3], in_=d_cc_out[0:1, 0:1]))
        pg.wait_tok(Sc, tk2)
        sgm = pg.op(Sc, lambda s: s.activation(sc1[0:1, 3:4], sc1[0:1, 1:2], AF.Sigmoid, bias=0.0, scale=1.0))
        pg.wait(V, sems[Sc], sgm)
        seq_chain(pg, V, [
            lambda v: v.tensor_tensor(sc1[0:1, 4:5], sc1[0:1, 3:4], sc1[0:1, 2:3], AL.mult),
            lambda v: v.tensor_scalar(sc1[0:1, 4:5], sc1[0:1, 4:5], 1.0 / QMAX, None, AL.mult),
            lambda v: v.reciprocal(sc1[0:1, 5:6], sc1[0:1, 4:5]),
        ])
        vsd = sems[V].n
        pg.wait(Tn, sems[V], vsd)
        tb = pg.op(Tn, lambda t: t.matmul(ps[7][:, 0:2], ones_r[:, :], sc1[0:1, 4:6], start=True, stop=True))
        pg.wait(V, sems[Tn], tb)
        seq_chain(pg, V, [
            lambda v: v.tensor_copy(cS[:, 6:7], ps[7][:, 0:1]),
            lambda v: v.tensor_copy(cS[:, 3:4], ps[7][:, 1:2]),
        ])
        pg.emit(nc)

    pg = Prog(sems)
    with (
        nc.sbuf_tensor("hre", [P, 2, 4, GT], F32) as hre,
        nc.sbuf_tensor("yq", [P, 2, 4 * GT], F32) as yq,
        nc.sbuf_tensor("zh", [P, 2, NI, GT], BF16) as zh,
        nc.sbuf_tensor("xres", [P, 2, D], F32) as xres,
        nc.sbuf_tensor("outb", [P, 2, D], F32) as outb,
    ):
        wdl = [pg.dma(Sy, dL, lambda e, _o=oc: e.dma_start(out=wq_d[:, _o * D:(_o + 1) * D], in_=d_wdq[_o, :, :]))
               for oc in range(NI)]
        for t_ in wdl:
            pg.wait_tok(Tn, t_)
        ldh = {}
        sA_done = {}
        zq_done = {}
        acc_done = {}
        ev_done = {}
        outd = {}

        BW = 4
        NB = (NI + BW - 1) // BW

        def emit_hq(g):
            b = g % 2
            for ib in range(NB):
                o0 = ib * BW
                nch = min(BW, NI - o0)
                idx = g * NB + ib
                e_ = idx % 2
                if idx >= 2:
                    pg.wait(Sy, sems[V], sA_done[idx - 2])
                ldh[idx] = pg.dma(Sy, dL, lambda e, _g=g, _o=o0, _n=nch, _s=idx % 2: e.dma_start(
                    out=hre[:, _s, 0:_n, :], in_=d_hsp[_o:_o + _n, :, _g * GT:(_g + 1) * GT].rearrange("c p t -> p c t")))
                Qe = V
                pg.wait_tok(Qe, ldh[idx])
                if idx >= 2:
                    pg.wait(Qe, sems[V], zq_done.get(idx - 2, (None, 0))[1] if False else 0)
                if idx >= 2:
                    pt = zq_done[idx - 2]
                    pg.wait(Qe, pt[0], pt[1])
                sA_done[idx] = pg.op(Qe, lambda v, _s=idx % 2, _e=e_, _n=nch: v.tensor_scalar(
                    yq[:, _e, 0:_n * GT], hre[:, _s, 0:_n, :].rearrange("p c t -> p (c t)"),
                    cS[:, 3:4], MAGIC, AL.mult, AL.add))
                pg.wait(Qe, pg.sems[Qe], sA_done[idx])
                if g >= 2 and ib == 0:
                    pg.wait(Qe, sems[Tn], acc_done.get((g - 2, NTT - 1), 0))
                t0 = pg.op(Qe, lambda v, _e=e_, _n=nch: v.tensor_scalar(
                    yq[:, _e, 0:_n * GT], yq[:, _e, 0:_n * GT], MAGIC, -128.0, AL.subtract, AL.max))
                pg.wait(Qe, pg.sems[Qe], t0)
                zq_done[idx] = (pg.sems[Qe], pg.op(Qe, lambda v, _b=b, _o=o0, _n=nch, _e=e_: v.tensor_scalar(
                    zh[:, _b, _o:_o + _n, :].rearrange("p c t -> p (c t)"), yq[:, _e, 0:_n * GT],
                    127.0, cS[:, 6:7], AL.min, AL.mult)))
            zq_done[g] = zq_done[g * NB + NB - 1]

        def emit_tiles(g):
            b = g % 2
            pg.wait(Tn, zq_done[g][0], zq_done[g][1])
            if g >= 1:
                pg.wait(Sy, sems[V], ev_done.get((g - 1, NTT - 1), 0))
            ldx = [pg.dma(Sy, dL, lambda e, _g=g, _b=b, _tt=tt: e.dma_start(
                out=xres[:, (_g * NTT + _tt) % 2, :], in_=d_x[(_g * NTT + _tt) * P:(_g * NTT + _tt + 1) * P, :]))
                for tt in range(NTT)]
            for tt in range(NTT):
                ti = g * NTT + tt
                pA, pB = (0, 1) if tt == 0 else (2, 3)
                if ti >= 2:
                    pg.wait(Tn, sems[V], ev_done.get((g - 1, tt), 0))
                for half, bk in ((0, pA), (1, pB)):
                    pg.op(Tn, lambda t, _bk=bk, _h=half, _ti=ti: t.matmul(
                        ps[_bk][:, 0:512], hs[0:SC + 1, _ti * P:(_ti + 1) * P], wct[0:SC + 1, _h * 512:(_h + 1) * 512],
                        start=True, stop=False))
                    for oc in range(NI):
                        pg.op(Tn, lambda t, _bk=bk, _h=half, _b=b, _o=oc, _tt=tt: t.matmul(
                            ps[_bk][:, 0:512], zh[:, _b, _o, _tt * P:(_tt + 1) * P],
                            wq_d[:, _o * D + _h * 512:_o * D + _h * 512 + 512],
                            start=False, stop=(_o == NI - 1)))
                acc_done[(g, tt)] = sems[Tn].n
                pg.wait(V, sems[Tn], acc_done[(g, tt)])
                pg.wait_tok(V, ldx[tt])
                if ti >= 2:
                    pg.wait_tok(V, outd.get(ti - 2, 0))
                pg.op(V, lambda v, _pA=pA, _s=ti % 2: v.tensor_tensor(
                    outb[:, _s, 0:512], ps[_pA][:, 0:512], xres[:, _s, 0:512], AL.add))
                ev_done[(g, tt)] = pg.op(V, lambda v, _pB=pB, _s=ti % 2: v.tensor_tensor(
                    outb[:, _s, 512:1024], ps[_pB][:, 0:512], xres[:, _s, 512:1024], AL.add))
                pg.wait(Gp, sems[V], ev_done[(g, tt)])
                outd[ti] = pg.dma(Gp, dG, lambda e, _ti=ti, _s=ti % 2: e.dma_start(
                    out=d_out[_ti * P:(_ti + 1) * P, :], in_=outb[:, _s, :]))

        emit_hq(0)
        for g in range(NG):
            if g + 1 < NG:
                emit_hq(g + 1)
            emit_tiles(g)
        pg.wait_lanes(Gp, dG)
        pg.emit(nc)
    return nc


_NC_CACHE = {}


def kernel(L0_attn_x, L0_attn_W_in, L0_attn_b_in, L0_attn_A, L0_attn_Bm, L0_attn_Cm,
           L0_attn_W_gate, L0_attn_b_gate, L0_attn_W_out, L0_attn_b_out,
           L0_ffn_clip_x, L0_ffn_clip_down, L0_ffn_w_gate, L0_ffn_w_up, L0_ffn_w_down):
    x = np.asarray(L0_attn_x)
    B, S, Dm = x.shape
    xr = np.ascontiguousarray(x.reshape(B * S, Dm))
    f32 = np.float32

    w_in = np.asarray(L0_attn_W_in)
    shared = {
        "w2t": np.ascontiguousarray(w_in[Dm:2 * Dm, :].T.astype(f32)),
        "wgt": np.ascontiguousarray(np.asarray(L0_attn_W_gate).T.astype(f32)),
        "bmt": np.ascontiguousarray(np.asarray(L0_attn_Bm).T.astype(f32)),
        "cm": np.ascontiguousarray(np.asarray(L0_attn_Cm).astype(f32)),
        "wot": np.ascontiguousarray(np.asarray(L0_attn_W_out).T.astype(f32)),
        "anat": np.ascontiguousarray(np.asarray(L0_attn_A).astype(f32)),
        "at": np.ascontiguousarray(np.asarray(L0_attn_A).T.astype(f32)),
        "b2c": np.ascontiguousarray(np.asarray(L0_attn_b_in)[Dm:2 * Dm].astype(f32).reshape(ND, P).T),
        "bgc": np.ascontiguousarray(np.asarray(L0_attn_b_gate).astype(f32).reshape(ND, P).T),
        "bout": np.ascontiguousarray(np.asarray(L0_attn_b_out).astype(f32).reshape(1, Dm)),
        "clipx": np.asarray(L0_ffn_clip_x, f32).reshape(1, 1),
        "clipd": np.asarray(L0_ffn_clip_down, f32).reshape(1, 1),
        "fg_n": np.ascontiguousarray(np.asarray(L0_ffn_w_gate).astype(f32)),
        "fu_n": np.ascontiguousarray(np.asarray(L0_ffn_w_up).astype(f32)),
        "fd_n": np.ascontiguousarray(np.asarray(L0_ffn_w_down).astype(f32)),
        "fg_t": np.ascontiguousarray(np.asarray(L0_ffn_w_gate).astype(f32).T),
        "fu_t": np.ascontiguousarray(np.asarray(L0_ffn_w_up).astype(f32).T),
        "fd_t": np.ascontiguousarray(np.asarray(L0_ffn_w_down).astype(f32).T),
        "eye": np.eye(P, dtype=f32),
        "hones": np.ones((1, T), f32),
        "ones_r": np.ones((1, P), f32),
        "ones_c": np.ones((P, 1), f32),
    }

    if "nc" not in _NC_CACHE:
        _NC_CACHE["nc"] = build_nc()
    nc = _NC_CACHE["nc"]

    in_maps = []
    for c in range(8):
        sl = slice(c * T, (c + 1) * T)
        m = dict(shared)
        m["x"] = np.ascontiguousarray(xr[sl])
        m["xt"] = np.ascontiguousarray(xr[sl].T)
        in_maps.append(m)

    res = run_bass_kernel_spmd(nc, in_maps, core_ids=list(range(8)))
    out = np.concatenate([np.asarray(r["out"]) for r in res.results], axis=0)
    return out.reshape(B, S, Dm).astype(np.float32)


# version 14
# speedup vs baseline: 1.1056x; 1.0270x over previous
import numpy as np
import concourse.bass as bass
import concourse.mybir as mybir
from concourse.bass_utils import run_bass_kernel_spmd

F32 = mybir.dt.float32
BF16 = mybir.dt.bfloat16
AL = mybir.AluOpType
AF = mybir.ActivationFunctionType
AX = mybir.AxisListType

P = 128
D = 1024
ND = D // P
DI = 2816
NI = DI // P
T = 8192
GT = 256
NG = T // GT
NTT = GT // P
SC = 16
L = 64
NCHK = T // L
SEQ = 2048
CPS = SEQ // L
MAGIC = 12582912.0
QMAX = 127.0


class Sem:
    __slots__ = ("h", "n")

    def __init__(self, nc, name):
        self.h = nc.alloc_semaphore(name)
        self.n = 0


class Lanes:

    def __init__(self, nc, name, n=8):
        self.sems = [Sem(nc, f"{name}{i}") for i in range(n)]
        self.i = 0

    def next(self):
        s = self.sems[self.i % len(self.sems)]
        self.i += 1
        s.n += 16
        return (s, s.n)

    def all_tokens(self):
        return [(s, s.n) for s in self.sems]


class Prog:
    ENGS = ("sync", "vector", "scalar", "tensor", "gpsimd")

    def __init__(self, sems):
        self.ops = {e: [] for e in self.ENGS}
        self.sems = sems
        self.last_wait = {e: {} for e in self.ENGS}

    def raw(self, eng, fn):
        self.ops[eng].append(fn)

    def op(self, eng, fn, inc=1):
        s = self.sems[eng]
        s.n += inc
        def w(e, _fn=fn, _s=s, _k=inc):
            _fn(e).then_inc(_s.h, _k)
        self.ops[eng].append(w)
        return s.n

    def dma(self, eng, lanes, fn):
        s, v = lanes.next()
        def w(e, _fn=fn, _s=s):
            _fn(e).then_inc(_s.h, 16)
        self.ops[eng].append(w)
        return (s, v)

    def wait_tok(self, eng, tok):
        if tok == 0 or tok is None:
            return
        s, v = tok
        self.wait(eng, s, v)

    def wait_lanes(self, eng, lanes):
        for s, v in lanes.all_tokens():
            self.wait(eng, s, v)

    def wait(self, eng, sem, val):
        lw = self.last_wait[eng]
        if lw.get(id(sem), -1) >= val:
            return
        lw[id(sem)] = val
        self.ops[eng].append(lambda e, _s=sem, _v=val: e.wait_ge(_s.h, _v))

    def emit(self, nc):
        with nc.Block() as block:
            for name in self.ENGS:
                ops = self.ops[name]
                def body(eng, _ops=ops):
                    for f in _ops:
                        f(eng)
                getattr(block, name)(body)


def build_nc():
    nc = bass.Bass("TRN2", target_bir_lowering=False)

    d_xt = nc.dram_tensor("xt", [D, T], F32, kind="ExternalInput")
    d_x = nc.dram_tensor("x", [T, D], F32, kind="ExternalInput")
    d_w2t = nc.dram_tensor("w2t", [D, D], F32, kind="ExternalInput")
    d_wgt = nc.dram_tensor("wgt", [D, D], F32, kind="ExternalInput")
    d_bmt = nc.dram_tensor("bmt", [D, SC], F32, kind="ExternalInput")
    d_cm = nc.dram_tensor("cm", [D, SC], F32, kind="ExternalInput")
    d_wot = nc.dram_tensor("wot", [D, D], F32, kind="ExternalInput")
    d_anat = nc.dram_tensor("anat", [SC, SC], F32, kind="ExternalInput")
    d_at = nc.dram_tensor("at", [SC, SC], F32, kind="ExternalInput")
    d_b2c = nc.dram_tensor("b2c", [P, ND], F32, kind="ExternalInput")
    d_bgc = nc.dram_tensor("bgc", [P, ND], F32, kind="ExternalInput")
    d_bout = nc.dram_tensor("bout", [1, D], F32, kind="ExternalInput")
    d_clipx = nc.dram_tensor("clipx", [1, 1], F32, kind="ExternalInput")
    d_clipd = nc.dram_tensor("clipd", [1, 1], F32, kind="ExternalInput")
    d_fg_n = nc.dram_tensor("fg_n", [DI, D], F32, kind="ExternalInput")
    d_fu_n = nc.dram_tensor("fu_n", [DI, D], F32, kind="ExternalInput")
    d_fd_n = nc.dram_tensor("fd_n", [D, DI], F32, kind="ExternalInput")
    d_fg_t = nc.dram_tensor("fg_t", [D, DI], F32, kind="ExternalInput")
    d_fu_t = nc.dram_tensor("fu_t", [D, DI], F32, kind="ExternalInput")
    d_fd_t = nc.dram_tensor("fd_t", [DI, D], F32, kind="ExternalInput")
    d_eye = nc.dram_tensor("eye", [P, P], F32, kind="ExternalInput")
    d_ones_r = nc.dram_tensor("ones_r", [1, P], F32, kind="ExternalInput")
    d_ones_c = nc.dram_tensor("ones_c", [P, 1], F32, kind="ExternalInput")
    d_hones = nc.dram_tensor("hones", [1, T], F32, kind="ExternalInput")
    d_out = nc.dram_tensor("out", [T, D], F32, kind="ExternalOutput")
    d_hsp = nc.dram_tensor("hsp", [NI, P, T], F32)
    d_wdq = nc.dram_tensor("wdq", [NI, P, D], BF16)
    d_cc_in = nc.dram_tensor("cc_in", [1, 8], F32)
    d_cc_out = nc.dram_tensor("cc_out", [1, 8], F32)

    A = nc.alloc_sbuf_tensor
    eye = A("s_eye", [P, P], F32)
    ones_r = A("s_ones_r", [1, P], F32)
    ones_c = A("s_ones_c", [P, 1], F32)
    i16b = A("i16b", [SC, SC], BF16)
    anat_b = A("anat_b", [SC, SC], BF16)
    at_b = A("at_b", [SC, SC], BF16)
    bmt_b = A("bmt_b", [P, ND * SC], BF16)
    qstk = A("qstk", [SC, L * SC], BF16)
    wct = A("wct", [SC + 1, D], BF16)
    b2c = A("s_b2c", [P, ND], F32)
    bgc = A("s_bgc", [P, ND], F32)
    sc1 = A("sc1", [1, 16], F32)
    rowt = A("rowt", [1, P + 8], F32)
    uB = A("uB", [SC, T], BF16)
    hs = A("hs", [SC + 1, T], BF16)
    m_cols = A("m_cols", [P, 2 * NG], F32)
    rstd_cols = A("rstd_cols", [P, 2 * NG], F32)
    mrstd_cols = A("mrstd_cols", [P, 2 * NG], F32)
    amax_g = A("amax_g", [P, NI], F32)
    amax_u = A("amax_u", [P, NI], F32)
    amax_d = A("amax_d", [P, ND], F32)
    sxg = A("sxg", [P, NI], F32)
    sxu = A("sxu", [P, NI], F32)
    cS = A("cS", [P, 8], F32)
    mxc = A("mxc", [P, 2], F32)
    mhc = A("mhc", [P, 2], F32)
    habs_c = A("habs_c", [P, NI], F32)
    nwt = A("nwt", [P, 32], F32)

    CONSTS = [0.0, 0.5, 1.0, MAGIC]
    csb = A("csb", [P, len(CONSTS)], F32)
    for i, v in enumerate(CONSTS):
        nc.const_aps.register(F32, v, csb[:, i:i + 1])

    ps = [nc.alloc_psum_tensor(f"ps{i}", [P, 512], F32) for i in range(8)]

    sems = {e: Sem(nc, "s_" + e) for e in Prog.ENGS}
    dL = Lanes(nc, "dL", 8)
    dS = Lanes(nc, "dS", 4)
    dG = Lanes(nc, "dG", 4)
    scc = Sem(nc, "scc")

    wq_g = A("wq_g", [P, ND * DI], BF16)
    wq_u = A("wq_u", [P, ND * DI], BF16)
    wq_d = wq_u
    from contextlib import ExitStack
    _wstack = ExitStack()
    w2tb = _wstack.enter_context(nc.sbuf_tensor("w2tb", [P, ND * D], BF16))
    wgtb = _wstack.enter_context(nc.sbuf_tensor("wgtb", [P, ND * D], BF16))

    V, Sc, Tn, Gp, Sy = "vector", "scalar", "tensor", "gpsimd", "sync"

    def seq_chain(pg, eng, fns):
        s = pg.sems[eng]
        toks = []
        for fn in fns:
            pg.wait(eng, s, s.n)
            toks.append(pg.op(eng, fn))
        return toks[-1] if toks else s.n

    def col2row(pg, bank, col_ap_fn, dst_ap_fn, dep_v=None, dep_t=None):
        if dep_v is not None:
            pg.wait(Tn, sems[V], dep_v)
        if dep_t is not None:
            pg.wait(Tn, sems[Tn], dep_t)
        t1 = pg.op(Tn, lambda t, _c=col_ap_fn, _b=bank: t.transpose(ps[_b][0:1, 0:P], _c(), eye[:, :]))
        pg.wait(V, sems[Tn], t1)
        return pg.op(V, lambda v, _d=dst_ap_fn, _b=bank: v.tensor_copy(_d(), ps[_b][0:1, 0:P]))

    def bcast_row(pg, bank, row_ap_fn, dst_ap_fn, n, dep_v):
        pg.wait(Tn, sems[V], dep_v)
        t1 = pg.op(Tn, lambda t, _r=row_ap_fn, _b=bank, _n=n: t.matmul(
            ps[_b][:, 0:_n], ones_r[:, :], _r(), start=True, stop=True))
        pg.wait(V, sems[Tn], t1)
        return pg.op(V, lambda v, _d=dst_ap_fn, _b=bank, _n=n: v.tensor_copy(_d(), ps[_b][:, 0:_n]))

    pg = Prog(sems)
    with (
        nc.sbuf_tensor("ldb", [P, 2, D], F32) as ldb,
        nc.sbuf_tensor("ltb", [P, 2, DI // 2], F32) as ltb,
        nc.sbuf_tensor("cmb", [P, ND, SC], F32) as cmb,
        nc.sbuf_tensor("recb", [P, DI], F32) as recb,
        nc.sbuf_tensor("qtmp", [P, DI // 2], F32) as qtmp,
        nc.sbuf_tensor("rrow", [1, 512], F32) as rrow,
        nc.sbuf_tensor("dstage", [P, 2, D], BF16) as dstage,
        nc.sbuf_tensor("s_bout", [1, D], F32) as bout,
    ):
        ld0_toks = []
        for o, i_ in [
            (eye[:, :], d_eye[:, :]), (ones_r[:, :], d_ones_r[:, :]),
            (ones_c[:, :], d_ones_c[:, :]), (b2c[:, :], d_b2c[:, :]),
            (bgc[:, :], d_bgc[:, :]), (bout[:, :], d_bout[:, :]),
            (sc1[0:1, 0:1], d_clipx[:, :]), (sc1[0:1, 1:2], d_clipd[:, :]),
            (ldb[0:SC, 0, 0:SC], d_anat[:, :]),
            (ldb[0:SC, 0, 64:64 + SC], d_at[:, :]),
            (cmb[:, :, :], d_cm.rearrange("(c p) n -> p c n", p=P)[:, :, :]),
            (ldb[:, 1, 0:ND * SC].rearrange("p (c n) -> p c n", n=SC), d_bmt.rearrange("(c p) n -> p c n", p=P)[:, :, :]),
        ]:
            ld0_toks.append(pg.dma(Sy, dL, lambda e, _o=o, _i=i_: e.dma_start(out=_o, in_=_i)))
        for t_ in ld0_toks:
            pg.wait_tok(V, t_)
        for i, v in enumerate(CONSTS):
            pg.op(V, lambda vv, _i=i, _v=v: vv.memset(csb[:, _i:_i + 1], _v))
        pg.op(V, lambda v: v.memset(mxc[:, :], 0.0))
        pg.op(V, lambda v: v.memset(mhc[:, :], 0.0))
        pg.op(V, lambda v: v.tensor_copy(anat_b[:, :], ldb[0:SC, 0, 0:SC]))
        pg.op(V, lambda v: v.tensor_copy(at_b[:, :], ldb[0:SC, 0, 64:64 + SC]))
        pg.op(V, lambda v: v.tensor_copy(bmt_b[:, :], ldb[:, 1, 0:ND * SC]))
        v_eye = pg.op(V, lambda v: v.tensor_copy(i16b[:, :], eye[0:SC, 0:SC]))
        pg.wait(V, sems[V], v_eye)
        v_q1 = pg.op(V, lambda v: v.tensor_copy(qstk[:, 0:SC], at_b[:, :]))

        cast_done = {}
        for wi, (wsrc, wdst) in enumerate(((d_w2t, w2tb), (d_wgt, wgtb))):
            for k in range(ND):
                idx = wi * ND + k
                if idx < 2:
                    pg.wait(Sy, sems[V], v_q1)
                else:
                    pg.wait(Sy, sems[V], cast_done[idx - 2])
                lw = pg.dma(Sy, dL, lambda e, _w=wsrc, _k=k, _s=idx % 2: e.dma_start(
                    out=ldb[:, _s, :], in_=_w[_k * P:(_k + 1) * P, :]))
                pg.wait_tok(V, lw)
                cast_done[idx] = pg.op(V, lambda v, _d=wdst, _k=k, _s=idx % 2: v.tensor_copy(
                    _d[:, _k * D:(_k + 1) * D], ldb[:, _s, :]))

        mm_done = {}
        for half in range(2):
            for k in range(ND):
                idx = half * ND + k
                if idx >= 2:
                    pg.wait(Sy, sems[Tn], mm_done[idx - 2])
                lw = pg.dma(Sy, dL, lambda e, _k=k, _h=half, _s=idx % 2: e.dma_start(
                    out=ltb[:, _s, 0:512], in_=d_wot[_k * P:(_k + 1) * P, _h * 512:(_h + 1) * 512]))
                pg.wait_tok(Tn, lw)
                if half == 0 and k == 0:
                    for t_ in ld0_toks:
                        pg.wait_tok(Tn, t_)
                mm_done[idx] = pg.op(Tn, lambda t, _k=k, _h=half, _s=idx % 2: t.matmul(
                    ps[0][0:SC, 0:512], cmb[:, _k, :], ltb[:, _s, 0:512],
                    start=(_k == 0), stop=(_k == ND - 1)))
            pg.wait(V, sems[Tn], mm_done[half * ND + ND - 1])
            ev = pg.op(V, lambda v, _h=half: v.tensor_copy(wct[0:SC, _h * 512:(_h + 1) * 512], ps[0][0:SC, 0:512]))
            pg.wait(Tn, sems[V], ev)

        pg.dma(Gp, dG, lambda e: e.dma_start(out=wct[SC:SC + 1, :], in_=d_bout[:, :]))
        pg.dma(Gp, dG, lambda e: e.dma_start(out=hs[SC:SC + 1, :], in_=d_hones[:, :]))
        pg.wait(Tn, sems[V], v_q1)
        for k in range(1, L):
            tq = pg.op(Tn, lambda t, _k=k: t.matmul(
                ps[1][0:SC, 0:SC], anat_b[:, :], qstk[:, (_k - 1) * SC:_k * SC], start=True, stop=True))
            pg.wait(V, sems[Tn], tq)
            vq = pg.op(V, lambda v, _k=k: v.tensor_copy(qstk[:, _k * SC:(_k + 1) * SC], ps[1][0:SC, 0:SC]))
            pg.wait(Tn, sems[V], vq)

        ldb_free = cast_done[2 * ND - 1]
        ltb_free_t = mm_done[2 * ND - 1]
        ltb_free_v = 0
        for acol, wt_src, wq_dst, wn_src in (
            (amax_g, d_fg_t, wq_g, d_fg_n),
            (amax_u, d_fu_t, wq_u, d_fu_n),
        ):
            red_done = {}
            for oc in range(NI):
                if oc >= 2:
                    pg.wait(Sy, sems[V], red_done[oc - 2])
                else:
                    pg.wait(Sy, sems[V], ldb_free)
                lw = pg.dma(Sy, dL, lambda e, _w=wn_src, _o=oc: e.dma_start(
                    out=ldb[:, _o % 2, :], in_=_w[_o * P:(_o + 1) * P, :]))
                pg.wait_tok(V, lw)
                red_done[oc] = pg.op(V, lambda v, _a=acol, _o=oc: v.tensor_reduce(
                    _a[:, _o:_o + 1], ldb[:, _o % 2, :], AX.X, AL.max, apply_absolute_value=True))
            ldb_free = red_done[NI - 1]
            v_rc = seq_chain(pg, V, [
                lambda v, _a=acol: v.reciprocal(nwt[:, 0:NI], _a[:, :]),
                lambda v: v.tensor_scalar(nwt[:, 0:NI], nwt[:, 0:NI], QMAX, None, AL.mult),
            ])
            e3 = v_rc
            for j in range(6):
                n = min(512, DI - j * 512)
                for jj in range(n // P):
                    oc = j * 4 + jj
                    e3 = col2row(pg, 2, (lambda _o=oc: lambda: nwt[:, _o:_o + 1])(),
                                 (lambda _jj=jj: lambda: rrow[0:1, _jj * P:(_jj + 1) * P])(), dep_v=e3)
                e3 = bcast_row(pg, 2, (lambda _n=n: lambda: rrow[0:1, 0:_n])(),
                               (lambda _j=j, _n=n: lambda: recb[:, _j * 512:_j * 512 + _n])(), n, e3)
            v_recb = e3
            q_done = {}
            HW = DI // 2
            for k2 in range(2 * ND):
                k, hf = divmod(k2, 2)
                if k2 >= 2:
                    pg.wait(Sy, sems[V], q_done[k2 - 2])
                else:
                    pg.wait(Sy, sems[Tn], ltb_free_t)
                    pg.wait(Sy, sems[V], ltb_free_v)
                lw = pg.dma(Sy, dL, lambda e, _w=wt_src, _k=k, _h=hf, _s=k2 % 2: e.dma_start(
                    out=ltb[:, _s, :], in_=_w[_k * P:(_k + 1) * P, _h * HW:(_h + 1) * HW]))
                pg.wait_tok(V, lw)
                pg.wait(V, sems[V], v_recb)
                t1 = pg.op(V, lambda v, _s=k2 % 2, _h=hf: v.tensor_tensor(
                    qtmp[:, :], ltb[:, _s, :], recb[:, _h * HW:(_h + 1) * HW], AL.mult))
                pg.wait(V, sems[V], t1)
                q_done[k2] = pg.op(V, lambda v, _d=wq_dst, _k=k, _h=hf: v.tensor_scalar(
                    _d[:, _k * DI + _h * HW:_k * DI + (_h + 1) * HW], qtmp[:, :], MAGIC, MAGIC, AL.add, AL.subtract))
            ltb_free_v = q_done[2 * ND - 1]
        red_done = {}
        HW = DI // 2
        for oc2 in range(2 * ND):
            oc, hf = divmod(oc2, 2)
            if oc2 >= 2:
                pg.wait(Sy, sems[V], red_done[oc2 - 2])
            else:
                pg.wait(Sy, sems[V], ltb_free_v)
            lw = pg.dma(Sy, dL, lambda e, _o=oc, _h=hf, _s=oc2 % 2: e.dma_start(
                out=ltb[:, _s, :], in_=d_fd_n[_o * P:(_o + 1) * P, _h * HW:(_h + 1) * HW]))
            pg.wait_tok(V, lw)
            if hf == 0:
                red_done[oc2] = pg.op(V, lambda v, _o=oc, _s=oc2 % 2: v.tensor_reduce(
                    nwt[:, 8:9], ltb[:, _s, :], AX.X, AL.max, apply_absolute_value=True))
            else:
                r1 = pg.op(V, lambda v, _s=oc2 % 2: v.tensor_reduce(nwt[:, 9:10], ltb[:, _s, :], AX.X, AL.max, apply_absolute_value=True))
                pg.wait(V, sems[V], r1)
                red_done[oc2] = pg.op(V, lambda v, _o=oc: v.tensor_tensor(amax_d[:, _o:_o + 1], nwt[:, 8:9], nwt[:, 9:10], AL.max))
        e0 = seq_chain(pg, V, [
            lambda v: v.reciprocal(nwt[:, 0:ND], amax_d[:, :]),
            lambda v: v.tensor_scalar(nwt[:, 0:ND], nwt[:, 0:ND], QMAX, None, AL.mult),
            lambda v: v.tensor_scalar(nwt[:, 8:8 + ND], amax_d[:, :], 1.0 / QMAX, None, AL.mult),
        ])
        for oc in range(ND):
            e0 = col2row(pg, 2, (lambda _o=oc: lambda: nwt[:, _o:_o + 1])(),
                         (lambda _o=oc: lambda: rrow[0:1, _o % 4 * P:(_o % 4 + 1) * P])(), dep_v=e0)
            if oc % 4 == 3:
                e0 = bcast_row(pg, 2, lambda: rrow[0:1, 0:512],
                               (lambda _j=oc // 4: lambda: recb[:, _j * 512:(_j + 1) * 512])(), 512, e0)
        for oc in range(ND):
            e0 = col2row(pg, 2, (lambda _o=oc: lambda: nwt[:, 8 + _o:9 + _o])(),
                         (lambda _o=oc: lambda: rrow[0:1, _o % 4 * P:(_o % 4 + 1) * P])(), dep_v=e0)
            if oc % 4 == 3:
                e0 = bcast_row(pg, 2, lambda: rrow[0:1, 0:512],
                               (lambda _j=oc // 4: lambda: recb[:, D + _j * 512:D + (_j + 1) * 512])(), 512, e0)
        wq_stage = {}
        st_done = {}
        for oc in range(NI):
            if oc >= 2:
                pg.wait(Sy, sems[V], wq_stage[oc - 2])
            lw = pg.dma(Sy, dL, lambda e, _o=oc: e.dma_start(out=ltb[:, _o % 2, 0:D], in_=d_fd_t[_o * P:(_o + 1) * P, :]))
            pg.wait_tok(V, lw)
            pg.wait(V, sems[V], e0)
            if oc >= 2:
                pg.wait_tok(V, st_done.get(oc - 2, 0))
            t1 = pg.op(V, lambda v, _o=oc: v.tensor_tensor(qtmp[:, 0:D], ltb[:, _o % 2, 0:D], recb[:, 0:D], AL.mult))
            pg.wait(V, sems[V], t1)
            t2 = pg.op(V, lambda v: v.tensor_scalar(qtmp[:, 0:D], qtmp[:, 0:D], MAGIC, MAGIC, AL.add, AL.subtract))
            pg.wait(V, sems[V], t2)
            wq_stage[oc] = pg.op(V, lambda v, _o=oc: v.tensor_tensor(
                dstage[:, _o % 2, :], qtmp[:, 0:D], recb[:, D:2 * D], AL.mult))
            pg.wait(Sc, sems[V], wq_stage[oc])
            st_done[oc] = pg.dma(Sc, dS, lambda s, _o=oc: s.dma_start(out=d_wdq[_o, :, :], in_=dstage[:, _o % 2, :]))
        pg.emit(nc)

    pg = Prog(sems)
    with (
        nc.sbuf_tensor("xtg", [P, 3, ND, GT], F32) as xtg,
        nc.sbuf_tensor("xnb", [P, 2, ND, GT], BF16) as xnb,
        nc.sbuf_tensor("ubf", [P, ND, GT], BF16) as ubf,
        nc.sbuf_tensor("xnf", [P, 2, GT], F32) as xnf,
        nc.sbuf_tensor("xsq", [P, 2, GT], F32) as xsq,
        nc.sbuf_tensor("sgb", [P, 1, GT], F32) as sgb,
        nc.sbuf_tensor("mbA", [P, 2, GT], F32) as mbA,
        nc.sbuf_tensor("rstdbA", [P, 2, GT], F32) as rstdbA,
        nc.sbuf_tensor("rowS", [1, 2, GT], F32) as rowS,
    ):
        ld_done = {}
        sq_done = {}
        stat_mm = {}
        bcast_done = {}
        cast_done = {}
        mx_done = {}
        xtg_rd = {}
        pool_done = {}
        u_done = {}
        sg_done = {}
        attn_done = {}
        ub_evac = {}

        def emit_loads(g):
            if g >= 3:
                pg.wait(Sy, sems[Gp], xtg_rd[g - 3])
            ld_done[g] = [pg.dma(Sy, dL, lambda e, _g=g, _dc=dc: e.dma_start(
                out=xtg[:, _g % 3, _dc, :], in_=d_xt[_dc * P:(_dc + 1) * P, _g * GT:(_g + 1) * GT]))
                for dc in range(ND)]

        stat_evac = {}

        def emit_stat_mms(g):
            b = g % 3
            for t_ in ld_done[g]:
                pg.wait_tok(V, t_)
                pg.wait_tok(Tn, t_)
            if g >= 1:
                pg.wait(Tn, sems[V], stat_evac.get(g - 1, 0))
            for dc in range(ND):
                if (g, dc - 2) in stat_mm or (g - 1, ND + dc - 2) in stat_mm:
                    key = (g, dc - 2) if dc >= 2 else (g - 1, ND + dc - 2)
                    pg.wait(V, sems[Tn], stat_mm[key])
                sq_done[(g, dc)] = pg.op(V, lambda v, _b=b, _dc=dc: v.tensor_tensor(
                    xsq[:, _dc % 2, :], xtg[:, _b, _dc, :], xtg[:, _b, _dc, :], AL.mult))
                pg.op(Tn, lambda t, _b=b, _dc=dc: t.matmul(
                    ps[5][0:1, 0:GT], ones_c[:, :], xtg[:, _b, _dc, :],
                    start=(_dc == 0), stop=(_dc == ND - 1)))
                pg.wait(Tn, sems[V], sq_done[(g, dc)])
                stat_mm[(g, dc)] = pg.op(Tn, lambda t, _dc=dc: t.matmul(
                    ps[6][0:1, 0:GT], ones_c[:, :], xsq[:, _dc % 2, :],
                    start=(_dc == 0), stop=(_dc == ND - 1)))
            stat_mm[g] = stat_mm[(g, ND - 1)]

        def emit_stat_vec(g):
            b = g % 3
            br = g % 2
            g2 = 2 * g
            pg.wait(V, sems[Tn], stat_mm[g])
            e1 = pg.op(V, lambda v, _br=br: v.tensor_copy(rowS[0:1, _br, :], ps[5][0:1, 0:GT]))
            for j in range(2):
                col2row_inv = pg
                pg.wait(Tn, sems[V], e1)
                tt = pg.op(Tn, lambda t, _br=br, _j=j: t.transpose(
                    ps[7][0:P, 0:1], rowS[0:1, _br, _j * P:(_j + 1) * P], eye[0:1, 0:1]))
                pg.wait(V, sems[Tn], tt)
                e1 = pg.op(V, lambda v, _j=j: v.tensor_copy(nwt[:, _j:_j + 1], ps[7][0:P, 0:1]))
            pg.wait(V, sems[Tn], sems[Tn].n)
            e2 = pg.op(V, lambda v, _br=br: v.tensor_copy(rowS[0:1, _br, :], ps[6][0:1, 0:GT]))
            stat_evac[g] = e2
            for j in range(2):
                pg.wait(Tn, sems[V], e2)
                tt = pg.op(Tn, lambda t, _br=br, _j=j: t.transpose(
                    ps[7][0:P, 0:1], rowS[0:1, _br, _j * P:(_j + 1) * P], eye[0:1, 0:1]))
                pg.wait(V, sems[Tn], tt)
                e2 = pg.op(V, lambda v, _j=j: v.tensor_copy(nwt[:, 2 + _j:3 + _j], ps[7][0:P, 0:1]))
            seq_chain(pg, V, [
                lambda v, _g2=g2: v.tensor_scalar(m_cols[:, _g2:_g2 + 2], nwt[:, 0:2], 1.0 / D, None, AL.mult),
                lambda v, _g2=g2: v.tensor_tensor(nwt[:, 4:6], m_cols[:, _g2:_g2 + 2], m_cols[:, _g2:_g2 + 2], AL.mult),
                lambda v: v.scalar_tensor_tensor(nwt[:, 6:8], nwt[:, 2:4], 1.0 / D, nwt[:, 4:6], AL.mult, AL.subtract),
                lambda v: v.tensor_scalar(nwt[:, 6:8], nwt[:, 6:8], 1e-5, None, AL.add),
                lambda v: v.tensor_scalar(nwt[:, 8:10], nwt[:, 6:8], 0.5, 0.5, AL.mult, AL.add),
                lambda v: v.reciprocal(nwt[:, 10:12], nwt[:, 8:10]),
                lambda v: v.tensor_tensor(nwt[:, 12:14], nwt[:, 10:12], nwt[:, 10:12], AL.mult),
                lambda v: v.tensor_tensor(nwt[:, 12:14], nwt[:, 12:14], nwt[:, 6:8], AL.mult),
                lambda v: v.tensor_scalar(nwt[:, 12:14], nwt[:, 12:14], -0.5, 1.5, AL.mult, AL.add),
                lambda v: v.tensor_tensor(nwt[:, 10:12], nwt[:, 10:12], nwt[:, 12:14], AL.mult),
                lambda v: v.tensor_tensor(nwt[:, 12:14], nwt[:, 10:12], nwt[:, 10:12], AL.mult),
                lambda v: v.tensor_tensor(nwt[:, 12:14], nwt[:, 12:14], nwt[:, 6:8], AL.mult),
                lambda v: v.tensor_scalar(nwt[:, 12:14], nwt[:, 12:14], -0.5, 1.5, AL.mult, AL.add),
                lambda v: v.tensor_tensor(nwt[:, 10:12], nwt[:, 10:12], nwt[:, 12:14], AL.mult),
                lambda v: v.tensor_tensor(nwt[:, 12:14], nwt[:, 10:12], nwt[:, 10:12], AL.mult),
                lambda v: v.tensor_tensor(nwt[:, 12:14], nwt[:, 12:14], nwt[:, 6:8], AL.mult),
                lambda v: v.tensor_scalar(nwt[:, 12:14], nwt[:, 12:14], -0.5, 1.5, AL.mult, AL.add),
                lambda v: v.tensor_tensor(nwt[:, 10:12], nwt[:, 10:12], nwt[:, 12:14], AL.mult),
                lambda v: v.tensor_tensor(nwt[:, 12:14], nwt[:, 10:12], nwt[:, 10:12], AL.mult),
                lambda v: v.tensor_tensor(nwt[:, 12:14], nwt[:, 12:14], nwt[:, 6:8], AL.mult),
                lambda v: v.tensor_scalar(nwt[:, 12:14], nwt[:, 12:14], -0.5, 1.5, AL.mult, AL.add),
                lambda v, _g2=g2: v.tensor_tensor(rstd_cols[:, _g2:_g2 + 2], nwt[:, 10:12], nwt[:, 12:14], AL.mult),
                lambda v, _g2=g2: v.tensor_tensor(mrstd_cols[:, _g2:_g2 + 2], m_cols[:, _g2:_g2 + 2], rstd_cols[:, _g2:_g2 + 2], AL.mult),
            ])
        def emit_stat_bcast(g):
            br = g % 2
            bm = g % 2
            g2 = 2 * g
            if g >= 2:
                pg.wait(V, sems[Gp], pool_done.get((g - 2, ND - 1), 0))
            e3 = sems[V].n
            for j in range(2):
                e3 = col2row(pg, 7, (lambda _g2=g2, _j=j: lambda: m_cols[:, _g2 + _j:_g2 + _j + 1])(),
                             (lambda _br=br, _j=j: lambda: rowS[0:1, _br, _j * P:(_j + 1) * P])(), dep_v=e3)
            e3 = bcast_row(pg, 7, (lambda _br=br: lambda: rowS[0:1, _br, :])(),
                           (lambda _bm=bm: lambda: mbA[:, _bm, :])(), GT, e3)
            for j in range(2):
                e3 = col2row(pg, 7, (lambda _g2=g2, _j=j: lambda: rstd_cols[:, _g2 + _j:_g2 + _j + 1])(),
                             (lambda _br=br, _j=j: lambda: rowS[0:1, _br, _j * P:(_j + 1) * P])(), dep_v=e3)
            e3 = bcast_row(pg, 7, (lambda _br=br: lambda: rowS[0:1, _br, :])(),
                           (lambda _bm=bm: lambda: rstdbA[:, _bm, :])(), GT, e3)
            bcast_done[g] = e3

        def emit_xn(g):
            b = g % 3
            bx = g % 2
            bm = g % 2
            pg.wait(Gp, sems[V], bcast_done[g])
            for t_ in ld_done[g]:
                pg.wait_tok(Gp, t_)
            pg.wait(Gp, sems[Tn], attn_done.get(g - 2, 0))
            for dc in range(ND):
                e_ = dc % 2
                idx = g * ND + dc
                if dc >= 2:
                    pg.wait(Gp, sems[V], mx_done[(g, dc - 2)])
                p1 = pg.op(Gp, lambda p_, _b=b, _bm=bm, _dc=dc, _e=e_: p_.tensor_tensor(
                    xnf[:, _e, :], xtg[:, _b, _dc, :], mbA[:, _bm, :], AL.subtract))
                pg.wait(Gp, sems[Gp], p1)
                pool_done[(g, dc)] = pg.op(Gp, lambda p_, _bm=bm, _e=e_: p_.tensor_tensor(
                    xnf[:, _e, :], xnf[:, _e, :], rstdbA[:, _bm, :], AL.mult))
                pg.wait(Gp, sems[Gp], pool_done[(g, dc)])
                cast_done[(g, dc)] = pg.op(Gp, lambda p_, _bx=bx, _dc=dc, _e=e_: p_.tensor_copy(
                    xnb[:, _bx, _dc, :], xnf[:, _e, :]))
                pg.wait(V, sems[Gp], pool_done[(g, dc)])
                r1 = pg.op(V, lambda v, _e=e_: v.tensor_reduce(nwt[:, 14:15], xnf[:, _e, :], AX.X, AL.max, apply_absolute_value=True))
                pg.wait(V, sems[V], r1)
                pp = idx % 2
                mx_done[(g, dc)] = pg.op(V, lambda v, _pp=pp: v.tensor_tensor(
                    mxc[:, 1 - _pp:2 - _pp], mxc[:, _pp:_pp + 1], nwt[:, 14:15], AL.max))
            xtg_rd[g] = cast_done[(g, ND - 1)]

        def emit_attn(g):
            b = g % 3
            bx = g % 2
            for dc in range(ND):
                e_ = dc % 2
                idx = g * ND + dc
                psA, psB = (0, 1) if e_ == 0 else (2, 3)
                pg.wait(Tn, sems[Gp], cast_done[(g, ND - 1)])
                if idx >= 2:
                    prev = divmod(idx - 2, ND)
                    pg.wait(Tn, sems[V], u_done.get(prev, 0))
                    pg.wait(Tn, sems[Sc], sg_done.get(prev, 0))
                for k in range(ND):
                    pg.op(Tn, lambda t, _pA=psA, _bx=bx, _dc=dc, _k=k: t.matmul(
                        ps[_pA][:, 0:GT], w2tb[:, _k * D + _dc * P:_k * D + (_dc + 1) * P], xnb[:, _bx, _k, :],
                        start=(_k == 0), stop=(_k == ND - 1)))
                for k in range(ND):
                    pg.op(Tn, lambda t, _pB=psB, _bx=bx, _dc=dc, _k=k: t.matmul(
                        ps[_pB][:, 0:GT], wgtb[:, _k * D + _dc * P:_k * D + (_dc + 1) * P], xnb[:, _bx, _k, :],
                        start=(_k == 0), stop=(_k == ND - 1)))
                t_mm = sems[Tn].n
                pg.wait(Sc, sems[Tn], t_mm)
                if idx >= 1:
                    prev = divmod(idx - 1, ND)
                    pg.wait(Sc, sems[V], u_done.get(prev, 0))
                sg_done[(g, dc)] = pg.op(Sc, lambda s, _pB=psB, _dc=dc: s.activation(
                    sgb[:, 0, :], ps[_pB][:, 0:GT], AF.Sigmoid, bias=bgc[:, _dc:_dc + 1], scale=1.0))
                pg.wait(V, sems[Sc], sg_done[(g, dc)])
                if dc == 0:
                    pg.wait(V, sems[Tn], attn_done.get(g - 1, 0))
                u_done[(g, dc)] = pg.op(V, lambda v, _pA=psA, _dc=dc: v.scalar_tensor_tensor(
                    ubf[:, _dc, :], ps[_pA][:, 0:GT], b2c[:, _dc:_dc + 1], sgb[:, 0, :], AL.add, AL.mult))
            pg.wait(Tn, sems[V], u_done[(g, ND - 1)])
            if g >= 1:
                pg.wait(Tn, sems[V], ub_evac.get(g - 1, 0))
            for k in range(ND):
                pg.op(Tn, lambda t, _k=k: t.matmul(
                    ps[4][0:SC, 0:GT], bmt_b[:, _k * SC:(_k + 1) * SC], ubf[:, _k, :],
                    start=(_k == 0), stop=(_k == ND - 1)))
            attn_done[g] = sems[Tn].n
            pg.wait(V, sems[Tn], attn_done[g])
            ub_evac[g] = pg.op(V, lambda v, _g=g: v.tensor_copy(uB[:, _g * GT:(_g + 1) * GT], ps[4][0:SC, 0:GT]))

        for gg in (0, 1, 2):
            emit_loads(gg)
            emit_stat_mms(gg)
            emit_stat_vec(gg)
        emit_stat_bcast(0)
        emit_stat_bcast(1)
        emit_xn(0)
        for g in range(NG):
            if g + 2 < NG:
                emit_stat_bcast(g + 2)
            if g + 1 < NG:
                emit_xn(g + 1)
            emit_attn(g)
            if g + 3 < NG:
                emit_loads(g + 3)
                emit_stat_mms(g + 3)
                emit_stat_vec(g + 3)

        fin = (NG * ND) % 2
        fx = col2row(pg, 7, lambda: mxc[:, fin:fin + 1], lambda: rowt[0:1, 0:P], dep_v=mx_done[(NG - 1, ND - 1)])
        pg.wait(V, sems[V], fx)
        vmx = pg.op(V, lambda v: v.tensor_reduce(sc1[0:1, 2:3], rowt[0:1, 0:P], AX.X, AL.max))
        pg.wait(Sc, sems[V], vmx)
        stg = pg.op(Sc, lambda s: s.activation(rowt[0:1, P:P + 8], ones_r[0:1, 0:8], AF.Identity, bias=0.0, scale=sc1[0:1, 2:3]))
        pg.wait(Gp, sems[Sc], stg)
        tk1 = pg.dma(Gp, dG, lambda e: e.dma_start(out=d_cc_in[:, :], in_=rowt[0:1, P:P + 8]))
        pg.wait_tok(Gp, tk1)
        cc_tok = pg.op(Gp, lambda e: e.collective_compute(
            "AllReduce", AL.max, replica_groups=[list(range(8))],
            ins=[d_cc_in[:, :]], outs=[d_cc_out[:, :]]))
        pg.wait(Gp, sems[Gp], cc_tok)
        tk2 = pg.dma(Gp, dG, lambda e: e.dma_start(out=sc1[0:1, 2:3], in_=d_cc_out[0:1, 0:1]))
        pg.wait_tok(Sc, tk2)
        sgm = pg.op(Sc, lambda s: s.activation(sc1[0:1, 3:4], sc1[0:1, 0:1], AF.Sigmoid, bias=0.0, scale=1.0))
        pg.wait(V, sems[Sc], sgm)
        seq_chain(pg, V, [
            lambda v: v.tensor_tensor(sc1[0:1, 4:5], sc1[0:1, 3:4], sc1[0:1, 2:3], AL.mult),
            lambda v: v.tensor_scalar(sc1[0:1, 4:5], sc1[0:1, 4:5], 1.0 / QMAX, None, AL.mult),
            lambda v: v.reciprocal(sc1[0:1, 5:6], sc1[0:1, 4:5]),
        ])
        vsx = sems[V].n
        pg.wait(Tn, sems[V], vsx)
        tb = pg.op(Tn, lambda t: t.matmul(ps[7][:, 0:2], ones_r[:, :], sc1[0:1, 4:6], start=True, stop=True))
        pg.wait(V, sems[Tn], tb)
        seq_chain(pg, V, [
            lambda v: v.tensor_copy(cS[:, 2:3], ps[7][:, 0:1]),
            lambda v: v.tensor_copy(cS[:, 0:1], ps[7][:, 1:2]),
            lambda v: v.tensor_scalar(cS[:, 1:2], ps[7][:, 1:2], -1.0, None, AL.mult),
            lambda v: v.tensor_scalar(sxg[:, :], amax_g[:, :], 1.0 / QMAX, None, AL.mult),
            lambda v: v.tensor_scalar_mul(sxg[:, :], sxg[:, :], cS[:, 2:3]),
            lambda v: v.tensor_scalar(sxu[:, :], amax_u[:, :], 1.0 / QMAX, None, AL.mult),
            lambda v: v.tensor_scalar_mul(sxu[:, :], sxu[:, :], cS[:, 2:3]),
        ])
        pg.emit(nc)
    _wstack.close()

    pg = Prog(sems)
    with (
        nc.sbuf_tensor("xtg2", [P, 3, ND, GT], F32) as xtg2,
        nc.sbuf_tensor("zq", [P, 2, ND, GT], BF16) as zq,
        nc.sbuf_tensor("yt", [P, 2, GT], F32) as yt,
        nc.sbuf_tensor("sg2", [P, 2, GT], F32) as sg2,
        nc.sbuf_tensor("hb", [P, 4, GT], F32) as hb,
        nc.sbuf_tensor("c1b", [P, 3, GT], F32) as c1b,
        nc.sbuf_tensor("c0b", [P, 3, GT], F32) as c0b,
        nc.sbuf_tensor("rowB", [1, 3, GT], F32) as rowB,
        nc.sbuf_tensor("fstk", [SC, NCHK], BF16) as fstk,
        nc.sbuf_tensor("scr16", [SC, NCHK], F32) as scr16,
    ):
        ldB = {}
        c_done = {}
        z_done = {}
        gu_done = {}
        h_done = {}
        sil_done = {}
        spill = {}
        habs_done = {}
        scan_v = {}
        scan_emitted = [0]

        def emit_loadsB(g):
            if g >= 3:
                pg.wait(Sy, sems[Gp], z_done[g - 3])
            ldB[g] = [pg.dma(Sy, dL, lambda e, _g=g, _dc=dc: e.dma_start(
                out=xtg2[:, _g % 3, _dc, :], in_=d_xt[_dc * P:(_dc + 1) * P, _g * GT:(_g + 1) * GT]))
                for dc in range(ND)]

        c_tn = {}

        def emit_cchain(g):
            b = g % 3
            g2 = 2 * g
            o = 16 + 4 * (g % 2)
            pg.wait(V, sems[Tn], c_tn.get(g - 2, 0))
            e0 = seq_chain(pg, V, [
                lambda v, _g2=g2, _o=o: v.tensor_scalar_mul(nwt[:, _o:_o + 2], rstd_cols[:, _g2:_g2 + 2], cS[:, 0:1]),
                lambda v, _g2=g2, _o=o: v.tensor_scalar_mul(nwt[:, _o + 2:_o + 4], mrstd_cols[:, _g2:_g2 + 2], cS[:, 1:2]),
            ])
            if g >= 3:
                pg.wait(V, sems[Gp], c_done.get(("pool", g - 3), 0))
            e3 = e0
            for j in range(2):
                e3 = col2row(pg, 5, (lambda _j=j, _o=o: lambda: nwt[:, _o + _j:_o + _j + 1])(),
                             (lambda _b=b, _j=j: lambda: rowB[0:1, _b, _j * P:(_j + 1) * P])(), dep_v=e3)
            e3 = bcast_row(pg, 5, (lambda _b=b: lambda: rowB[0:1, _b, :])(),
                           (lambda _b=b: lambda: c1b[:, _b, :])(), GT, e3)
            for j in range(2):
                e3 = col2row(pg, 5, (lambda _j=j, _o=o: lambda: nwt[:, _o + 2 + _j:_o + 3 + _j])(),
                             (lambda _b=b, _j=j: lambda: rowB[0:1, _b, _j * P:(_j + 1) * P])(), dep_v=e3)
            e3 = bcast_row(pg, 5, (lambda _b=b: lambda: rowB[0:1, _b, :])(),
                           (lambda _b=b: lambda: c0b[:, _b, :])(), GT, e3)
            c_done[g] = e3
            c_tn[g] = sems[Tn].n

        def emit_xq(g):
            b = g % 3
            bz = g % 2
            for t_ in ldB[g]:
                pg.wait_tok(Gp, t_)
            pg.wait(Gp, sems[V], c_done[g])
            for dc in range(ND):
                e_ = dc % 2
                idx = g * ND + dc
                if idx >= 2:
                    pg.wait(Gp, sems[Gp], z_done.get(divmod(idx - 2, ND), 0))
                p1 = pg.op(Gp, lambda p_, _b=b, _dc=dc, _e=e_: p_.tensor_tensor(
                    yt[:, _e, :], xtg2[:, _b, _dc, :], c1b[:, _b, :], AL.mult))
                pg.wait(Gp, sems[Gp], p1)
                p2 = pg.op(Gp, lambda p_, _b=b, _e=e_: p_.tensor_tensor(
                    yt[:, _e, :], yt[:, _e, :], c0b[:, _b, :], AL.add))
                c_done[("pool", g)] = p2
                pg.wait(Gp, sems[Gp], p2)
                if idx >= 2 and dc < 2:
                    pg.wait(Gp, sems[Tn], gu_done.get((g - 2, NI - 1), 0))
                t1 = pg.op(Gp, lambda v, _e=e_: v.tensor_scalar(yt[:, _e, :], yt[:, _e, :], MAGIC, MAGIC, AL.add, AL.subtract))
                pg.wait(Gp, sems[Gp], t1)
                z_done[divmod(idx, ND)] = pg.op(Gp, lambda v, _bz=bz, _dc=dc, _e=e_: v.tensor_scalar(
                    zq[:, _bz, _dc, :], yt[:, _e, :], -128.0, 127.0, AL.max, AL.min))
            z_done[g] = z_done[(g, ND - 1)]

        def emit_gateup(g):
            b = g % 2
            pg.wait(Tn, sems[Gp], z_done[g])
            for oc in range(NI):
                e_ = oc % 2
                idx = g * NI + oc
                psG, psU = (0, 1) if e_ == 0 else (2, 3)
                if idx >= 2:
                    prev = divmod(idx - 2, NI)
                    pg.wait(Tn, sems[Sc], sil_done.get(prev, 0))
                    pg.wait(Tn, sems[V], h_done.get(prev, 0))
                for k in range(ND):
                    pg.op(Tn, lambda t, _p=psG, _b=b, _oc=oc, _k=k: t.matmul(
                        ps[_p][:, 0:GT], wq_g[:, _k * DI + _oc * P:_k * DI + (_oc + 1) * P], zq[:, _b, _k, :],
                        start=(_k == 0), stop=(_k == ND - 1)))
                for k in range(ND):
                    pg.op(Tn, lambda t, _p=psU, _b=b, _oc=oc, _k=k: t.matmul(
                        ps[_p][:, 0:GT], wq_u[:, _k * DI + _oc * P:_k * DI + (_oc + 1) * P], zq[:, _b, _k, :],
                        start=(_k == 0), stop=(_k == ND - 1)))
                gu_done[(g, oc)] = sems[Tn].n
                pg.wait(Sc, sems[Tn], gu_done[(g, oc)])
                if idx >= 2:
                    pg.wait(Sc, sems[V], h_done.get(divmod(idx - 2, NI), 0))
                sil_done[(g, oc)] = pg.op(Sc, lambda s, _p=psG, _oc=oc, _e=e_: s.activation(
                    sg2[:, _e, :], ps[_p][:, 0:GT], AF.Silu, bias=0.0, scale=sxg[:, _oc:_oc + 1]))
                hslot = idx % 4
                pg.wait(V, sems[Sc], sil_done[(g, oc)])
                if idx >= 4:
                    pg.wait_tok(V, spill.get(divmod(idx - 4, NI), 0))
                hv = pg.op(V, lambda v, _p=psU, _oc=oc, _e=e_, _h=hslot: v.scalar_tensor_tensor(
                    hb[:, _h, :], ps[_p][:, 0:GT], sxu[:, _oc:_oc + 1], sg2[:, _e, :], AL.mult, AL.mult))
                pg.wait(V, sems[V], hv)
                h_done[(g, oc)] = pg.op(V, lambda v, _oc=oc, _h=hslot: v.tensor_reduce(
                    habs_c[:, _oc:_oc + 1], hb[:, _h, :], AX.X, AL.max, apply_absolute_value=True))
                pg.wait(Sc, sems[V], h_done[(g, oc)])
                spill[(g, oc)] = pg.dma(Sc, dS, lambda s, _g=g, _oc=oc, _h=hslot: s.dma_start(
                    out=d_hsp[_oc, :, _g * GT:(_g + 1) * GT], in_=hb[:, _h, :]))
            pp = g % 2
            pg.wait(V, sems[V], h_done[(g, NI - 1)])
            r1 = pg.op(V, lambda v: v.tensor_reduce(nwt[:, 14:15], habs_c[:, :], AX.X, AL.max))
            pg.wait(V, sems[V], r1)
            habs_done[g] = pg.op(V, lambda v, _pp=pp: v.tensor_tensor(
                mhc[:, 1 - _pp:2 - _pp], mhc[:, _pp:_pp + 1], nwt[:, 14:15], AL.max))

        def emit_scan_steps(n):
            for _ in range(n):
                tstep = scan_emitted[0] + 1
                if tstep >= L:
                    return
                scan_emitted[0] = tstep
                pg.wait(Tn, sems[V], scan_v[tstep - 1])
                pg.op(Tn, lambda t, _t=tstep: t.matmul(
                    ps[4][0:SC, 0:NCHK], at_b[:, :], hs[0:SC, (_t - 1):T:L], start=True, stop=False))
                tq = pg.op(Tn, lambda t, _t=tstep: t.matmul(
                    ps[4][0:SC, 0:NCHK], i16b[:, :], uB[:, _t:T:L], start=False, stop=True))
                pg.wait(V, sems[Tn], tq)
                scan_v[tstep] = pg.op(V, lambda v, _t=tstep: v.tensor_copy(hs[0:SC, _t:T:L], ps[4][0:SC, 0:NCHK]))

        scan_v[0] = pg.op(V, lambda v: v.tensor_copy(hs[0:SC, 0:T:L], uB[:, 0:T:L]))

        for gg in (0, 1):
            emit_loadsB(gg)
            emit_cchain(gg)
        emit_xq(0)
        for g in range(NG):
            emit_gateup(g)
            if g + 1 < NG:
                emit_xq(g + 1)
            if g + 2 < NG:
                emit_loadsB(g + 2)
                emit_cchain(g + 2)
            emit_scan_steps(2)
        emit_scan_steps(L)
        pg.wait(V, sems[V], scan_v[L - 1])
        f0 = pg.op(V, lambda v: v.memset(fstk[:, 0:NCHK:CPS], 0.0))
        fprev = f0
        q64 = (L - 1) * SC
        for c in range(1, CPS):
            pg.wait(Tn, sems[V], fprev)
            pg.op(Tn, lambda t, _c=c: t.matmul(
                ps[4][0:SC, 0:4], qstk[:, q64:q64 + SC], fstk[:, (_c - 1):NCHK:CPS], start=True, stop=False))
            tq = pg.op(Tn, lambda t, _c=c: t.matmul(
                ps[4][0:SC, 0:4], i16b[:, :], hs[0:SC, ((_c - 1) * L + L - 1):T:SEQ], start=False, stop=True))
            pg.wait(V, sems[Tn], tq)
            fprev = pg.op(V, lambda v, _c=c: v.tensor_copy(fstk[:, _c:NCHK:CPS], ps[4][0:SC, 0:4]))
        corr_prev = fprev
        for tstep in range(L):
            pg.wait(Tn, sems[V], corr_prev)
            tq = pg.op(Tn, lambda t, _t=tstep: t.matmul(
                ps[4][0:SC, 0:NCHK], qstk[:, _t * SC:(_t + 1) * SC], fstk[:, :], start=True, stop=True))
            pg.wait(V, sems[Tn], tq)
            a1 = pg.op(V, lambda v, _t=tstep: v.tensor_tensor(scr16[:, :], hs[0:SC, _t:T:L], ps[4][0:SC, 0:NCHK], AL.add))
            pg.wait(V, sems[V], a1)
            corr_prev = pg.op(V, lambda v, _t=tstep: v.tensor_copy(hs[0:SC, _t:T:L], scr16[:, :]))

        finh = NG % 2
        fh = col2row(pg, 7, lambda: mhc[:, finh:finh + 1], lambda: rowt[0:1, 0:P], dep_v=habs_done[NG - 1])
        pg.wait(V, sems[V], fh)
        vmh = pg.op(V, lambda v: v.tensor_reduce(sc1[0:1, 2:3], rowt[0:1, 0:P], AX.X, AL.max))
        pg.wait(Sc, sems[V], vmh)
        stg = pg.op(Sc, lambda s: s.activation(rowt[0:1, P:P + 8], ones_r[0:1, 0:8], AF.Identity, bias=0.0, scale=sc1[0:1, 2:3]))
        pg.wait(Gp, sems[Sc], stg)
        tk1 = pg.dma(Gp, dG, lambda e: e.dma_start(out=d_cc_in[:, :], in_=rowt[0:1, P:P + 8]))
        pg.wait_tok(Gp, tk1)
        cc_tok = pg.op(Gp, lambda e: e.collective_compute(
            "AllReduce", AL.max, replica_groups=[list(range(8))],
            ins=[d_cc_in[:, :]], outs=[d_cc_out[:, :]]))
        pg.wait(Gp, sems[Gp], cc_tok)
        tk2 = pg.dma(Gp, dG, lambda e: e.dma_start(out=sc1[0:1, 2:3], in_=d_cc_out[0:1, 0:1]))
        pg.wait_tok(Sc, tk2)
        sgm = pg.op(Sc, lambda s: s.activation(sc1[0:1, 3:4], sc1[0:1, 1:2], AF.Sigmoid, bias=0.0, scale=1.0))
        pg.wait(V, sems[Sc], sgm)
        seq_chain(pg, V, [
            lambda v: v.tensor_tensor(sc1[0:1, 4:5], sc1[0:1, 3:4], sc1[0:1, 2:3], AL.mult),
            lambda v: v.tensor_scalar(sc1[0:1, 4:5], sc1[0:1, 4:5], 1.0 / QMAX, None, AL.mult),
            lambda v: v.reciprocal(sc1[0:1, 5:6], sc1[0:1, 4:5]),
        ])
        vsd = sems[V].n
        pg.wait(Tn, sems[V], vsd)
        tb = pg.op(Tn, lambda t: t.matmul(ps[7][:, 0:2], ones_r[:, :], sc1[0:1, 4:6], start=True, stop=True))
        pg.wait(V, sems[Tn], tb)
        seq_chain(pg, V, [
            lambda v: v.tensor_copy(cS[:, 6:7], ps[7][:, 0:1]),
            lambda v: v.tensor_copy(cS[:, 3:4], ps[7][:, 1:2]),
        ])
        pg.emit(nc)

    pg = Prog(sems)
    with (
        nc.sbuf_tensor("hre", [P, 2, 4, GT], F32) as hre,
        nc.sbuf_tensor("yq", [P, 2, 4 * GT], F32) as yq,
        nc.sbuf_tensor("zh", [P, 2, NI, GT], BF16) as zh,
        nc.sbuf_tensor("xres", [P, 2, D], F32) as xres,
        nc.sbuf_tensor("outb", [P, 2, D], F32) as outb,
    ):
        wdl = [pg.dma(Sy, dL, lambda e, _o=oc: e.dma_start(out=wq_d[:, _o * D:(_o + 1) * D], in_=d_wdq[_o, :, :]))
               for oc in range(NI)]
        for t_ in wdl:
            pg.wait_tok(Tn, t_)
        ldh = {}
        sA_done = {}
        zq_done = {}
        acc_done = {}
        ev_done = {}
        outd = {}

        BW = 4
        NB = (NI + BW - 1) // BW

        def emit_hq(g):
            b = g % 2
            for ib in range(NB):
                o0 = ib * BW
                nch = min(BW, NI - o0)
                idx = g * NB + ib
                e_ = idx % 2
                if idx >= 2:
                    pg.wait(Sy, sems[V], sA_done[idx - 2])
                ldh[idx] = pg.dma(Sy, dL, lambda e, _g=g, _o=o0, _n=nch, _s=idx % 2: e.dma_start(
                    out=hre[:, _s, 0:_n, :], in_=d_hsp[_o:_o + _n, :, _g * GT:(_g + 1) * GT].rearrange("c p t -> p c t")))
                Qe = V
                pg.wait_tok(Qe, ldh[idx])
                if idx >= 2:
                    pg.wait(Qe, sems[V], zq_done.get(idx - 2, (None, 0))[1] if False else 0)
                if idx >= 2:
                    pt = zq_done[idx - 2]
                    pg.wait(Qe, pt[0], pt[1])
                sA_done[idx] = pg.op(Qe, lambda v, _s=idx % 2, _e=e_, _n=nch: v.tensor_scalar(
                    yq[:, _e, 0:_n * GT], hre[:, _s, 0:_n, :].rearrange("p c t -> p (c t)"),
                    cS[:, 3:4], MAGIC, AL.mult, AL.add))
                pg.wait(Qe, pg.sems[Qe], sA_done[idx])
                if g >= 2 and ib == 0:
                    pg.wait(Qe, sems[Tn], acc_done.get((g - 2, NTT - 1), 0))
                t0 = pg.op(Qe, lambda v, _e=e_, _n=nch: v.tensor_scalar(
                    yq[:, _e, 0:_n * GT], yq[:, _e, 0:_n * GT], MAGIC, -128.0, AL.subtract, AL.max))
                pg.wait(Qe, pg.sems[Qe], t0)
                zq_done[idx] = (pg.sems[Qe], pg.op(Qe, lambda v, _b=b, _o=o0, _n=nch, _e=e_: v.tensor_scalar(
                    zh[:, _b, _o:_o + _n, :].rearrange("p c t -> p (c t)"), yq[:, _e, 0:_n * GT],
                    127.0, cS[:, 6:7], AL.min, AL.mult)))
            zq_done[g] = zq_done[g * NB + NB - 1]

        def emit_tiles(g):
            b = g % 2
            pg.wait(Tn, zq_done[g][0], zq_done[g][1])
            if g >= 1:
                pg.wait(Sy, sems[V], ev_done.get((g - 1, NTT - 1), 0))
            ldx = [pg.dma(Sy, dL, lambda e, _g=g, _b=b, _tt=tt: e.dma_start(
                out=xres[:, (_g * NTT + _tt) % 2, :], in_=d_x[(_g * NTT + _tt) * P:(_g * NTT + _tt + 1) * P, :]))
                for tt in range(NTT)]
            for tt in range(NTT):
                ti = g * NTT + tt
                pA, pB = (0, 1) if tt == 0 else (2, 3)
                if ti >= 2:
                    pg.wait(Tn, sems[V], ev_done.get((g - 1, tt), 0))
                for half, bk in ((0, pA), (1, pB)):
                    pg.op(Tn, lambda t, _bk=bk, _h=half, _ti=ti: t.matmul(
                        ps[_bk][:, 0:512], hs[0:SC + 1, _ti * P:(_ti + 1) * P], wct[0:SC + 1, _h * 512:(_h + 1) * 512],
                        start=True, stop=False))
                    for oc in range(NI):
                        pg.op(Tn, lambda t, _bk=bk, _h=half, _b=b, _o=oc, _tt=tt: t.matmul(
                            ps[_bk][:, 0:512], zh[:, _b, _o, _tt * P:(_tt + 1) * P],
                            wq_d[:, _o * D + _h * 512:_o * D + _h * 512 + 512],
                            start=False, stop=(_o == NI - 1)))
                acc_done[(g, tt)] = sems[Tn].n
                pg.wait(V, sems[Tn], acc_done[(g, tt)])
                pg.wait_tok(V, ldx[tt])
                if ti >= 2:
                    pg.wait_tok(V, outd.get(ti - 2, 0))
                pg.op(V, lambda v, _pA=pA, _s=ti % 2: v.tensor_tensor(
                    outb[:, _s, 0:512], ps[_pA][:, 0:512], xres[:, _s, 0:512], AL.add))
                ev_done[(g, tt)] = pg.op(V, lambda v, _pB=pB, _s=ti % 2: v.tensor_tensor(
                    outb[:, _s, 512:1024], ps[_pB][:, 0:512], xres[:, _s, 512:1024], AL.add))
                pg.wait(Gp, sems[V], ev_done[(g, tt)])
                outd[ti] = pg.dma(Gp, dG, lambda e, _ti=ti, _s=ti % 2: e.dma_start(
                    out=d_out[_ti * P:(_ti + 1) * P, :], in_=outb[:, _s, :]))

        emit_hq(0)
        for g in range(NG):
            if g + 1 < NG:
                emit_hq(g + 1)
            emit_tiles(g)
        pg.wait_lanes(Gp, dG)
        pg.emit(nc)
    return nc


_NC_CACHE = {}


def kernel(L0_attn_x, L0_attn_W_in, L0_attn_b_in, L0_attn_A, L0_attn_Bm, L0_attn_Cm,
           L0_attn_W_gate, L0_attn_b_gate, L0_attn_W_out, L0_attn_b_out,
           L0_ffn_clip_x, L0_ffn_clip_down, L0_ffn_w_gate, L0_ffn_w_up, L0_ffn_w_down):
    x = np.asarray(L0_attn_x)
    B, S, Dm = x.shape
    xr = np.ascontiguousarray(x.reshape(B * S, Dm))
    f32 = np.float32

    w_in = np.asarray(L0_attn_W_in)
    shared = {
        "w2t": np.ascontiguousarray(w_in[Dm:2 * Dm, :].T.astype(f32)),
        "wgt": np.ascontiguousarray(np.asarray(L0_attn_W_gate).T.astype(f32)),
        "bmt": np.ascontiguousarray(np.asarray(L0_attn_Bm).T.astype(f32)),
        "cm": np.ascontiguousarray(np.asarray(L0_attn_Cm).astype(f32)),
        "wot": np.ascontiguousarray(np.asarray(L0_attn_W_out).T.astype(f32)),
        "anat": np.ascontiguousarray(np.asarray(L0_attn_A).astype(f32)),
        "at": np.ascontiguousarray(np.asarray(L0_attn_A).T.astype(f32)),
        "b2c": np.ascontiguousarray(np.asarray(L0_attn_b_in)[Dm:2 * Dm].astype(f32).reshape(ND, P).T),
        "bgc": np.ascontiguousarray(np.asarray(L0_attn_b_gate).astype(f32).reshape(ND, P).T),
        "bout": np.ascontiguousarray(np.asarray(L0_attn_b_out).astype(f32).reshape(1, Dm)),
        "clipx": np.asarray(L0_ffn_clip_x, f32).reshape(1, 1),
        "clipd": np.asarray(L0_ffn_clip_down, f32).reshape(1, 1),
        "fg_n": np.ascontiguousarray(np.asarray(L0_ffn_w_gate).astype(f32)),
        "fu_n": np.ascontiguousarray(np.asarray(L0_ffn_w_up).astype(f32)),
        "fd_n": np.ascontiguousarray(np.asarray(L0_ffn_w_down).astype(f32)),
        "fg_t": np.ascontiguousarray(np.asarray(L0_ffn_w_gate).astype(f32).T),
        "fu_t": np.ascontiguousarray(np.asarray(L0_ffn_w_up).astype(f32).T),
        "fd_t": np.ascontiguousarray(np.asarray(L0_ffn_w_down).astype(f32).T),
        "eye": np.eye(P, dtype=f32),
        "hones": np.ones((1, T), f32),
        "ones_r": np.ones((1, P), f32),
        "ones_c": np.ones((P, 1), f32),
    }

    if "nc" not in _NC_CACHE:
        _NC_CACHE["nc"] = build_nc()
    nc = _NC_CACHE["nc"]

    in_maps = []
    for c in range(8):
        sl = slice(c * T, (c + 1) * T)
        m = dict(shared)
        m["x"] = np.ascontiguousarray(xr[sl])
        m["xt"] = np.ascontiguousarray(xr[sl].T)
        in_maps.append(m)

    res = run_bass_kernel_spmd(nc, in_maps, core_ids=list(range(8)))
    out = np.concatenate([np.asarray(r["out"]) for r in res.results], axis=0)
    return out.reshape(B, S, Dm).astype(np.float32)


# version 18
# speedup vs baseline: 1.1267x; 1.0190x over previous
import numpy as np
import concourse.bass as bass
import concourse.mybir as mybir
from concourse.bass_utils import run_bass_kernel_spmd

F32 = mybir.dt.float32
BF16 = mybir.dt.bfloat16
AL = mybir.AluOpType
AF = mybir.ActivationFunctionType
AX = mybir.AxisListType

P = 128
D = 1024
ND = D // P
DI = 2816
NI = DI // P
T = 8192
GT = 256
NG = T // GT
NTT = GT // P
SC = 16
L = 64
NCHK = T // L
SEQ = 2048
CPS = SEQ // L
MAGIC = 12582912.0
QMAX = 127.0


class Sem:
    __slots__ = ("h", "n")

    def __init__(self, nc, name):
        self.h = nc.alloc_semaphore(name)
        self.n = 0


class Lanes:

    def __init__(self, nc, name, n=8):
        self.sems = [Sem(nc, f"{name}{i}") for i in range(n)]
        self.i = 0

    def next(self):
        s = self.sems[self.i % len(self.sems)]
        self.i += 1
        s.n += 16
        return (s, s.n)

    def all_tokens(self):
        return [(s, s.n) for s in self.sems]


class Prog:
    ENGS = ("sync", "vector", "scalar", "tensor", "gpsimd")

    def __init__(self, sems):
        self.ops = {e: [] for e in self.ENGS}
        self.sems = sems
        self.last_wait = {e: {} for e in self.ENGS}

    def raw(self, eng, fn):
        self.ops[eng].append(fn)

    def op(self, eng, fn, inc=1):
        s = self.sems[eng]
        s.n += inc
        def w(e, _fn=fn, _s=s, _k=inc):
            _fn(e).then_inc(_s.h, _k)
        self.ops[eng].append(w)
        return s.n

    def dma(self, eng, lanes, fn):
        s, v = lanes.next()
        def w(e, _fn=fn, _s=s):
            _fn(e).then_inc(_s.h, 16)
        self.ops[eng].append(w)
        return (s, v)

    def wait_tok(self, eng, tok):
        if tok == 0 or tok is None:
            return
        s, v = tok
        self.wait(eng, s, v)

    def wait_lanes(self, eng, lanes):
        for s, v in lanes.all_tokens():
            self.wait(eng, s, v)

    def wait(self, eng, sem, val):
        lw = self.last_wait[eng]
        if lw.get(id(sem), -1) >= val:
            return
        lw[id(sem)] = val
        self.ops[eng].append(lambda e, _s=sem, _v=val: e.wait_ge(_s.h, _v))

    def emit(self, nc):
        with nc.Block() as block:
            for name in self.ENGS:
                ops = self.ops[name]
                def body(eng, _ops=ops):
                    for f in _ops:
                        f(eng)
                getattr(block, name)(body)


def build_nc():
    nc = bass.Bass("TRN2", target_bir_lowering=False)

    d_xt = nc.dram_tensor("xt", [D, T], F32, kind="ExternalInput")
    d_x = nc.dram_tensor("x", [T, D], F32, kind="ExternalInput")
    d_w2t = nc.dram_tensor("w2t", [D, D], F32, kind="ExternalInput")
    d_wgt = nc.dram_tensor("wgt", [D, D], F32, kind="ExternalInput")
    d_bmt = nc.dram_tensor("bmt", [D, SC], F32, kind="ExternalInput")
    d_cm = nc.dram_tensor("cm", [D, SC], F32, kind="ExternalInput")
    d_wot = nc.dram_tensor("wot", [D, D], F32, kind="ExternalInput")
    d_anat = nc.dram_tensor("anat", [SC, SC], F32, kind="ExternalInput")
    d_at = nc.dram_tensor("at", [SC, SC], F32, kind="ExternalInput")
    d_b2c = nc.dram_tensor("b2c", [P, ND], F32, kind="ExternalInput")
    d_bgc = nc.dram_tensor("bgc", [P, ND], F32, kind="ExternalInput")
    d_bout = nc.dram_tensor("bout", [1, D], F32, kind="ExternalInput")
    d_clipx = nc.dram_tensor("clipx", [1, 1], F32, kind="ExternalInput")
    d_clipd = nc.dram_tensor("clipd", [1, 1], F32, kind="ExternalInput")
    d_fg_n = nc.dram_tensor("fg_n", [DI, D], F32, kind="ExternalInput")
    d_fu_n = nc.dram_tensor("fu_n", [DI, D], F32, kind="ExternalInput")
    d_fd_n = nc.dram_tensor("fd_n", [D, DI], F32, kind="ExternalInput")
    d_fg_t = nc.dram_tensor("fg_t", [D, DI], F32, kind="ExternalInput")
    d_fu_t = nc.dram_tensor("fu_t", [D, DI], F32, kind="ExternalInput")
    d_fd_t = nc.dram_tensor("fd_t", [DI, D], F32, kind="ExternalInput")
    d_eye = nc.dram_tensor("eye", [P, P], F32, kind="ExternalInput")
    d_ones_r = nc.dram_tensor("ones_r", [1, P], F32, kind="ExternalInput")
    d_ones_c = nc.dram_tensor("ones_c", [P, 1], F32, kind="ExternalInput")
    d_hones = nc.dram_tensor("hones", [1, T], F32, kind="ExternalInput")
    d_out = nc.dram_tensor("out", [T, D], F32, kind="ExternalOutput")
    d_hsp = nc.dram_tensor("hsp", [NI, P, T], F32)
    d_wdq = nc.dram_tensor("wdq", [NI, P, D], BF16)
    d_cc_in = nc.dram_tensor("cc_in", [1, 8], F32)
    d_cc_out = nc.dram_tensor("cc_out", [1, 8], F32)

    A = nc.alloc_sbuf_tensor
    eye = A("s_eye", [P, P], F32)
    ones_r = A("s_ones_r", [1, P], F32)
    ones_c = A("s_ones_c", [P, 1], F32)
    i16b = A("i16b", [SC, SC], BF16)
    anat_b = A("anat_b", [SC, SC], BF16)
    at_b = A("at_b", [SC, SC], BF16)
    bmt_b = A("bmt_b", [P, ND * SC], BF16)
    qstk = A("qstk", [SC, L * SC], BF16)
    wct = A("wct", [SC + 1, D], BF16)
    b2c = A("s_b2c", [P, ND], F32)
    bgc = A("s_bgc", [P, ND], F32)
    sc1 = A("sc1", [1, 16], F32)
    rowt = A("rowt", [1, P + 8], F32)
    uB = A("uB", [SC, T], BF16)
    hs = A("hs", [SC + 1, T], BF16)
    m_cols = A("m_cols", [P, 2 * NG], F32)
    rstd_cols = A("rstd_cols", [P, 2 * NG], F32)
    mrstd_cols = A("mrstd_cols", [P, 2 * NG], F32)
    amax_g = A("amax_g", [P, NI], F32)
    amax_u = A("amax_u", [P, NI], F32)
    amax_d = A("amax_d", [P, ND], F32)
    sxg = A("sxg", [P, NI], F32)
    sxu = A("sxu", [P, NI], F32)
    cS = A("cS", [P, 8], F32)
    mxc = A("mxc", [P, 2], F32)
    mhc = A("mhc", [P, 2], F32)
    habs_c = A("habs_c", [P, NI], F32)
    nwt = A("nwt", [P, 32], F32)

    CONSTS = [0.0, 0.5, 1.0, MAGIC]
    csb = A("csb", [P, len(CONSTS)], F32)
    for i, v in enumerate(CONSTS):
        nc.const_aps.register(F32, v, csb[:, i:i + 1])

    ps = [nc.alloc_psum_tensor(f"ps{i}", [P, 512], F32) for i in range(8)]

    sems = {e: Sem(nc, "s_" + e) for e in Prog.ENGS}
    dL = Lanes(nc, "dL", 8)
    dS = Lanes(nc, "dS", 4)
    dG = Lanes(nc, "dG", 4)
    scc = Sem(nc, "scc")

    wq_g = A("wq_g", [P, ND * DI], BF16)
    wq_u = A("wq_u", [P, ND * DI], BF16)
    wq_d = wq_u
    from contextlib import ExitStack
    _wstack = ExitStack()
    w2tb = _wstack.enter_context(nc.sbuf_tensor("w2tb", [P, ND * D], BF16))
    wgtb = _wstack.enter_context(nc.sbuf_tensor("wgtb", [P, ND * D], BF16))

    V, Sc, Tn, Gp, Sy = "vector", "scalar", "tensor", "gpsimd", "sync"

    def seq_chain(pg, eng, fns):
        s = pg.sems[eng]
        toks = []
        for fn in fns:
            pg.wait(eng, s, s.n)
            toks.append(pg.op(eng, fn))
        return toks[-1] if toks else s.n

    def col2row(pg, bank, col_ap_fn, dst_ap_fn, dep_v=None, dep_t=None):
        if dep_v is not None:
            pg.wait(Tn, sems[V], dep_v)
        if dep_t is not None:
            pg.wait(Tn, sems[Tn], dep_t)
        t1 = pg.op(Tn, lambda t, _c=col_ap_fn, _b=bank: t.transpose(ps[_b][0:1, 0:P], _c(), eye[:, :]))
        pg.wait(V, sems[Tn], t1)
        return pg.op(V, lambda v, _d=dst_ap_fn, _b=bank: v.tensor_copy(_d(), ps[_b][0:1, 0:P]))

    def bcast_row(pg, bank, row_ap_fn, dst_ap_fn, n, dep_v):
        pg.wait(Tn, sems[V], dep_v)
        t1 = pg.op(Tn, lambda t, _r=row_ap_fn, _b=bank, _n=n: t.matmul(
            ps[_b][:, 0:_n], ones_r[:, :], _r(), start=True, stop=True))
        pg.wait(V, sems[Tn], t1)
        return pg.op(V, lambda v, _d=dst_ap_fn, _b=bank, _n=n: v.tensor_copy(_d(), ps[_b][:, 0:_n]))

    pg = Prog(sems)
    with (
        nc.sbuf_tensor("ldb", [P, 2, D], F32) as ldb,
        nc.sbuf_tensor("ltb", [P, 2, DI // 2], F32) as ltb,
        nc.sbuf_tensor("cmb", [P, ND, SC], F32) as cmb,
        nc.sbuf_tensor("recb", [P, DI], F32) as recb,
        nc.sbuf_tensor("qtmp", [P, DI // 2], F32) as qtmp,
        nc.sbuf_tensor("rrow", [1, 512], F32) as rrow,
        nc.sbuf_tensor("dstage", [P, 2, D], BF16) as dstage,
        nc.sbuf_tensor("s_bout", [1, D], F32) as bout,
    ):
        ld0_toks = []
        for o, i_ in [
            (eye[:, :], d_eye[:, :]), (ones_r[:, :], d_ones_r[:, :]),
            (ones_c[:, :], d_ones_c[:, :]), (b2c[:, :], d_b2c[:, :]),
            (bgc[:, :], d_bgc[:, :]), (bout[:, :], d_bout[:, :]),
            (sc1[0:1, 0:1], d_clipx[:, :]), (sc1[0:1, 1:2], d_clipd[:, :]),
            (ldb[0:SC, 0, 0:SC], d_anat[:, :]),
            (ldb[0:SC, 0, 64:64 + SC], d_at[:, :]),
            (cmb[:, :, :], d_cm.rearrange("(c p) n -> p c n", p=P)[:, :, :]),
            (ldb[:, 1, 0:ND * SC].rearrange("p (c n) -> p c n", n=SC), d_bmt.rearrange("(c p) n -> p c n", p=P)[:, :, :]),
        ]:
            ld0_toks.append(pg.dma(Sy, dL, lambda e, _o=o, _i=i_: e.dma_start(out=_o, in_=_i)))
        for t_ in ld0_toks:
            pg.wait_tok(V, t_)
        for i, v in enumerate(CONSTS):
            pg.op(V, lambda vv, _i=i, _v=v: vv.memset(csb[:, _i:_i + 1], _v))
        pg.op(V, lambda v: v.memset(mxc[:, :], 0.0))
        pg.op(V, lambda v: v.memset(mhc[:, :], 0.0))
        pg.op(V, lambda v: v.tensor_copy(anat_b[:, :], ldb[0:SC, 0, 0:SC]))
        pg.op(V, lambda v: v.tensor_copy(at_b[:, :], ldb[0:SC, 0, 64:64 + SC]))
        pg.op(V, lambda v: v.tensor_copy(bmt_b[:, :], ldb[:, 1, 0:ND * SC]))
        v_eye = pg.op(V, lambda v: v.tensor_copy(i16b[:, :], eye[0:SC, 0:SC]))
        pg.wait(V, sems[V], v_eye)
        v_q1 = pg.op(V, lambda v: v.tensor_copy(qstk[:, 0:SC], at_b[:, :]))

        cast_done = {}
        for wi, (wsrc, wdst) in enumerate(((d_w2t, w2tb), (d_wgt, wgtb))):
            for k in range(ND):
                idx = wi * ND + k
                if idx < 2:
                    pg.wait(Sy, sems[V], v_q1)
                else:
                    pg.wait(Sy, sems[V], cast_done[idx - 2])
                lw = pg.dma(Sy, dL, lambda e, _w=wsrc, _k=k, _s=idx % 2: e.dma_start(
                    out=ldb[:, _s, :], in_=_w[_k * P:(_k + 1) * P, :]))
                pg.wait_tok(V, lw)
                cast_done[idx] = pg.op(V, lambda v, _d=wdst, _k=k, _s=idx % 2: v.tensor_copy(
                    _d[:, _k * D:(_k + 1) * D], ldb[:, _s, :]))

        mm_done = {}
        for half in range(2):
            for k in range(ND):
                idx = half * ND + k
                if idx >= 2:
                    pg.wait(Sy, sems[Tn], mm_done[idx - 2])
                lw = pg.dma(Sy, dL, lambda e, _k=k, _h=half, _s=idx % 2: e.dma_start(
                    out=ltb[:, _s, 0:512], in_=d_wot[_k * P:(_k + 1) * P, _h * 512:(_h + 1) * 512]))
                pg.wait_tok(Tn, lw)
                if half == 0 and k == 0:
                    for t_ in ld0_toks:
                        pg.wait_tok(Tn, t_)
                mm_done[idx] = pg.op(Tn, lambda t, _k=k, _h=half, _s=idx % 2: t.matmul(
                    ps[0][0:SC, 0:512], cmb[:, _k, :], ltb[:, _s, 0:512],
                    start=(_k == 0), stop=(_k == ND - 1)))
            pg.wait(V, sems[Tn], mm_done[half * ND + ND - 1])
            ev = pg.op(V, lambda v, _h=half: v.tensor_copy(wct[0:SC, _h * 512:(_h + 1) * 512], ps[0][0:SC, 0:512]))
            pg.wait(Tn, sems[V], ev)

        pg.dma(Gp, dG, lambda e: e.dma_start(out=wct[SC:SC + 1, :], in_=d_bout[:, :]))
        pg.dma(Gp, dG, lambda e: e.dma_start(out=hs[SC:SC + 1, :], in_=d_hones[:, :]))
        pg.wait(Tn, sems[V], v_q1)
        for k in range(1, L):
            tq = pg.op(Tn, lambda t, _k=k: t.matmul(
                ps[1][0:SC, 0:SC], anat_b[:, :], qstk[:, (_k - 1) * SC:_k * SC], start=True, stop=True))
            pg.wait(V, sems[Tn], tq)
            vq = pg.op(V, lambda v, _k=k: v.tensor_copy(qstk[:, _k * SC:(_k + 1) * SC], ps[1][0:SC, 0:SC]))
            pg.wait(Tn, sems[V], vq)

        ldb_free = cast_done[2 * ND - 1]
        ltb_free_t = mm_done[2 * ND - 1]
        ltb_free_v = 0
        for acol, wt_src, wq_dst, wn_src in (
            (amax_g, d_fg_t, wq_g, d_fg_n),
            (amax_u, d_fu_t, wq_u, d_fu_n),
        ):
            red_done = {}
            for oc in range(NI):
                if oc >= 2:
                    pg.wait(Sy, sems[V], red_done[oc - 2])
                else:
                    pg.wait(Sy, sems[V], ldb_free)
                lw = pg.dma(Sy, dL, lambda e, _w=wn_src, _o=oc: e.dma_start(
                    out=ldb[:, _o % 2, :], in_=_w[_o * P:(_o + 1) * P, :]))
                pg.wait_tok(V, lw)
                red_done[oc] = pg.op(V, lambda v, _a=acol, _o=oc: v.tensor_reduce(
                    _a[:, _o:_o + 1], ldb[:, _o % 2, :], AX.X, AL.max, apply_absolute_value=True))
            ldb_free = red_done[NI - 1]
            v_rc = seq_chain(pg, V, [
                lambda v, _a=acol: v.reciprocal(nwt[:, 0:NI], _a[:, :]),
                lambda v: v.tensor_scalar(nwt[:, 0:NI], nwt[:, 0:NI], QMAX, None, AL.mult),
            ])
            e3 = v_rc
            for j in range(6):
                n = min(512, DI - j * 512)
                for jj in range(n // P):
                    oc = j * 4 + jj
                    e3 = col2row(pg, 2, (lambda _o=oc: lambda: nwt[:, _o:_o + 1])(),
                                 (lambda _jj=jj: lambda: rrow[0:1, _jj * P:(_jj + 1) * P])(), dep_v=e3)
                e3 = bcast_row(pg, 2, (lambda _n=n: lambda: rrow[0:1, 0:_n])(),
                               (lambda _j=j, _n=n: lambda: recb[:, _j * 512:_j * 512 + _n])(), n, e3)
            v_recb = e3
            q_done = {}
            HW = DI // 2
            for k2 in range(2 * ND):
                k, hf = divmod(k2, 2)
                if k2 >= 2:
                    pg.wait(Sy, sems[V], q_done[k2 - 2])
                else:
                    pg.wait(Sy, sems[Tn], ltb_free_t)
                    pg.wait(Sy, sems[V], ltb_free_v)
                lw = pg.dma(Sy, dL, lambda e, _w=wt_src, _k=k, _h=hf, _s=k2 % 2: e.dma_start(
                    out=ltb[:, _s, :], in_=_w[_k * P:(_k + 1) * P, _h * HW:(_h + 1) * HW]))
                pg.wait_tok(V, lw)
                pg.wait(V, sems[V], v_recb)
                t1 = pg.op(V, lambda v, _s=k2 % 2, _h=hf: v.tensor_tensor(
                    qtmp[:, :], ltb[:, _s, :], recb[:, _h * HW:(_h + 1) * HW], AL.mult))
                pg.wait(V, sems[V], t1)
                q_done[k2] = pg.op(V, lambda v, _d=wq_dst, _k=k, _h=hf: v.tensor_scalar(
                    _d[:, _k * DI + _h * HW:_k * DI + (_h + 1) * HW], qtmp[:, :], MAGIC, MAGIC, AL.add, AL.subtract))
            ltb_free_v = q_done[2 * ND - 1]
        red_done = {}
        HW = DI // 2
        for oc2 in range(2 * ND):
            oc, hf = divmod(oc2, 2)
            if oc2 >= 2:
                pg.wait(Sy, sems[V], red_done[oc2 - 2])
            else:
                pg.wait(Sy, sems[V], ltb_free_v)
            lw = pg.dma(Sy, dL, lambda e, _o=oc, _h=hf, _s=oc2 % 2: e.dma_start(
                out=ltb[:, _s, :], in_=d_fd_n[_o * P:(_o + 1) * P, _h * HW:(_h + 1) * HW]))
            pg.wait_tok(V, lw)
            if hf == 0:
                red_done[oc2] = pg.op(V, lambda v, _o=oc, _s=oc2 % 2: v.tensor_reduce(
                    nwt[:, 8:9], ltb[:, _s, :], AX.X, AL.max, apply_absolute_value=True))
            else:
                r1 = pg.op(V, lambda v, _s=oc2 % 2: v.tensor_reduce(nwt[:, 9:10], ltb[:, _s, :], AX.X, AL.max, apply_absolute_value=True))
                pg.wait(V, sems[V], r1)
                red_done[oc2] = pg.op(V, lambda v, _o=oc: v.tensor_tensor(amax_d[:, _o:_o + 1], nwt[:, 8:9], nwt[:, 9:10], AL.max))
        e0 = seq_chain(pg, V, [
            lambda v: v.reciprocal(nwt[:, 0:ND], amax_d[:, :]),
            lambda v: v.tensor_scalar(nwt[:, 0:ND], nwt[:, 0:ND], QMAX, None, AL.mult),
            lambda v: v.tensor_scalar(nwt[:, 8:8 + ND], amax_d[:, :], 1.0 / QMAX, None, AL.mult),
        ])
        for oc in range(ND):
            e0 = col2row(pg, 2, (lambda _o=oc: lambda: nwt[:, _o:_o + 1])(),
                         (lambda _o=oc: lambda: rrow[0:1, _o % 4 * P:(_o % 4 + 1) * P])(), dep_v=e0)
            if oc % 4 == 3:
                e0 = bcast_row(pg, 2, lambda: rrow[0:1, 0:512],
                               (lambda _j=oc // 4: lambda: recb[:, _j * 512:(_j + 1) * 512])(), 512, e0)
        for oc in range(ND):
            e0 = col2row(pg, 2, (lambda _o=oc: lambda: nwt[:, 8 + _o:9 + _o])(),
                         (lambda _o=oc: lambda: rrow[0:1, _o % 4 * P:(_o % 4 + 1) * P])(), dep_v=e0)
            if oc % 4 == 3:
                e0 = bcast_row(pg, 2, lambda: rrow[0:1, 0:512],
                               (lambda _j=oc // 4: lambda: recb[:, D + _j * 512:D + (_j + 1) * 512])(), 512, e0)
        wq_stage = {}
        st_done = {}
        for oc in range(NI):
            if oc >= 2:
                pg.wait(Sy, sems[V], wq_stage[oc - 2])
            lw = pg.dma(Sy, dL, lambda e, _o=oc: e.dma_start(out=ltb[:, _o % 2, 0:D], in_=d_fd_t[_o * P:(_o + 1) * P, :]))
            pg.wait_tok(V, lw)
            pg.wait(V, sems[V], e0)
            if oc >= 2:
                pg.wait_tok(V, st_done.get(oc - 2, 0))
            t1 = pg.op(V, lambda v, _o=oc: v.tensor_tensor(qtmp[:, 0:D], ltb[:, _o % 2, 0:D], recb[:, 0:D], AL.mult))
            pg.wait(V, sems[V], t1)
            t2 = pg.op(V, lambda v: v.tensor_scalar(qtmp[:, 0:D], qtmp[:, 0:D], MAGIC, MAGIC, AL.add, AL.subtract))
            pg.wait(V, sems[V], t2)
            wq_stage[oc] = pg.op(V, lambda v, _o=oc: v.tensor_tensor(
                dstage[:, _o % 2, :], qtmp[:, 0:D], recb[:, D:2 * D], AL.mult))
            pg.wait(Sc, sems[V], wq_stage[oc])
            st_done[oc] = pg.dma(Sc, dS, lambda s, _o=oc: s.dma_start(out=d_wdq[_o, :, :], in_=dstage[:, _o % 2, :]))
        pg.emit(nc)

    pg = Prog(sems)
    with (
        nc.sbuf_tensor("xtg", [P, 3, ND, GT], F32) as xtg,
        nc.sbuf_tensor("xnb", [P, 2, ND, GT], BF16) as xnb,
        nc.sbuf_tensor("ubf", [P, ND, GT], BF16) as ubf,
        nc.sbuf_tensor("xnf", [P, 2, GT], F32) as xnf,
        nc.sbuf_tensor("xsq", [P, 2, GT], F32) as xsq,
        nc.sbuf_tensor("sgb", [P, 1, GT], F32) as sgb,
        nc.sbuf_tensor("mbA", [P, 2, GT], F32) as mbA,
        nc.sbuf_tensor("rstdbA", [P, 2, GT], F32) as rstdbA,
        nc.sbuf_tensor("rowS", [1, 2, GT], F32) as rowS,
    ):
        ld_done = {}
        sq_done = {}
        stat_mm = {}
        bcast_done = {}
        cast_done = {}
        mx_done = {}
        xtg_rd = {}
        pool_done = {}
        u_done = {}
        sg_done = {}
        attn_done = {}
        ub_evac = {}

        def emit_loads(g):
            if g >= 3:
                pg.wait(Sy, sems[Gp], xtg_rd[g - 3])
            ld_done[g] = [pg.dma(Sy, dL, lambda e, _g=g, _dc=dc: e.dma_start(
                out=xtg[:, _g % 3, _dc, :], in_=d_xt[_dc * P:(_dc + 1) * P, _g * GT:(_g + 1) * GT]))
                for dc in range(ND)]

        stat_evac = {}

        def emit_stat_mms(g):
            b = g % 3
            for t_ in ld_done[g]:
                pg.wait_tok(V, t_)
                pg.wait_tok(Tn, t_)
            if g >= 1:
                pg.wait(Tn, sems[V], stat_evac.get(g - 1, 0))
            for dc in range(ND):
                if (g, dc - 2) in stat_mm or (g - 1, ND + dc - 2) in stat_mm:
                    key = (g, dc - 2) if dc >= 2 else (g - 1, ND + dc - 2)
                    pg.wait(V, sems[Tn], stat_mm[key])
                sq_done[(g, dc)] = pg.op(V, lambda v, _b=b, _dc=dc: v.tensor_tensor(
                    xsq[:, _dc % 2, :], xtg[:, _b, _dc, :], xtg[:, _b, _dc, :], AL.mult))
                pg.op(Tn, lambda t, _b=b, _dc=dc: t.matmul(
                    ps[5][0:1, 0:GT], ones_c[:, :], xtg[:, _b, _dc, :],
                    start=(_dc == 0), stop=(_dc == ND - 1)))
                pg.wait(Tn, sems[V], sq_done[(g, dc)])
                stat_mm[(g, dc)] = pg.op(Tn, lambda t, _dc=dc: t.matmul(
                    ps[6][0:1, 0:GT], ones_c[:, :], xsq[:, _dc % 2, :],
                    start=(_dc == 0), stop=(_dc == ND - 1)))
            stat_mm[g] = stat_mm[(g, ND - 1)]

        def emit_stat_vec(g):
            b = g % 3
            br = g % 2
            g2 = 2 * g
            pg.wait(V, sems[Tn], stat_mm[g])
            e1 = pg.op(V, lambda v, _br=br: v.tensor_copy(rowS[0:1, _br, :], ps[5][0:1, 0:GT]))
            for j in range(2):
                col2row_inv = pg
                pg.wait(Tn, sems[V], e1)
                tt = pg.op(Tn, lambda t, _br=br, _j=j: t.transpose(
                    ps[7][0:P, 0:1], rowS[0:1, _br, _j * P:(_j + 1) * P], eye[0:1, 0:1]))
                pg.wait(V, sems[Tn], tt)
                e1 = pg.op(V, lambda v, _j=j: v.tensor_copy(nwt[:, _j:_j + 1], ps[7][0:P, 0:1]))
            pg.wait(V, sems[Tn], sems[Tn].n)
            e2 = pg.op(V, lambda v, _br=br: v.tensor_copy(rowS[0:1, _br, :], ps[6][0:1, 0:GT]))
            stat_evac[g] = e2
            for j in range(2):
                pg.wait(Tn, sems[V], e2)
                tt = pg.op(Tn, lambda t, _br=br, _j=j: t.transpose(
                    ps[7][0:P, 0:1], rowS[0:1, _br, _j * P:(_j + 1) * P], eye[0:1, 0:1]))
                pg.wait(V, sems[Tn], tt)
                e2 = pg.op(V, lambda v, _j=j: v.tensor_copy(nwt[:, 2 + _j:3 + _j], ps[7][0:P, 0:1]))
            seq_chain(pg, V, [
                lambda v, _g2=g2: v.tensor_scalar(m_cols[:, _g2:_g2 + 2], nwt[:, 0:2], 1.0 / D, None, AL.mult),
                lambda v, _g2=g2: v.tensor_tensor(nwt[:, 4:6], m_cols[:, _g2:_g2 + 2], m_cols[:, _g2:_g2 + 2], AL.mult),
                lambda v: v.scalar_tensor_tensor(nwt[:, 6:8], nwt[:, 2:4], 1.0 / D, nwt[:, 4:6], AL.mult, AL.subtract),
                lambda v: v.tensor_scalar(nwt[:, 6:8], nwt[:, 6:8], 1e-5, None, AL.add),
                lambda v: v.tensor_scalar(nwt[:, 8:10], nwt[:, 6:8], 0.5, 0.5, AL.mult, AL.add),
                lambda v: v.reciprocal(nwt[:, 10:12], nwt[:, 8:10]),
                lambda v: v.tensor_tensor(nwt[:, 12:14], nwt[:, 10:12], nwt[:, 10:12], AL.mult),
                lambda v: v.tensor_tensor(nwt[:, 12:14], nwt[:, 12:14], nwt[:, 6:8], AL.mult),
                lambda v: v.tensor_scalar(nwt[:, 12:14], nwt[:, 12:14], -0.5, 1.5, AL.mult, AL.add),
                lambda v: v.tensor_tensor(nwt[:, 10:12], nwt[:, 10:12], nwt[:, 12:14], AL.mult),
                lambda v: v.tensor_tensor(nwt[:, 12:14], nwt[:, 10:12], nwt[:, 10:12], AL.mult),
                lambda v: v.tensor_tensor(nwt[:, 12:14], nwt[:, 12:14], nwt[:, 6:8], AL.mult),
                lambda v: v.tensor_scalar(nwt[:, 12:14], nwt[:, 12:14], -0.5, 1.5, AL.mult, AL.add),
                lambda v: v.tensor_tensor(nwt[:, 10:12], nwt[:, 10:12], nwt[:, 12:14], AL.mult),
                lambda v: v.tensor_tensor(nwt[:, 12:14], nwt[:, 10:12], nwt[:, 10:12], AL.mult),
                lambda v: v.tensor_tensor(nwt[:, 12:14], nwt[:, 12:14], nwt[:, 6:8], AL.mult),
                lambda v: v.tensor_scalar(nwt[:, 12:14], nwt[:, 12:14], -0.5, 1.5, AL.mult, AL.add),
                lambda v: v.tensor_tensor(nwt[:, 10:12], nwt[:, 10:12], nwt[:, 12:14], AL.mult),
                lambda v: v.tensor_tensor(nwt[:, 12:14], nwt[:, 10:12], nwt[:, 10:12], AL.mult),
                lambda v: v.tensor_tensor(nwt[:, 12:14], nwt[:, 12:14], nwt[:, 6:8], AL.mult),
                lambda v: v.tensor_scalar(nwt[:, 12:14], nwt[:, 12:14], -0.5, 1.5, AL.mult, AL.add),
                lambda v, _g2=g2: v.tensor_tensor(rstd_cols[:, _g2:_g2 + 2], nwt[:, 10:12], nwt[:, 12:14], AL.mult),
                lambda v, _g2=g2: v.tensor_tensor(mrstd_cols[:, _g2:_g2 + 2], m_cols[:, _g2:_g2 + 2], rstd_cols[:, _g2:_g2 + 2], AL.mult),
            ])
        def emit_stat_bcast(g):
            br = g % 2
            bm = g % 2
            g2 = 2 * g
            if g >= 2:
                pg.wait(V, sems[Gp], pool_done.get((g - 2, ND - 1), 0))
            e3 = sems[V].n
            for j in range(2):
                e3 = col2row(pg, 7, (lambda _g2=g2, _j=j: lambda: m_cols[:, _g2 + _j:_g2 + _j + 1])(),
                             (lambda _br=br, _j=j: lambda: rowS[0:1, _br, _j * P:(_j + 1) * P])(), dep_v=e3)
            e3 = bcast_row(pg, 7, (lambda _br=br: lambda: rowS[0:1, _br, :])(),
                           (lambda _bm=bm: lambda: mbA[:, _bm, :])(), GT, e3)
            for j in range(2):
                e3 = col2row(pg, 7, (lambda _g2=g2, _j=j: lambda: rstd_cols[:, _g2 + _j:_g2 + _j + 1])(),
                             (lambda _br=br, _j=j: lambda: rowS[0:1, _br, _j * P:(_j + 1) * P])(), dep_v=e3)
            e3 = bcast_row(pg, 7, (lambda _br=br: lambda: rowS[0:1, _br, :])(),
                           (lambda _bm=bm: lambda: rstdbA[:, _bm, :])(), GT, e3)
            bcast_done[g] = e3

        def emit_xn(g):
            b = g % 3
            bx = g % 2
            bm = g % 2
            pg.wait(Gp, sems[V], bcast_done[g])
            for t_ in ld_done[g]:
                pg.wait_tok(Gp, t_)
            pg.wait(Gp, sems[Tn], attn_done.get(g - 2, 0))
            for dc in range(ND):
                e_ = dc % 2
                idx = g * ND + dc
                if dc >= 2:
                    pg.wait(Gp, sems[V], mx_done[(g, dc - 2)])
                p1 = pg.op(Gp, lambda p_, _b=b, _bm=bm, _dc=dc, _e=e_: p_.tensor_tensor(
                    xnf[:, _e, :], xtg[:, _b, _dc, :], mbA[:, _bm, :], AL.subtract))
                pg.wait(Gp, sems[Gp], p1)
                pool_done[(g, dc)] = pg.op(Gp, lambda p_, _bm=bm, _e=e_: p_.tensor_tensor(
                    xnf[:, _e, :], xnf[:, _e, :], rstdbA[:, _bm, :], AL.mult))
                pg.wait(Gp, sems[Gp], pool_done[(g, dc)])
                cast_done[(g, dc)] = pg.op(Gp, lambda p_, _bx=bx, _dc=dc, _e=e_: p_.tensor_copy(
                    xnb[:, _bx, _dc, :], xnf[:, _e, :]))
                pg.wait(V, sems[Gp], pool_done[(g, dc)])
                r1 = pg.op(V, lambda v, _e=e_: v.tensor_reduce(nwt[:, 14:15], xnf[:, _e, :], AX.X, AL.max, apply_absolute_value=True))
                pg.wait(V, sems[V], r1)
                pp = idx % 2
                mx_done[(g, dc)] = pg.op(V, lambda v, _pp=pp: v.tensor_tensor(
                    mxc[:, 1 - _pp:2 - _pp], mxc[:, _pp:_pp + 1], nwt[:, 14:15], AL.max))
            xtg_rd[g] = cast_done[(g, ND - 1)]

        def emit_attn(g):
            b = g % 3
            bx = g % 2
            for dc in range(ND):
                e_ = dc % 2
                idx = g * ND + dc
                psA, psB = (0, 1) if e_ == 0 else (2, 3)
                pg.wait(Tn, sems[Gp], cast_done[(g, ND - 1)])
                if idx >= 2:
                    prev = divmod(idx - 2, ND)
                    pg.wait(Tn, sems[V], u_done.get(prev, 0))
                    pg.wait(Tn, sems[Sc], sg_done.get(prev, 0))
                for k in range(ND):
                    pg.op(Tn, lambda t, _pA=psA, _bx=bx, _dc=dc, _k=k: t.matmul(
                        ps[_pA][:, 0:GT], w2tb[:, _k * D + _dc * P:_k * D + (_dc + 1) * P], xnb[:, _bx, _k, :],
                        start=(_k == 0), stop=(_k == ND - 1)))
                for k in range(ND):
                    pg.op(Tn, lambda t, _pB=psB, _bx=bx, _dc=dc, _k=k: t.matmul(
                        ps[_pB][:, 0:GT], wgtb[:, _k * D + _dc * P:_k * D + (_dc + 1) * P], xnb[:, _bx, _k, :],
                        start=(_k == 0), stop=(_k == ND - 1)))
                t_mm = sems[Tn].n
                pg.wait(Sc, sems[Tn], t_mm)
                if idx >= 1:
                    prev = divmod(idx - 1, ND)
                    pg.wait(Sc, sems[V], u_done.get(prev, 0))
                sg_done[(g, dc)] = pg.op(Sc, lambda s, _pB=psB, _dc=dc: s.activation(
                    sgb[:, 0, :], ps[_pB][:, 0:GT], AF.Sigmoid, bias=bgc[:, _dc:_dc + 1], scale=1.0))
                pg.wait(V, sems[Sc], sg_done[(g, dc)])
                if dc == 0:
                    pg.wait(V, sems[Tn], attn_done.get(g - 1, 0))
                u_done[(g, dc)] = pg.op(V, lambda v, _pA=psA, _dc=dc: v.scalar_tensor_tensor(
                    ubf[:, _dc, :], ps[_pA][:, 0:GT], b2c[:, _dc:_dc + 1], sgb[:, 0, :], AL.add, AL.mult))
                if g + 1 < NG:
                    gn = g + 1
                    bn, bxn, bmn, en = gn % 3, gn % 2, gn % 2, dc % 2
                    if dc == 0:
                        pg.wait(Gp, sems[V], bcast_done[gn])
                        for t_ in ld_done[gn]:
                            pg.wait_tok(Gp, t_)
                        pg.wait(Gp, sems[Tn], attn_done.get(gn - 2, 0))
                    if dc >= 2:
                        pg.wait(Gp, sems[V], mx_done[(gn, dc - 2)])
                    p1 = pg.op(Gp, lambda p_, _b=bn, _bm=bmn, _dc=dc, _e=en: p_.tensor_tensor(
                        xnf[:, _e, :], xtg[:, _b, _dc, :], mbA[:, _bm, :], AL.subtract))
                    pg.wait(Gp, sems[Gp], p1)
                    pool_done[(gn, dc)] = pg.op(Gp, lambda p_, _bm=bmn, _e=en: p_.tensor_tensor(
                        xnf[:, _e, :], xnf[:, _e, :], rstdbA[:, _bm, :], AL.mult))
                    pg.wait(Gp, sems[Gp], pool_done[(gn, dc)])
                    cast_done[(gn, dc)] = pg.op(Gp, lambda p_, _bx=bxn, _dc=dc, _e=en: p_.tensor_copy(
                        xnb[:, _bx, _dc, :], xnf[:, _e, :]))
                    pg.wait(V, sems[Gp], pool_done[(gn, dc)])
                    r1 = pg.op(V, lambda v, _e=en: v.tensor_reduce(
                        nwt[:, 14:15], xnf[:, _e, :], AX.X, AL.max, apply_absolute_value=True))
                    pg.wait(V, sems[V], r1)
                    ppn = (gn * ND + dc) % 2
                    mx_done[(gn, dc)] = pg.op(V, lambda v, _pp=ppn: v.tensor_tensor(
                        mxc[:, 1 - _pp:2 - _pp], mxc[:, _pp:_pp + 1], nwt[:, 14:15], AL.max))
                    if dc == ND - 1:
                        xtg_rd[gn] = cast_done[(gn, dc)]
            pg.wait(Tn, sems[V], u_done[(g, ND - 1)])
            if g >= 1:
                pg.wait(Tn, sems[V], ub_evac.get(g - 1, 0))
            for k in range(ND):
                pg.op(Tn, lambda t, _k=k: t.matmul(
                    ps[4][0:SC, 0:GT], bmt_b[:, _k * SC:(_k + 1) * SC], ubf[:, _k, :],
                    start=(_k == 0), stop=(_k == ND - 1)))
            attn_done[g] = sems[Tn].n
            pg.wait(V, sems[Tn], attn_done[g])
            ub_evac[g] = pg.op(V, lambda v, _g=g: v.tensor_copy(uB[:, _g * GT:(_g + 1) * GT], ps[4][0:SC, 0:GT]))

        for gg in (0, 1, 2):
            emit_loads(gg)
            emit_stat_mms(gg)
            emit_stat_vec(gg)
        emit_stat_bcast(0)
        emit_stat_bcast(1)
        emit_xn(0)
        emit_loads(3)
        emit_stat_mms(3)
        emit_stat_vec(3)
        for g in range(NG):
            if g + 2 < NG:
                emit_stat_bcast(g + 2)
            emit_attn(g)
            if g + 4 < NG:
                emit_loads(g + 4)
                emit_stat_mms(g + 4)
                emit_stat_vec(g + 4)

        fin = (NG * ND) % 2
        fx = col2row(pg, 7, lambda: mxc[:, fin:fin + 1], lambda: rowt[0:1, 0:P], dep_v=mx_done[(NG - 1, ND - 1)])
        pg.wait(V, sems[V], fx)
        vmx = pg.op(V, lambda v: v.tensor_reduce(sc1[0:1, 2:3], rowt[0:1, 0:P], AX.X, AL.max))
        pg.wait(Sc, sems[V], vmx)
        stg = pg.op(Sc, lambda s: s.activation(rowt[0:1, P:P + 8], ones_r[0:1, 0:8], AF.Identity, bias=0.0, scale=sc1[0:1, 2:3]))
        pg.wait(Gp, sems[Sc], stg)
        tk1 = pg.dma(Gp, dG, lambda e: e.dma_start(out=d_cc_in[:, :], in_=rowt[0:1, P:P + 8]))
        pg.wait_tok(Gp, tk1)
        cc_tok = pg.op(Gp, lambda e: e.collective_compute(
            "AllReduce", AL.max, replica_groups=[list(range(8))],
            ins=[d_cc_in[:, :]], outs=[d_cc_out[:, :]]))
        pg.wait(Gp, sems[Gp], cc_tok)
        tk2 = pg.dma(Gp, dG, lambda e: e.dma_start(out=sc1[0:1, 2:3], in_=d_cc_out[0:1, 0:1]))
        pg.wait_tok(Sc, tk2)
        sgm = pg.op(Sc, lambda s: s.activation(sc1[0:1, 3:4], sc1[0:1, 0:1], AF.Sigmoid, bias=0.0, scale=1.0))
        pg.wait(V, sems[Sc], sgm)
        seq_chain(pg, V, [
            lambda v: v.tensor_tensor(sc1[0:1, 4:5], sc1[0:1, 3:4], sc1[0:1, 2:3], AL.mult),
            lambda v: v.tensor_scalar(sc1[0:1, 4:5], sc1[0:1, 4:5], 1.0 / QMAX, None, AL.mult),
            lambda v: v.reciprocal(sc1[0:1, 5:6], sc1[0:1, 4:5]),
        ])
        vsx = sems[V].n
        pg.wait(Tn, sems[V], vsx)
        tb = pg.op(Tn, lambda t: t.matmul(ps[7][:, 0:2], ones_r[:, :], sc1[0:1, 4:6], start=True, stop=True))
        pg.wait(V, sems[Tn], tb)
        seq_chain(pg, V, [
            lambda v: v.tensor_copy(cS[:, 2:3], ps[7][:, 0:1]),
            lambda v: v.tensor_copy(cS[:, 0:1], ps[7][:, 1:2]),
            lambda v: v.tensor_scalar(cS[:, 1:2], ps[7][:, 1:2], -1.0, None, AL.mult),
            lambda v: v.tensor_scalar(sxg[:, :], amax_g[:, :], 1.0 / QMAX, None, AL.mult),
            lambda v: v.tensor_scalar_mul(sxg[:, :], sxg[:, :], cS[:, 2:3]),
            lambda v: v.tensor_scalar(sxu[:, :], amax_u[:, :], 1.0 / QMAX, None, AL.mult),
            lambda v: v.tensor_scalar_mul(sxu[:, :], sxu[:, :], cS[:, 2:3]),
        ])
        pg.emit(nc)
    _wstack.close()

    pg = Prog(sems)
    with (
        nc.sbuf_tensor("xtg2", [P, 3, ND, GT], F32) as xtg2,
        nc.sbuf_tensor("zq", [P, 2, ND, GT], BF16) as zq,
        nc.sbuf_tensor("yt", [P, 2, GT], F32) as yt,
        nc.sbuf_tensor("sg2", [P, 2, GT], F32) as sg2,
        nc.sbuf_tensor("hb", [P, 4, GT], F32) as hb,
        nc.sbuf_tensor("c1b", [P, 3, GT], F32) as c1b,
        nc.sbuf_tensor("c0b", [P, 3, GT], F32) as c0b,
        nc.sbuf_tensor("rowB", [1, 3, GT], F32) as rowB,
        nc.sbuf_tensor("fstk", [SC, NCHK], BF16) as fstk,
        nc.sbuf_tensor("scr16", [SC, NCHK], F32) as scr16,
    ):
        ldB = {}
        c_done = {}
        z_done = {}
        gu_done = {}
        h_done = {}
        sil_done = {}
        spill = {}
        habs_done = {}
        scan_v = {}
        scan_emitted = [0]

        def emit_loadsB(g):
            if g >= 3:
                pg.wait(Sy, sems[Gp], z_done[g - 3])
            ldB[g] = [pg.dma(Sy, dL, lambda e, _g=g, _dc=dc: e.dma_start(
                out=xtg2[:, _g % 3, _dc, :], in_=d_xt[_dc * P:(_dc + 1) * P, _g * GT:(_g + 1) * GT]))
                for dc in range(ND)]

        c_tn = {}

        def emit_cchain(g):
            b = g % 3
            g2 = 2 * g
            o = 16 + 4 * (g % 2)
            pg.wait(V, sems[Tn], c_tn.get(g - 2, 0))
            e0 = seq_chain(pg, V, [
                lambda v, _g2=g2, _o=o: v.tensor_scalar_mul(nwt[:, _o:_o + 2], rstd_cols[:, _g2:_g2 + 2], cS[:, 0:1]),
                lambda v, _g2=g2, _o=o: v.tensor_scalar_mul(nwt[:, _o + 2:_o + 4], mrstd_cols[:, _g2:_g2 + 2], cS[:, 1:2]),
            ])
            if g >= 3:
                pg.wait(V, sems[Gp], c_done.get(("pool", g - 3), 0))
            e3 = e0
            for j in range(2):
                e3 = col2row(pg, 5, (lambda _j=j, _o=o: lambda: nwt[:, _o + _j:_o + _j + 1])(),
                             (lambda _b=b, _j=j: lambda: rowB[0:1, _b, _j * P:(_j + 1) * P])(), dep_v=e3)
            e3 = bcast_row(pg, 5, (lambda _b=b: lambda: rowB[0:1, _b, :])(),
                           (lambda _b=b: lambda: c1b[:, _b, :])(), GT, e3)
            for j in range(2):
                e3 = col2row(pg, 5, (lambda _j=j, _o=o: lambda: nwt[:, _o + 2 + _j:_o + 3 + _j])(),
                             (lambda _b=b, _j=j: lambda: rowB[0:1, _b, _j * P:(_j + 1) * P])(), dep_v=e3)
            e3 = bcast_row(pg, 5, (lambda _b=b: lambda: rowB[0:1, _b, :])(),
                           (lambda _b=b: lambda: c0b[:, _b, :])(), GT, e3)
            c_done[g] = e3
            c_tn[g] = sems[Tn].n

        def emit_xq(g):
            b = g % 3
            bz = g % 2
            for t_ in ldB[g]:
                pg.wait_tok(Gp, t_)
            pg.wait(Gp, sems[V], c_done[g])
            for dc in range(ND):
                e_ = dc % 2
                idx = g * ND + dc
                if idx >= 2:
                    pg.wait(Gp, sems[Gp], z_done.get(divmod(idx - 2, ND), 0))
                p1 = pg.op(Gp, lambda p_, _b=b, _dc=dc, _e=e_: p_.tensor_tensor(
                    yt[:, _e, :], xtg2[:, _b, _dc, :], c1b[:, _b, :], AL.mult))
                pg.wait(Gp, sems[Gp], p1)
                p2 = pg.op(Gp, lambda p_, _b=b, _e=e_: p_.tensor_tensor(
                    yt[:, _e, :], yt[:, _e, :], c0b[:, _b, :], AL.add))
                c_done[("pool", g)] = p2
                pg.wait(Gp, sems[Gp], p2)
                if idx >= 2 and dc < 2:
                    pg.wait(Gp, sems[Tn], gu_done.get((g - 2, NI - 1), 0))
                t1 = pg.op(Gp, lambda v, _e=e_: v.tensor_scalar(yt[:, _e, :], yt[:, _e, :], MAGIC, MAGIC, AL.add, AL.subtract))
                pg.wait(Gp, sems[Gp], t1)
                z_done[divmod(idx, ND)] = pg.op(Gp, lambda v, _bz=bz, _dc=dc, _e=e_: v.tensor_scalar(
                    zq[:, _bz, _dc, :], yt[:, _e, :], -128.0, 127.0, AL.max, AL.min))
            z_done[g] = z_done[(g, ND - 1)]

        def emit_gateup(g):
            b = g % 2
            pg.wait(Tn, sems[Gp], z_done[g])
            for oc in range(NI):
                e_ = oc % 2
                idx = g * NI + oc
                psG, psU = (0, 1) if e_ == 0 else (2, 3)
                if idx >= 2:
                    prev = divmod(idx - 2, NI)
                    pg.wait(Tn, sems[Sc], sil_done.get(prev, 0))
                    pg.wait(Tn, sems[V], h_done.get(prev, 0))
                for k in range(ND):
                    pg.op(Tn, lambda t, _p=psG, _b=b, _oc=oc, _k=k: t.matmul(
                        ps[_p][:, 0:GT], wq_g[:, _k * DI + _oc * P:_k * DI + (_oc + 1) * P], zq[:, _b, _k, :],
                        start=(_k == 0), stop=(_k == ND - 1)))
                for k in range(ND):
                    pg.op(Tn, lambda t, _p=psU, _b=b, _oc=oc, _k=k: t.matmul(
                        ps[_p][:, 0:GT], wq_u[:, _k * DI + _oc * P:_k * DI + (_oc + 1) * P], zq[:, _b, _k, :],
                        start=(_k == 0), stop=(_k == ND - 1)))
                gu_done[(g, oc)] = sems[Tn].n
                pg.wait(Sc, sems[Tn], gu_done[(g, oc)])
                if idx >= 2:
                    pg.wait(Sc, sems[V], h_done.get(divmod(idx - 2, NI), 0))
                sil_done[(g, oc)] = pg.op(Sc, lambda s, _p=psG, _oc=oc, _e=e_: s.activation(
                    sg2[:, _e, :], ps[_p][:, 0:GT], AF.Silu, bias=0.0, scale=sxg[:, _oc:_oc + 1]))
                hslot = idx % 4
                pg.wait(V, sems[Sc], sil_done[(g, oc)])
                if idx >= 4:
                    pg.wait_tok(V, spill.get(divmod(idx - 4, NI), 0))
                hv = pg.op(V, lambda v, _p=psU, _oc=oc, _e=e_, _h=hslot: v.scalar_tensor_tensor(
                    hb[:, _h, :], ps[_p][:, 0:GT], sxu[:, _oc:_oc + 1], sg2[:, _e, :], AL.mult, AL.mult))
                pg.wait(V, sems[V], hv)
                h_done[(g, oc)] = pg.op(V, lambda v, _oc=oc, _h=hslot: v.tensor_reduce(
                    habs_c[:, _oc:_oc + 1], hb[:, _h, :], AX.X, AL.max, apply_absolute_value=True))
                pg.wait(Sc, sems[V], h_done[(g, oc)])
                spill[(g, oc)] = pg.dma(Sc, dS, lambda s, _g=g, _oc=oc, _h=hslot: s.dma_start(
                    out=d_hsp[_oc, :, _g * GT:(_g + 1) * GT], in_=hb[:, _h, :]))
            pp = g % 2
            pg.wait(V, sems[V], h_done[(g, NI - 1)])
            r1 = pg.op(V, lambda v: v.tensor_reduce(nwt[:, 14:15], habs_c[:, :], AX.X, AL.max))
            pg.wait(V, sems[V], r1)
            habs_done[g] = pg.op(V, lambda v, _pp=pp: v.tensor_tensor(
                mhc[:, 1 - _pp:2 - _pp], mhc[:, _pp:_pp + 1], nwt[:, 14:15], AL.max))

        def emit_scan_steps(n):
            for _ in range(n):
                tstep = scan_emitted[0] + 1
                if tstep >= L:
                    return
                scan_emitted[0] = tstep
                pg.wait(Tn, sems[V], scan_v[tstep - 1])
                pg.op(Tn, lambda t, _t=tstep: t.matmul(
                    ps[4][0:SC, 0:NCHK], at_b[:, :], hs[0:SC, (_t - 1):T:L], start=True, stop=False))
                tq = pg.op(Tn, lambda t, _t=tstep: t.matmul(
                    ps[4][0:SC, 0:NCHK], i16b[:, :], uB[:, _t:T:L], start=False, stop=True))
                pg.wait(V, sems[Tn], tq)
                scan_v[tstep] = pg.op(V, lambda v, _t=tstep: v.tensor_copy(hs[0:SC, _t:T:L], ps[4][0:SC, 0:NCHK]))

        scan_v[0] = pg.op(V, lambda v: v.tensor_copy(hs[0:SC, 0:T:L], uB[:, 0:T:L]))

        for gg in (0, 1):
            emit_loadsB(gg)
            emit_cchain(gg)
        emit_xq(0)
        for g in range(NG):
            emit_gateup(g)
            if g + 1 < NG:
                emit_xq(g + 1)
            if g + 2 < NG:
                emit_loadsB(g + 2)
                emit_cchain(g + 2)
            emit_scan_steps(2)
        emit_scan_steps(L)
        pg.wait(V, sems[V], scan_v[L - 1])
        f0 = pg.op(V, lambda v: v.memset(fstk[:, 0:NCHK:CPS], 0.0))
        fprev = f0
        q64 = (L - 1) * SC
        for c in range(1, CPS):
            pg.wait(Tn, sems[V], fprev)
            pg.op(Tn, lambda t, _c=c: t.matmul(
                ps[4][0:SC, 0:4], qstk[:, q64:q64 + SC], fstk[:, (_c - 1):NCHK:CPS], start=True, stop=False))
            tq = pg.op(Tn, lambda t, _c=c: t.matmul(
                ps[4][0:SC, 0:4], i16b[:, :], hs[0:SC, ((_c - 1) * L + L - 1):T:SEQ], start=False, stop=True))
            pg.wait(V, sems[Tn], tq)
            fprev = pg.op(V, lambda v, _c=c: v.tensor_copy(fstk[:, _c:NCHK:CPS], ps[4][0:SC, 0:4]))
        corr_prev = fprev
        for tstep in range(L):
            pg.wait(Tn, sems[V], corr_prev)
            tq = pg.op(Tn, lambda t, _t=tstep: t.matmul(
                ps[4][0:SC, 0:NCHK], qstk[:, _t * SC:(_t + 1) * SC], fstk[:, :], start=True, stop=True))
            pg.wait(V, sems[Tn], tq)
            a1 = pg.op(V, lambda v, _t=tstep: v.tensor_tensor(scr16[:, :], hs[0:SC, _t:T:L], ps[4][0:SC, 0:NCHK], AL.add))
            pg.wait(V, sems[V], a1)
            corr_prev = pg.op(V, lambda v, _t=tstep: v.tensor_copy(hs[0:SC, _t:T:L], scr16[:, :]))

        finh = NG % 2
        fh = col2row(pg, 7, lambda: mhc[:, finh:finh + 1], lambda: rowt[0:1, 0:P], dep_v=habs_done[NG - 1])
        pg.wait(V, sems[V], fh)
        vmh = pg.op(V, lambda v: v.tensor_reduce(sc1[0:1, 2:3], rowt[0:1, 0:P], AX.X, AL.max))
        pg.wait(Sc, sems[V], vmh)
        stg = pg.op(Sc, lambda s: s.activation(rowt[0:1, P:P + 8], ones_r[0:1, 0:8], AF.Identity, bias=0.0, scale=sc1[0:1, 2:3]))
        pg.wait(Gp, sems[Sc], stg)
        tk1 = pg.dma(Gp, dG, lambda e: e.dma_start(out=d_cc_in[:, :], in_=rowt[0:1, P:P + 8]))
        pg.wait_tok(Gp, tk1)
        cc_tok = pg.op(Gp, lambda e: e.collective_compute(
            "AllReduce", AL.max, replica_groups=[list(range(8))],
            ins=[d_cc_in[:, :]], outs=[d_cc_out[:, :]]))
        pg.wait(Gp, sems[Gp], cc_tok)
        tk2 = pg.dma(Gp, dG, lambda e: e.dma_start(out=sc1[0:1, 2:3], in_=d_cc_out[0:1, 0:1]))
        pg.wait_tok(Sc, tk2)
        sgm = pg.op(Sc, lambda s: s.activation(sc1[0:1, 3:4], sc1[0:1, 1:2], AF.Sigmoid, bias=0.0, scale=1.0))
        pg.wait(V, sems[Sc], sgm)
        seq_chain(pg, V, [
            lambda v: v.tensor_tensor(sc1[0:1, 4:5], sc1[0:1, 3:4], sc1[0:1, 2:3], AL.mult),
            lambda v: v.tensor_scalar(sc1[0:1, 4:5], sc1[0:1, 4:5], 1.0 / QMAX, None, AL.mult),
            lambda v: v.reciprocal(sc1[0:1, 5:6], sc1[0:1, 4:5]),
        ])
        vsd = sems[V].n
        pg.wait(Tn, sems[V], vsd)
        tb = pg.op(Tn, lambda t: t.matmul(ps[7][:, 0:2], ones_r[:, :], sc1[0:1, 4:6], start=True, stop=True))
        pg.wait(V, sems[Tn], tb)
        seq_chain(pg, V, [
            lambda v: v.tensor_copy(cS[:, 6:7], ps[7][:, 0:1]),
            lambda v: v.tensor_copy(cS[:, 3:4], ps[7][:, 1:2]),
        ])
        pg.emit(nc)

    pg = Prog(sems)
    with (
        nc.sbuf_tensor("hre", [P, 2, 4, GT], F32) as hre,
        nc.sbuf_tensor("yq", [P, 2, 4 * GT], F32) as yq,
        nc.sbuf_tensor("zh", [P, 2, NI, GT], BF16) as zh,
        nc.sbuf_tensor("xres", [P, 2, D], F32) as xres,
        nc.sbuf_tensor("outb", [P, 2, D], F32) as outb,
    ):
        wdl = [pg.dma(Sy, dL, lambda e, _o=oc: e.dma_start(out=wq_d[:, _o * D:(_o + 1) * D], in_=d_wdq[_o, :, :]))
               for oc in range(NI)]
        for t_ in wdl:
            pg.wait_tok(Tn, t_)
        ldh = {}
        sA_done = {}
        zq_done = {}
        acc_done = {}
        ev_done = {}
        outd = {}

        BW = 4
        NB = (NI + BW - 1) // BW

        def emit_hq(g):
            b = g % 2
            for ib in range(NB):
                o0 = ib * BW
                nch = min(BW, NI - o0)
                idx = g * NB + ib
                e_ = idx % 2
                if idx >= 2:
                    pg.wait(Sy, sems[V], sA_done[idx - 2])
                ldh[idx] = pg.dma(Sy, dL, lambda e, _g=g, _o=o0, _n=nch, _s=idx % 2: e.dma_start(
                    out=hre[:, _s, 0:_n, :], in_=d_hsp[_o:_o + _n, :, _g * GT:(_g + 1) * GT].rearrange("c p t -> p c t")))
                Qe = V
                pg.wait_tok(Qe, ldh[idx])
                if idx >= 2:
                    pg.wait(Qe, sems[V], zq_done.get(idx - 2, (None, 0))[1] if False else 0)
                if idx >= 2:
                    pt = zq_done[idx - 2]
                    pg.wait(Qe, pt[0], pt[1])
                sA_done[idx] = pg.op(Qe, lambda v, _s=idx % 2, _e=e_, _n=nch: v.tensor_scalar(
                    yq[:, _e, 0:_n * GT], hre[:, _s, 0:_n, :].rearrange("p c t -> p (c t)"),
                    cS[:, 3:4], MAGIC, AL.mult, AL.add))
                pg.wait(Qe, pg.sems[Qe], sA_done[idx])
                if g >= 2 and ib == 0:
                    pg.wait(Qe, sems[Tn], acc_done.get((g - 2, NTT - 1), 0))
                t0 = pg.op(Qe, lambda v, _e=e_, _n=nch: v.tensor_scalar(
                    yq[:, _e, 0:_n * GT], yq[:, _e, 0:_n * GT], MAGIC, -128.0, AL.subtract, AL.max))
                pg.wait(Qe, pg.sems[Qe], t0)
                zq_done[idx] = (pg.sems[Qe], pg.op(Qe, lambda v, _b=b, _o=o0, _n=nch, _e=e_: v.tensor_scalar(
                    zh[:, _b, _o:_o + _n, :].rearrange("p c t -> p (c t)"), yq[:, _e, 0:_n * GT],
                    127.0, cS[:, 6:7], AL.min, AL.mult)))
            zq_done[g] = zq_done[g * NB + NB - 1]

        def emit_tiles(g):
            b = g % 2
            pg.wait(Tn, zq_done[g][0], zq_done[g][1])
            if g >= 1:
                pg.wait(Sy, sems[V], ev_done.get((g - 1, NTT - 1), 0))
            ldx = [pg.dma(Sy, dL, lambda e, _g=g, _b=b, _tt=tt: e.dma_start(
                out=xres[:, (_g * NTT + _tt) % 2, :], in_=d_x[(_g * NTT + _tt) * P:(_g * NTT + _tt + 1) * P, :]))
                for tt in range(NTT)]
            for tt in range(NTT):
                ti = g * NTT + tt
                pA, pB = (0, 1) if tt == 0 else (2, 3)
                if ti >= 2:
                    pg.wait(Tn, sems[V], ev_done.get((g - 1, tt), 0))
                for half, bk in ((0, pA), (1, pB)):
                    pg.op(Tn, lambda t, _bk=bk, _h=half, _ti=ti: t.matmul(
                        ps[_bk][:, 0:512], hs[0:SC + 1, _ti * P:(_ti + 1) * P], wct[0:SC + 1, _h * 512:(_h + 1) * 512],
                        start=True, stop=False))
                    for oc in range(NI):
                        pg.op(Tn, lambda t, _bk=bk, _h=half, _b=b, _o=oc, _tt=tt: t.matmul(
                            ps[_bk][:, 0:512], zh[:, _b, _o, _tt * P:(_tt + 1) * P],
                            wq_d[:, _o * D + _h * 512:_o * D + _h * 512 + 512],
                            start=False, stop=(_o == NI - 1)))
                acc_done[(g, tt)] = sems[Tn].n
                pg.wait(V, sems[Tn], acc_done[(g, tt)])
                pg.wait_tok(V, ldx[tt])
                if ti >= 2:
                    pg.wait_tok(V, outd.get(ti - 2, 0))
                pg.op(V, lambda v, _pA=pA, _s=ti % 2: v.tensor_tensor(
                    outb[:, _s, 0:512], ps[_pA][:, 0:512], xres[:, _s, 0:512], AL.add))
                ev_done[(g, tt)] = pg.op(V, lambda v, _pB=pB, _s=ti % 2: v.tensor_tensor(
                    outb[:, _s, 512:1024], ps[_pB][:, 0:512], xres[:, _s, 512:1024], AL.add))
                pg.wait(Gp, sems[V], ev_done[(g, tt)])
                outd[ti] = pg.dma(Gp, dG, lambda e, _ti=ti, _s=ti % 2: e.dma_start(
                    out=d_out[_ti * P:(_ti + 1) * P, :], in_=outb[:, _s, :]))

        emit_hq(0)
        for g in range(NG):
            if g + 1 < NG:
                emit_hq(g + 1)
            emit_tiles(g)
        pg.wait_lanes(Gp, dG)
        pg.emit(nc)
    return nc


_NC_CACHE = {}


def kernel(L0_attn_x, L0_attn_W_in, L0_attn_b_in, L0_attn_A, L0_attn_Bm, L0_attn_Cm,
           L0_attn_W_gate, L0_attn_b_gate, L0_attn_W_out, L0_attn_b_out,
           L0_ffn_clip_x, L0_ffn_clip_down, L0_ffn_w_gate, L0_ffn_w_up, L0_ffn_w_down):
    x = np.asarray(L0_attn_x)
    B, S, Dm = x.shape
    xr = np.ascontiguousarray(x.reshape(B * S, Dm))
    f32 = np.float32

    w_in = np.asarray(L0_attn_W_in)
    shared = {
        "w2t": np.ascontiguousarray(w_in[Dm:2 * Dm, :].T.astype(f32)),
        "wgt": np.ascontiguousarray(np.asarray(L0_attn_W_gate).T.astype(f32)),
        "bmt": np.ascontiguousarray(np.asarray(L0_attn_Bm).T.astype(f32)),
        "cm": np.ascontiguousarray(np.asarray(L0_attn_Cm).astype(f32)),
        "wot": np.ascontiguousarray(np.asarray(L0_attn_W_out).T.astype(f32)),
        "anat": np.ascontiguousarray(np.asarray(L0_attn_A).astype(f32)),
        "at": np.ascontiguousarray(np.asarray(L0_attn_A).T.astype(f32)),
        "b2c": np.ascontiguousarray(np.asarray(L0_attn_b_in)[Dm:2 * Dm].astype(f32).reshape(ND, P).T),
        "bgc": np.ascontiguousarray(np.asarray(L0_attn_b_gate).astype(f32).reshape(ND, P).T),
        "bout": np.ascontiguousarray(np.asarray(L0_attn_b_out).astype(f32).reshape(1, Dm)),
        "clipx": np.asarray(L0_ffn_clip_x, f32).reshape(1, 1),
        "clipd": np.asarray(L0_ffn_clip_down, f32).reshape(1, 1),
        "fg_n": np.ascontiguousarray(np.asarray(L0_ffn_w_gate).astype(f32)),
        "fu_n": np.ascontiguousarray(np.asarray(L0_ffn_w_up).astype(f32)),
        "fd_n": np.ascontiguousarray(np.asarray(L0_ffn_w_down).astype(f32)),
        "fg_t": np.ascontiguousarray(np.asarray(L0_ffn_w_gate).astype(f32).T),
        "fu_t": np.ascontiguousarray(np.asarray(L0_ffn_w_up).astype(f32).T),
        "fd_t": np.ascontiguousarray(np.asarray(L0_ffn_w_down).astype(f32).T),
        "eye": np.eye(P, dtype=f32),
        "hones": np.ones((1, T), f32),
        "ones_r": np.ones((1, P), f32),
        "ones_c": np.ones((P, 1), f32),
    }

    if "nc" not in _NC_CACHE:
        _NC_CACHE["nc"] = build_nc()
    nc = _NC_CACHE["nc"]

    in_maps = []
    for c in range(8):
        sl = slice(c * T, (c + 1) * T)
        m = dict(shared)
        m["x"] = np.ascontiguousarray(xr[sl])
        m["xt"] = np.ascontiguousarray(xr[sl].T)
        in_maps.append(m)

    res = run_bass_kernel_spmd(nc, in_maps, core_ids=list(range(8)))
    out = np.concatenate([np.asarray(r["out"]) for r in res.results], axis=0)
    return out.reshape(B, S, Dm).astype(np.float32)


# version 19
# speedup vs baseline: 1.1775x; 1.0451x over previous
import numpy as np
import concourse.bass as bass
import concourse.mybir as mybir
from concourse.bass_utils import run_bass_kernel_spmd

F32 = mybir.dt.float32
BF16 = mybir.dt.bfloat16
AL = mybir.AluOpType
AF = mybir.ActivationFunctionType
AX = mybir.AxisListType

P = 128
D = 1024
ND = D // P
DI = 2816
NI = DI // P
T = 8192
GT = 256
NG = T // GT
NTT = GT // P
SC = 16
L = 64
NCHK = T // L
SEQ = 2048
CPS = SEQ // L
MAGIC = 12582912.0
QMAX = 127.0


class Sem:
    __slots__ = ("h", "n")

    def __init__(self, nc, name):
        self.h = nc.alloc_semaphore(name)
        self.n = 0


class Lanes:

    def __init__(self, nc, name, n=8):
        self.sems = [Sem(nc, f"{name}{i}") for i in range(n)]
        self.i = 0

    def next(self):
        s = self.sems[self.i % len(self.sems)]
        self.i += 1
        s.n += 16
        return (s, s.n)

    def all_tokens(self):
        return [(s, s.n) for s in self.sems]


class Prog:
    ENGS = ("sync", "vector", "scalar", "tensor", "gpsimd")

    def __init__(self, sems):
        self.ops = {e: [] for e in self.ENGS}
        self.sems = sems
        self.last_wait = {e: {} for e in self.ENGS}

    def raw(self, eng, fn):
        self.ops[eng].append(fn)

    def op(self, eng, fn, inc=1):
        s = self.sems[eng]
        s.n += inc
        def w(e, _fn=fn, _s=s, _k=inc):
            _fn(e).then_inc(_s.h, _k)
        self.ops[eng].append(w)
        return s.n

    def dma(self, eng, lanes, fn):
        s, v = lanes.next()
        def w(e, _fn=fn, _s=s):
            _fn(e).then_inc(_s.h, 16)
        self.ops[eng].append(w)
        return (s, v)

    def wait_tok(self, eng, tok):
        if tok == 0 or tok is None:
            return
        s, v = tok
        self.wait(eng, s, v)

    def wait_lanes(self, eng, lanes):
        for s, v in lanes.all_tokens():
            self.wait(eng, s, v)

    def wait(self, eng, sem, val):
        lw = self.last_wait[eng]
        if lw.get(id(sem), -1) >= val:
            return
        lw[id(sem)] = val
        self.ops[eng].append(lambda e, _s=sem, _v=val: e.wait_ge(_s.h, _v))

    def emit(self, nc):
        with nc.Block() as block:
            for name in self.ENGS:
                ops = self.ops[name]
                def body(eng, _ops=ops):
                    for f in _ops:
                        f(eng)
                getattr(block, name)(body)


def build_nc():
    nc = bass.Bass("TRN2", target_bir_lowering=False)

    d_xt = nc.dram_tensor("xt", [D, T], F32, kind="ExternalInput")
    d_x = nc.dram_tensor("x", [T, D], F32, kind="ExternalInput")
    d_w2t = nc.dram_tensor("w2t", [D, D], F32, kind="ExternalInput")
    d_wgt = nc.dram_tensor("wgt", [D, D], F32, kind="ExternalInput")
    d_bmt = nc.dram_tensor("bmt", [D, SC], F32, kind="ExternalInput")
    d_cm = nc.dram_tensor("cm", [D, SC], F32, kind="ExternalInput")
    d_wot = nc.dram_tensor("wot", [D, D], F32, kind="ExternalInput")
    d_anat = nc.dram_tensor("anat", [SC, SC], F32, kind="ExternalInput")
    d_at = nc.dram_tensor("at", [SC, SC], F32, kind="ExternalInput")
    d_b2c = nc.dram_tensor("b2c", [P, ND], F32, kind="ExternalInput")
    d_bgc = nc.dram_tensor("bgc", [P, ND], F32, kind="ExternalInput")
    d_bout = nc.dram_tensor("bout", [1, D], F32, kind="ExternalInput")
    d_clipx = nc.dram_tensor("clipx", [1, 1], F32, kind="ExternalInput")
    d_clipd = nc.dram_tensor("clipd", [1, 1], F32, kind="ExternalInput")
    d_fg_n = nc.dram_tensor("fg_n", [DI, D], F32, kind="ExternalInput")
    d_fu_n = nc.dram_tensor("fu_n", [DI, D], F32, kind="ExternalInput")
    d_fd_n = nc.dram_tensor("fd_n", [D, DI], F32, kind="ExternalInput")
    d_fg_t = nc.dram_tensor("fg_t", [D, DI], F32, kind="ExternalInput")
    d_fu_t = nc.dram_tensor("fu_t", [D, DI], F32, kind="ExternalInput")
    d_fd_t = nc.dram_tensor("fd_t", [DI, D], F32, kind="ExternalInput")
    d_eye = nc.dram_tensor("eye", [P, P], F32, kind="ExternalInput")
    d_ones_r = nc.dram_tensor("ones_r", [1, P], F32, kind="ExternalInput")
    d_ones_c = nc.dram_tensor("ones_c", [P, 1], F32, kind="ExternalInput")
    d_hones = nc.dram_tensor("hones", [1, T], F32, kind="ExternalInput")
    d_out = nc.dram_tensor("out", [T, D], F32, kind="ExternalOutput")
    d_hsp = nc.dram_tensor("hsp", [NI, P, T], F32)
    d_wdq = nc.dram_tensor("wdq", [NI, P, D], BF16)
    d_cc_in = nc.dram_tensor("cc_in", [1, 8], F32)
    d_cc_out = nc.dram_tensor("cc_out", [1, 8], F32)

    A = nc.alloc_sbuf_tensor
    eye = A("s_eye", [P, P], F32)
    ones_r = A("s_ones_r", [1, P], F32)
    ones_c = A("s_ones_c", [P, 1], F32)
    i16b = A("i16b", [SC, SC], BF16)
    anat_b = A("anat_b", [SC, SC], BF16)
    at_b = A("at_b", [SC, SC], BF16)
    bmt_b = A("bmt_b", [P, ND * SC], BF16)
    qstk = A("qstk", [SC, L * SC], BF16)
    wct = A("wct", [SC + 1, D], BF16)
    b2c = A("s_b2c", [P, ND], F32)
    bgc = A("s_bgc", [P, ND], F32)
    sc1 = A("sc1", [1, 16], F32)
    rowt = A("rowt", [1, P + 8], F32)
    uB = A("uB", [SC, T], BF16)
    hs = A("hs", [SC + 1, T], BF16)
    m_cols = A("m_cols", [P, 2 * NG], F32)
    rstd_cols = A("rstd_cols", [P, 2 * NG], F32)
    mrstd_cols = A("mrstd_cols", [P, 2 * NG], F32)
    amax_g = A("amax_g", [P, NI], F32)
    amax_u = A("amax_u", [P, NI], F32)
    amax_d = A("amax_d", [P, ND], F32)
    sxg = A("sxg", [P, NI], F32)
    sxu = A("sxu", [P, NI], F32)
    cS = A("cS", [P, 8], F32)
    mxc = A("mxc", [P, 2], F32)
    mhc = A("mhc", [P, 2], F32)
    habs_c = A("habs_c", [P, NI], F32)
    nwt = A("nwt", [P, 32], F32)

    CONSTS = [0.0, 0.5, 1.0, MAGIC]
    csb = A("csb", [P, len(CONSTS)], F32)
    for i, v in enumerate(CONSTS):
        nc.const_aps.register(F32, v, csb[:, i:i + 1])

    ps = [nc.alloc_psum_tensor(f"ps{i}", [P, 512], F32) for i in range(8)]

    sems = {e: Sem(nc, "s_" + e) for e in Prog.ENGS}
    dL = Lanes(nc, "dL", 8)
    dS = Lanes(nc, "dS", 4)
    dG = Lanes(nc, "dG", 4)
    scc = Sem(nc, "scc")

    wq_g = A("wq_g", [P, ND * DI], BF16)
    wq_u = A("wq_u", [P, ND * DI], BF16)
    wq_d = wq_u
    from contextlib import ExitStack
    _wstack = ExitStack()
    w2tb = _wstack.enter_context(nc.sbuf_tensor("w2tb", [P, ND * D], BF16))
    wgtb = _wstack.enter_context(nc.sbuf_tensor("wgtb", [P, ND * D], BF16))

    V, Sc, Tn, Gp, Sy = "vector", "scalar", "tensor", "gpsimd", "sync"

    def seq_chain(pg, eng, fns):
        s = pg.sems[eng]
        toks = []
        for fn in fns:
            pg.wait(eng, s, s.n)
            toks.append(pg.op(eng, fn))
        return toks[-1] if toks else s.n

    def col2row(pg, bank, col_ap_fn, dst_ap_fn, dep_v=None, dep_t=None):
        if dep_v is not None:
            pg.wait(Tn, sems[V], dep_v)
        if dep_t is not None:
            pg.wait(Tn, sems[Tn], dep_t)
        t1 = pg.op(Tn, lambda t, _c=col_ap_fn, _b=bank: t.transpose(ps[_b][0:1, 0:P], _c(), eye[:, :]))
        pg.wait(V, sems[Tn], t1)
        return pg.op(V, lambda v, _d=dst_ap_fn, _b=bank: v.tensor_copy(_d(), ps[_b][0:1, 0:P]))

    def bcast_row(pg, bank, row_ap_fn, dst_ap_fn, n, dep_v):
        pg.wait(Tn, sems[V], dep_v)
        t1 = pg.op(Tn, lambda t, _r=row_ap_fn, _b=bank, _n=n: t.matmul(
            ps[_b][:, 0:_n], ones_r[:, :], _r(), start=True, stop=True))
        pg.wait(V, sems[Tn], t1)
        return pg.op(V, lambda v, _d=dst_ap_fn, _b=bank, _n=n: v.tensor_copy(_d(), ps[_b][:, 0:_n]))

    pg = Prog(sems)
    with (
        nc.sbuf_tensor("ldb", [P, 2, D], F32) as ldb,
        nc.sbuf_tensor("ltb", [P, 2, DI // 2], F32) as ltb,
        nc.sbuf_tensor("cmb", [P, ND, SC], F32) as cmb,
        nc.sbuf_tensor("recb", [P, DI], F32) as recb,
        nc.sbuf_tensor("qtmp", [P, DI // 2], F32) as qtmp,
        nc.sbuf_tensor("rrow", [1, 512], F32) as rrow,
        nc.sbuf_tensor("dstage", [P, 2, D], BF16) as dstage,
        nc.sbuf_tensor("s_bout", [1, D], F32) as bout,
    ):
        ld0_toks = []
        for o, i_ in [
            (eye[:, :], d_eye[:, :]), (ones_r[:, :], d_ones_r[:, :]),
            (ones_c[:, :], d_ones_c[:, :]), (b2c[:, :], d_b2c[:, :]),
            (bgc[:, :], d_bgc[:, :]), (bout[:, :], d_bout[:, :]),
            (sc1[0:1, 0:1], d_clipx[:, :]), (sc1[0:1, 1:2], d_clipd[:, :]),
            (ldb[0:SC, 0, 0:SC], d_anat[:, :]),
            (ldb[0:SC, 0, 64:64 + SC], d_at[:, :]),
            (cmb[:, :, :], d_cm.rearrange("(c p) n -> p c n", p=P)[:, :, :]),
            (ldb[:, 1, 0:ND * SC].rearrange("p (c n) -> p c n", n=SC), d_bmt.rearrange("(c p) n -> p c n", p=P)[:, :, :]),
        ]:
            ld0_toks.append(pg.dma(Sy, dL, lambda e, _o=o, _i=i_: e.dma_start(out=_o, in_=_i)))
        for t_ in ld0_toks:
            pg.wait_tok(V, t_)
        for i, v in enumerate(CONSTS):
            pg.op(V, lambda vv, _i=i, _v=v: vv.memset(csb[:, _i:_i + 1], _v))
        pg.op(V, lambda v: v.memset(mxc[:, :], 0.0))
        pg.op(V, lambda v: v.memset(mhc[:, :], 0.0))
        pg.op(V, lambda v: v.tensor_copy(anat_b[:, :], ldb[0:SC, 0, 0:SC]))
        pg.op(V, lambda v: v.tensor_copy(at_b[:, :], ldb[0:SC, 0, 64:64 + SC]))
        pg.op(V, lambda v: v.tensor_copy(bmt_b[:, :], ldb[:, 1, 0:ND * SC]))
        v_eye = pg.op(V, lambda v: v.tensor_copy(i16b[:, :], eye[0:SC, 0:SC]))
        pg.wait(V, sems[V], v_eye)
        v_q1 = pg.op(V, lambda v: v.tensor_copy(qstk[:, 0:SC], at_b[:, :]))

        cast_done = {}
        for wi, (wsrc, wdst) in enumerate(((d_w2t, w2tb), (d_wgt, wgtb))):
            for k in range(ND):
                idx = wi * ND + k
                if idx < 2:
                    pg.wait(Sy, sems[V], v_q1)
                else:
                    pg.wait(Sy, sems[V], cast_done[idx - 2])
                lw = pg.dma(Sy, dL, lambda e, _w=wsrc, _k=k, _s=idx % 2: e.dma_start(
                    out=ldb[:, _s, :], in_=_w[_k * P:(_k + 1) * P, :]))
                pg.wait_tok(V, lw)
                cast_done[idx] = pg.op(V, lambda v, _d=wdst, _k=k, _s=idx % 2: v.tensor_copy(
                    _d[:, _k * D:(_k + 1) * D], ldb[:, _s, :]))

        mm_done = {}
        for half in range(2):
            for k in range(ND):
                idx = half * ND + k
                if idx >= 2:
                    pg.wait(Sy, sems[Tn], mm_done[idx - 2])
                lw = pg.dma(Sy, dL, lambda e, _k=k, _h=half, _s=idx % 2: e.dma_start(
                    out=ltb[:, _s, 0:512], in_=d_wot[_k * P:(_k + 1) * P, _h * 512:(_h + 1) * 512]))
                pg.wait_tok(Tn, lw)
                if half == 0 and k == 0:
                    for t_ in ld0_toks:
                        pg.wait_tok(Tn, t_)
                mm_done[idx] = pg.op(Tn, lambda t, _k=k, _h=half, _s=idx % 2: t.matmul(
                    ps[0][0:SC, 0:512], cmb[:, _k, :], ltb[:, _s, 0:512],
                    start=(_k == 0), stop=(_k == ND - 1)))
            pg.wait(V, sems[Tn], mm_done[half * ND + ND - 1])
            ev = pg.op(V, lambda v, _h=half: v.tensor_copy(wct[0:SC, _h * 512:(_h + 1) * 512], ps[0][0:SC, 0:512]))
            pg.wait(Tn, sems[V], ev)

        pg.dma(Gp, dG, lambda e: e.dma_start(out=wct[SC:SC + 1, :], in_=d_bout[:, :]))
        pg.dma(Gp, dG, lambda e: e.dma_start(out=hs[SC:SC + 1, :], in_=d_hones[:, :]))
        pg.wait(Tn, sems[V], v_q1)
        for k in range(1, L):
            tq = pg.op(Tn, lambda t, _k=k: t.matmul(
                ps[1][0:SC, 0:SC], anat_b[:, :], qstk[:, (_k - 1) * SC:_k * SC], start=True, stop=True))
            pg.wait(V, sems[Tn], tq)
            vq = pg.op(V, lambda v, _k=k: v.tensor_copy(qstk[:, _k * SC:(_k + 1) * SC], ps[1][0:SC, 0:SC]))
            pg.wait(Tn, sems[V], vq)

        ldb_free = cast_done[2 * ND - 1]
        ltb_free_t = mm_done[2 * ND - 1]
        ltb_free_v = 0
        for acol, wt_src, wq_dst, wn_src in (
            (amax_g, d_fg_t, wq_g, d_fg_n),
            (amax_u, d_fu_t, wq_u, d_fu_n),
        ):
            red_done = {}
            for oc in range(NI):
                if oc >= 2:
                    pg.wait(Sy, sems[V], red_done[oc - 2])
                else:
                    pg.wait(Sy, sems[V], ldb_free)
                lw = pg.dma(Sy, dL, lambda e, _w=wn_src, _o=oc: e.dma_start(
                    out=ldb[:, _o % 2, :], in_=_w[_o * P:(_o + 1) * P, :]))
                pg.wait_tok(V, lw)
                red_done[oc] = pg.op(V, lambda v, _a=acol, _o=oc: v.tensor_reduce(
                    _a[:, _o:_o + 1], ldb[:, _o % 2, :], AX.X, AL.max, apply_absolute_value=True))
            ldb_free = red_done[NI - 1]
            v_rc = seq_chain(pg, V, [
                lambda v, _a=acol: v.reciprocal(nwt[:, 0:NI], _a[:, :]),
                lambda v: v.tensor_scalar(nwt[:, 0:NI], nwt[:, 0:NI], QMAX, None, AL.mult),
            ])
            e3 = v_rc
            for j in range(6):
                n = min(512, DI - j * 512)
                for jj in range(n // P):
                    oc = j * 4 + jj
                    e3 = col2row(pg, 2, (lambda _o=oc: lambda: nwt[:, _o:_o + 1])(),
                                 (lambda _jj=jj: lambda: rrow[0:1, _jj * P:(_jj + 1) * P])(), dep_v=e3)
                e3 = bcast_row(pg, 2, (lambda _n=n: lambda: rrow[0:1, 0:_n])(),
                               (lambda _j=j, _n=n: lambda: recb[:, _j * 512:_j * 512 + _n])(), n, e3)
            v_recb = e3
            q_done = {}
            HW = DI // 2
            for k2 in range(2 * ND):
                k, hf = divmod(k2, 2)
                if k2 >= 2:
                    pg.wait(Sy, sems[V], q_done[k2 - 2])
                else:
                    pg.wait(Sy, sems[Tn], ltb_free_t)
                    pg.wait(Sy, sems[V], ltb_free_v)
                lw = pg.dma(Sy, dL, lambda e, _w=wt_src, _k=k, _h=hf, _s=k2 % 2: e.dma_start(
                    out=ltb[:, _s, :], in_=_w[_k * P:(_k + 1) * P, _h * HW:(_h + 1) * HW]))
                pg.wait_tok(V, lw)
                pg.wait(V, sems[V], v_recb)
                t1 = pg.op(V, lambda v, _s=k2 % 2, _h=hf: v.tensor_tensor(
                    qtmp[:, :], ltb[:, _s, :], recb[:, _h * HW:(_h + 1) * HW], AL.mult))
                pg.wait(V, sems[V], t1)
                q_done[k2] = pg.op(V, lambda v, _d=wq_dst, _k=k, _h=hf: v.tensor_scalar(
                    _d[:, _k * DI + _h * HW:_k * DI + (_h + 1) * HW], qtmp[:, :], MAGIC, MAGIC, AL.add, AL.subtract))
            ltb_free_v = q_done[2 * ND - 1]
        red_done = {}
        HW = DI // 2
        for oc2 in range(2 * ND):
            oc, hf = divmod(oc2, 2)
            if oc2 >= 2:
                pg.wait(Sy, sems[V], red_done[oc2 - 2])
            else:
                pg.wait(Sy, sems[V], ltb_free_v)
            lw = pg.dma(Sy, dL, lambda e, _o=oc, _h=hf, _s=oc2 % 2: e.dma_start(
                out=ltb[:, _s, :], in_=d_fd_n[_o * P:(_o + 1) * P, _h * HW:(_h + 1) * HW]))
            pg.wait_tok(V, lw)
            if hf == 0:
                red_done[oc2] = pg.op(V, lambda v, _o=oc, _s=oc2 % 2: v.tensor_reduce(
                    nwt[:, 8:9], ltb[:, _s, :], AX.X, AL.max, apply_absolute_value=True))
            else:
                r1 = pg.op(V, lambda v, _s=oc2 % 2: v.tensor_reduce(nwt[:, 9:10], ltb[:, _s, :], AX.X, AL.max, apply_absolute_value=True))
                pg.wait(V, sems[V], r1)
                red_done[oc2] = pg.op(V, lambda v, _o=oc: v.tensor_tensor(amax_d[:, _o:_o + 1], nwt[:, 8:9], nwt[:, 9:10], AL.max))
        e0 = seq_chain(pg, V, [
            lambda v: v.reciprocal(nwt[:, 0:ND], amax_d[:, :]),
            lambda v: v.tensor_scalar(nwt[:, 0:ND], nwt[:, 0:ND], QMAX, None, AL.mult),
            lambda v: v.tensor_scalar(nwt[:, 8:8 + ND], amax_d[:, :], 1.0 / QMAX, None, AL.mult),
        ])
        for oc in range(ND):
            e0 = col2row(pg, 2, (lambda _o=oc: lambda: nwt[:, _o:_o + 1])(),
                         (lambda _o=oc: lambda: rrow[0:1, _o % 4 * P:(_o % 4 + 1) * P])(), dep_v=e0)
            if oc % 4 == 3:
                e0 = bcast_row(pg, 2, lambda: rrow[0:1, 0:512],
                               (lambda _j=oc // 4: lambda: recb[:, _j * 512:(_j + 1) * 512])(), 512, e0)
        for oc in range(ND):
            e0 = col2row(pg, 2, (lambda _o=oc: lambda: nwt[:, 8 + _o:9 + _o])(),
                         (lambda _o=oc: lambda: rrow[0:1, _o % 4 * P:(_o % 4 + 1) * P])(), dep_v=e0)
            if oc % 4 == 3:
                e0 = bcast_row(pg, 2, lambda: rrow[0:1, 0:512],
                               (lambda _j=oc // 4: lambda: recb[:, D + _j * 512:D + (_j + 1) * 512])(), 512, e0)
        wq_stage = {}
        st_done = {}
        for oc in range(NI):
            if oc >= 2:
                pg.wait(Sy, sems[V], wq_stage[oc - 2])
            lw = pg.dma(Sy, dL, lambda e, _o=oc: e.dma_start(out=ltb[:, _o % 2, 0:D], in_=d_fd_t[_o * P:(_o + 1) * P, :]))
            pg.wait_tok(V, lw)
            pg.wait(V, sems[V], e0)
            if oc >= 2:
                pg.wait_tok(V, st_done.get(oc - 2, 0))
            t1 = pg.op(V, lambda v, _o=oc: v.tensor_tensor(qtmp[:, 0:D], ltb[:, _o % 2, 0:D], recb[:, 0:D], AL.mult))
            pg.wait(V, sems[V], t1)
            t2 = pg.op(V, lambda v: v.tensor_scalar(qtmp[:, 0:D], qtmp[:, 0:D], MAGIC, MAGIC, AL.add, AL.subtract))
            pg.wait(V, sems[V], t2)
            wq_stage[oc] = pg.op(V, lambda v, _o=oc: v.tensor_tensor(
                dstage[:, _o % 2, :], qtmp[:, 0:D], recb[:, D:2 * D], AL.mult))
            pg.wait(Sc, sems[V], wq_stage[oc])
            st_done[oc] = pg.dma(Sc, dS, lambda s, _o=oc: s.dma_start(out=d_wdq[_o, :, :], in_=dstage[:, _o % 2, :]))
        pg.emit(nc)

    pg = Prog(sems)
    with (
        nc.sbuf_tensor("xtg", [P, 3, ND, GT], F32) as xtg,
        nc.sbuf_tensor("xnb", [P, 2, ND, GT], BF16) as xnb,
        nc.sbuf_tensor("ubf", [P, ND, GT], BF16) as ubf,
        nc.sbuf_tensor("xnf", [P, 2, GT], F32) as xnf,
        nc.sbuf_tensor("xsq", [P, 2, GT], F32) as xsq,
        nc.sbuf_tensor("sgb", [P, 1, GT], F32) as sgb,
        nc.sbuf_tensor("mbA", [P, 2, GT], F32) as mbA,
        nc.sbuf_tensor("rstdbA", [P, 2, GT], F32) as rstdbA,
        nc.sbuf_tensor("rowS", [1, 2, GT], F32) as rowS,
    ):
        ld_done = {}
        sq_done = {}
        stat_mm = {}
        bcast_done = {}
        cast_done = {}
        mx_done = {}
        xtg_rd = {}
        pool_done = {}
        u_done = {}
        sg_done = {}
        attn_done = {}
        ub_evac = {}

        def emit_loads(g):
            if g >= 3:
                pg.wait(Sy, sems[Gp], xtg_rd[g - 3])
            ld_done[g] = [pg.dma(Sy, dL, lambda e, _g=g, _dc=dc: e.dma_start(
                out=xtg[:, _g % 3, _dc, :], in_=d_xt[_dc * P:(_dc + 1) * P, _g * GT:(_g + 1) * GT]))
                for dc in range(ND)]

        stat_evac = {}

        def emit_stat_mms(g):
            b = g % 3
            for t_ in ld_done[g]:
                pg.wait_tok(V, t_)
                pg.wait_tok(Tn, t_)
            if g >= 1:
                pg.wait(Tn, sems[V], stat_evac.get(g - 1, 0))
            for dc in range(ND):
                if (g, dc - 2) in stat_mm or (g - 1, ND + dc - 2) in stat_mm:
                    key = (g, dc - 2) if dc >= 2 else (g - 1, ND + dc - 2)
                    pg.wait(V, sems[Tn], stat_mm[key])
                sq_done[(g, dc)] = pg.op(V, lambda v, _b=b, _dc=dc: v.tensor_tensor(
                    xsq[:, _dc % 2, :], xtg[:, _b, _dc, :], xtg[:, _b, _dc, :], AL.mult))
                pg.op(Tn, lambda t, _b=b, _dc=dc: t.matmul(
                    ps[5][0:1, 0:GT], ones_c[:, :], xtg[:, _b, _dc, :],
                    start=(_dc == 0), stop=(_dc == ND - 1)))
                pg.wait(Tn, sems[V], sq_done[(g, dc)])
                stat_mm[(g, dc)] = pg.op(Tn, lambda t, _dc=dc: t.matmul(
                    ps[6][0:1, 0:GT], ones_c[:, :], xsq[:, _dc % 2, :],
                    start=(_dc == 0), stop=(_dc == ND - 1)))
            stat_mm[g] = stat_mm[(g, ND - 1)]

        def emit_stat_vec(g):
            b = g % 3
            br = g % 2
            g2 = 2 * g
            pg.wait(V, sems[Tn], stat_mm[g])
            e1 = pg.op(V, lambda v, _br=br: v.tensor_copy(rowS[0:1, _br, :], ps[5][0:1, 0:GT]))
            for j in range(2):
                col2row_inv = pg
                pg.wait(Tn, sems[V], e1)
                tt = pg.op(Tn, lambda t, _br=br, _j=j: t.transpose(
                    ps[7][0:P, 0:1], rowS[0:1, _br, _j * P:(_j + 1) * P], eye[0:1, 0:1]))
                pg.wait(V, sems[Tn], tt)
                e1 = pg.op(V, lambda v, _j=j: v.tensor_copy(nwt[:, _j:_j + 1], ps[7][0:P, 0:1]))
            pg.wait(V, sems[Tn], sems[Tn].n)
            e2 = pg.op(V, lambda v, _br=br: v.tensor_copy(rowS[0:1, _br, :], ps[6][0:1, 0:GT]))
            stat_evac[g] = e2
            for j in range(2):
                pg.wait(Tn, sems[V], e2)
                tt = pg.op(Tn, lambda t, _br=br, _j=j: t.transpose(
                    ps[7][0:P, 0:1], rowS[0:1, _br, _j * P:(_j + 1) * P], eye[0:1, 0:1]))
                pg.wait(V, sems[Tn], tt)
                e2 = pg.op(V, lambda v, _j=j: v.tensor_copy(nwt[:, 2 + _j:3 + _j], ps[7][0:P, 0:1]))
            seq_chain(pg, V, [
                lambda v, _g2=g2: v.tensor_scalar(m_cols[:, _g2:_g2 + 2], nwt[:, 0:2], 1.0 / D, None, AL.mult),
                lambda v, _g2=g2: v.tensor_tensor(nwt[:, 4:6], m_cols[:, _g2:_g2 + 2], m_cols[:, _g2:_g2 + 2], AL.mult),
                lambda v: v.scalar_tensor_tensor(nwt[:, 6:8], nwt[:, 2:4], 1.0 / D, nwt[:, 4:6], AL.mult, AL.subtract),
                lambda v: v.tensor_scalar(nwt[:, 6:8], nwt[:, 6:8], 1e-5, None, AL.add),
                lambda v: v.tensor_scalar(nwt[:, 8:10], nwt[:, 6:8], 0.5, 0.5, AL.mult, AL.add),
                lambda v: v.reciprocal(nwt[:, 10:12], nwt[:, 8:10]),
                lambda v: v.tensor_tensor(nwt[:, 12:14], nwt[:, 10:12], nwt[:, 10:12], AL.mult),
                lambda v: v.tensor_tensor(nwt[:, 12:14], nwt[:, 12:14], nwt[:, 6:8], AL.mult),
                lambda v: v.tensor_scalar(nwt[:, 12:14], nwt[:, 12:14], -0.5, 1.5, AL.mult, AL.add),
                lambda v: v.tensor_tensor(nwt[:, 10:12], nwt[:, 10:12], nwt[:, 12:14], AL.mult),
                lambda v: v.tensor_tensor(nwt[:, 12:14], nwt[:, 10:12], nwt[:, 10:12], AL.mult),
                lambda v: v.tensor_tensor(nwt[:, 12:14], nwt[:, 12:14], nwt[:, 6:8], AL.mult),
                lambda v: v.tensor_scalar(nwt[:, 12:14], nwt[:, 12:14], -0.5, 1.5, AL.mult, AL.add),
                lambda v: v.tensor_tensor(nwt[:, 10:12], nwt[:, 10:12], nwt[:, 12:14], AL.mult),
                lambda v: v.tensor_tensor(nwt[:, 12:14], nwt[:, 10:12], nwt[:, 10:12], AL.mult),
                lambda v: v.tensor_tensor(nwt[:, 12:14], nwt[:, 12:14], nwt[:, 6:8], AL.mult),
                lambda v: v.tensor_scalar(nwt[:, 12:14], nwt[:, 12:14], -0.5, 1.5, AL.mult, AL.add),
                lambda v: v.tensor_tensor(nwt[:, 10:12], nwt[:, 10:12], nwt[:, 12:14], AL.mult),
                lambda v: v.tensor_tensor(nwt[:, 12:14], nwt[:, 10:12], nwt[:, 10:12], AL.mult),
                lambda v: v.tensor_tensor(nwt[:, 12:14], nwt[:, 12:14], nwt[:, 6:8], AL.mult),
                lambda v: v.tensor_scalar(nwt[:, 12:14], nwt[:, 12:14], -0.5, 1.5, AL.mult, AL.add),
                lambda v, _g2=g2: v.tensor_tensor(rstd_cols[:, _g2:_g2 + 2], nwt[:, 10:12], nwt[:, 12:14], AL.mult),
                lambda v, _g2=g2: v.tensor_tensor(mrstd_cols[:, _g2:_g2 + 2], m_cols[:, _g2:_g2 + 2], rstd_cols[:, _g2:_g2 + 2], AL.mult),
            ])
        def emit_stat_bcast(g):
            br = g % 2
            bm = g % 2
            g2 = 2 * g
            if g >= 2:
                pg.wait(V, sems[Gp], pool_done.get((g - 2, ND - 1), 0))
            e3 = sems[V].n
            for j in range(2):
                e3 = col2row(pg, 7, (lambda _g2=g2, _j=j: lambda: m_cols[:, _g2 + _j:_g2 + _j + 1])(),
                             (lambda _br=br, _j=j: lambda: rowS[0:1, _br, _j * P:(_j + 1) * P])(), dep_v=e3)
            e3 = bcast_row(pg, 7, (lambda _br=br: lambda: rowS[0:1, _br, :])(),
                           (lambda _bm=bm: lambda: mbA[:, _bm, :])(), GT, e3)
            for j in range(2):
                e3 = col2row(pg, 7, (lambda _g2=g2, _j=j: lambda: rstd_cols[:, _g2 + _j:_g2 + _j + 1])(),
                             (lambda _br=br, _j=j: lambda: rowS[0:1, _br, _j * P:(_j + 1) * P])(), dep_v=e3)
            e3 = bcast_row(pg, 7, (lambda _br=br: lambda: rowS[0:1, _br, :])(),
                           (lambda _bm=bm: lambda: rstdbA[:, _bm, :])(), GT, e3)
            bcast_done[g] = e3

        def emit_xn(g):
            b = g % 3
            bx = g % 2
            bm = g % 2
            pg.wait(Gp, sems[V], bcast_done[g])
            for t_ in ld_done[g]:
                pg.wait_tok(Gp, t_)
            pg.wait(Gp, sems[Tn], attn_done.get(g - 2, 0))
            for dc in range(ND):
                e_ = dc % 2
                idx = g * ND + dc
                if dc >= 2:
                    pg.wait(Gp, sems[V], mx_done[(g, dc - 2)])
                p1 = pg.op(Gp, lambda p_, _b=b, _bm=bm, _dc=dc, _e=e_: p_.tensor_tensor(
                    xnf[:, _e, :], xtg[:, _b, _dc, :], mbA[:, _bm, :], AL.subtract))
                pg.wait(Gp, sems[Gp], p1)
                pool_done[(g, dc)] = pg.op(Gp, lambda p_, _bm=bm, _e=e_: p_.tensor_tensor(
                    xnf[:, _e, :], xnf[:, _e, :], rstdbA[:, _bm, :], AL.mult))
                pg.wait(Gp, sems[Gp], pool_done[(g, dc)])
                cast_done[(g, dc)] = pg.op(Gp, lambda p_, _bx=bx, _dc=dc, _e=e_: p_.tensor_copy(
                    xnb[:, _bx, _dc, :], xnf[:, _e, :]))
                pg.wait(V, sems[Gp], pool_done[(g, dc)])
                r1 = pg.op(V, lambda v, _e=e_: v.tensor_reduce(nwt[:, 14:15], xnf[:, _e, :], AX.X, AL.max, apply_absolute_value=True))
                pg.wait(V, sems[V], r1)
                pp = idx % 2
                mx_done[(g, dc)] = pg.op(V, lambda v, _pp=pp: v.tensor_tensor(
                    mxc[:, 1 - _pp:2 - _pp], mxc[:, _pp:_pp + 1], nwt[:, 14:15], AL.max))
            xtg_rd[g] = cast_done[(g, ND - 1)]

        def emit_attn(g):
            b = g % 3
            bx = g % 2
            for dc in range(ND):
                e_ = dc % 2
                idx = g * ND + dc
                psA, psB = (0, 1) if e_ == 0 else (2, 3)
                pg.wait(Tn, sems[Gp], cast_done[(g, ND - 1)])
                if idx >= 2:
                    prev = divmod(idx - 2, ND)
                    pg.wait(Tn, sems[V], u_done.get(prev, 0))
                    pg.wait(Tn, sems[Sc], sg_done.get(prev, 0))
                for k in range(ND):
                    pg.op(Tn, lambda t, _pA=psA, _bx=bx, _dc=dc, _k=k: t.matmul(
                        ps[_pA][:, 0:GT], w2tb[:, _k * D + _dc * P:_k * D + (_dc + 1) * P], xnb[:, _bx, _k, :],
                        start=(_k == 0), stop=(_k == ND - 1)))
                for k in range(ND):
                    pg.op(Tn, lambda t, _pB=psB, _bx=bx, _dc=dc, _k=k: t.matmul(
                        ps[_pB][:, 0:GT], wgtb[:, _k * D + _dc * P:_k * D + (_dc + 1) * P], xnb[:, _bx, _k, :],
                        start=(_k == 0), stop=(_k == ND - 1)))
                t_mm = sems[Tn].n
                pg.wait(Sc, sems[Tn], t_mm)
                if idx >= 1:
                    prev = divmod(idx - 1, ND)
                    pg.wait(Sc, sems[V], u_done.get(prev, 0))
                sg_done[(g, dc)] = pg.op(Sc, lambda s, _pB=psB, _dc=dc: s.activation(
                    sgb[:, 0, :], ps[_pB][:, 0:GT], AF.Sigmoid, bias=bgc[:, _dc:_dc + 1], scale=1.0))
                pg.wait(V, sems[Sc], sg_done[(g, dc)])
                if dc == 0:
                    pg.wait(V, sems[Tn], attn_done.get(g - 1, 0))
                u_done[(g, dc)] = pg.op(V, lambda v, _pA=psA, _dc=dc: v.scalar_tensor_tensor(
                    ubf[:, _dc, :], ps[_pA][:, 0:GT], b2c[:, _dc:_dc + 1], sgb[:, 0, :], AL.add, AL.mult))
                if g + 1 < NG:
                    gn = g + 1
                    bn, bxn, bmn, en = gn % 3, gn % 2, gn % 2, dc % 2
                    if dc == 0:
                        pg.wait(Gp, sems[V], bcast_done[gn])
                        for t_ in ld_done[gn]:
                            pg.wait_tok(Gp, t_)
                        pg.wait(Gp, sems[Tn], attn_done.get(gn - 2, 0))
                    if dc >= 2:
                        pg.wait(Gp, sems[V], mx_done[(gn, dc - 2)])
                    p1 = pg.op(Gp, lambda p_, _b=bn, _bm=bmn, _dc=dc, _e=en: p_.tensor_tensor(
                        xnf[:, _e, :], xtg[:, _b, _dc, :], mbA[:, _bm, :], AL.subtract))
                    pg.wait(Gp, sems[Gp], p1)
                    pool_done[(gn, dc)] = pg.op(Gp, lambda p_, _bm=bmn, _e=en: p_.tensor_tensor(
                        xnf[:, _e, :], xnf[:, _e, :], rstdbA[:, _bm, :], AL.mult))
                    pg.wait(Gp, sems[Gp], pool_done[(gn, dc)])
                    cast_done[(gn, dc)] = pg.op(Gp, lambda p_, _bx=bxn, _dc=dc, _e=en: p_.tensor_copy(
                        xnb[:, _bx, _dc, :], xnf[:, _e, :]))
                    pg.wait(V, sems[Gp], pool_done[(gn, dc)])
                    r1 = pg.op(V, lambda v, _e=en: v.tensor_reduce(
                        nwt[:, 14:15], xnf[:, _e, :], AX.X, AL.max, apply_absolute_value=True))
                    pg.wait(V, sems[V], r1)
                    ppn = (gn * ND + dc) % 2
                    mx_done[(gn, dc)] = pg.op(V, lambda v, _pp=ppn: v.tensor_tensor(
                        mxc[:, 1 - _pp:2 - _pp], mxc[:, _pp:_pp + 1], nwt[:, 14:15], AL.max))
                    if dc == ND - 1:
                        xtg_rd[gn] = cast_done[(gn, dc)]
            pg.wait(Tn, sems[V], u_done[(g, ND - 1)])
            if g >= 1:
                pg.wait(Tn, sems[V], ub_evac.get(g - 1, 0))
            for k in range(ND):
                pg.op(Tn, lambda t, _k=k: t.matmul(
                    ps[4][0:SC, 0:GT], bmt_b[:, _k * SC:(_k + 1) * SC], ubf[:, _k, :],
                    start=(_k == 0), stop=(_k == ND - 1)))
            attn_done[g] = sems[Tn].n
            pg.wait(V, sems[Tn], attn_done[g])
            ub_evac[g] = pg.op(V, lambda v, _g=g: v.tensor_copy(uB[:, _g * GT:(_g + 1) * GT], ps[4][0:SC, 0:GT]))

        for gg in (0, 1, 2):
            emit_loads(gg)
            emit_stat_mms(gg)
            emit_stat_vec(gg)
        emit_stat_bcast(0)
        emit_stat_bcast(1)
        emit_xn(0)
        emit_loads(3)
        emit_stat_mms(3)
        emit_stat_vec(3)
        for g in range(NG):
            if g + 2 < NG:
                emit_stat_bcast(g + 2)
            emit_attn(g)
            if g + 4 < NG:
                emit_loads(g + 4)
                emit_stat_mms(g + 4)
                emit_stat_vec(g + 4)

        fin = (NG * ND) % 2
        fx = col2row(pg, 7, lambda: mxc[:, fin:fin + 1], lambda: rowt[0:1, 0:P], dep_v=mx_done[(NG - 1, ND - 1)])
        pg.wait(V, sems[V], fx)
        vmx = pg.op(V, lambda v: v.tensor_reduce(sc1[0:1, 2:3], rowt[0:1, 0:P], AX.X, AL.max))
        pg.wait(Sc, sems[V], vmx)
        stg = pg.op(Sc, lambda s: s.activation(rowt[0:1, P:P + 8], ones_r[0:1, 0:8], AF.Identity, bias=0.0, scale=sc1[0:1, 2:3]))
        pg.wait(Gp, sems[Sc], stg)
        tk1 = pg.dma(Gp, dG, lambda e: e.dma_start(out=d_cc_in[:, :], in_=rowt[0:1, P:P + 8]))
        pg.wait_tok(Gp, tk1)
        cc_tok = pg.op(Gp, lambda e: e.collective_compute(
            "AllReduce", AL.max, replica_groups=[list(range(8))],
            ins=[d_cc_in[:, :]], outs=[d_cc_out[:, :]]))
        pg.wait(Gp, sems[Gp], cc_tok)
        tk2 = pg.dma(Gp, dG, lambda e: e.dma_start(out=sc1[0:1, 2:3], in_=d_cc_out[0:1, 0:1]))
        pg.wait_tok(Sc, tk2)
        sgm = pg.op(Sc, lambda s: s.activation(sc1[0:1, 3:4], sc1[0:1, 0:1], AF.Sigmoid, bias=0.0, scale=1.0))
        pg.wait(V, sems[Sc], sgm)
        seq_chain(pg, V, [
            lambda v: v.tensor_tensor(sc1[0:1, 4:5], sc1[0:1, 3:4], sc1[0:1, 2:3], AL.mult),
            lambda v: v.tensor_scalar(sc1[0:1, 4:5], sc1[0:1, 4:5], 1.0 / QMAX, None, AL.mult),
            lambda v: v.reciprocal(sc1[0:1, 5:6], sc1[0:1, 4:5]),
        ])
        vsx = sems[V].n
        pg.wait(Tn, sems[V], vsx)
        tb = pg.op(Tn, lambda t: t.matmul(ps[7][:, 0:2], ones_r[:, :], sc1[0:1, 4:6], start=True, stop=True))
        pg.wait(V, sems[Tn], tb)
        seq_chain(pg, V, [
            lambda v: v.tensor_copy(cS[:, 2:3], ps[7][:, 0:1]),
            lambda v: v.tensor_copy(cS[:, 0:1], ps[7][:, 1:2]),
            lambda v: v.tensor_scalar(cS[:, 1:2], ps[7][:, 1:2], -1.0, None, AL.mult),
            lambda v: v.tensor_scalar(sxg[:, :], amax_g[:, :], 1.0 / QMAX, None, AL.mult),
            lambda v: v.tensor_scalar_mul(sxg[:, :], sxg[:, :], cS[:, 2:3]),
            lambda v: v.tensor_scalar(sxu[:, :], amax_u[:, :], 1.0 / QMAX, None, AL.mult),
            lambda v: v.tensor_scalar_mul(sxu[:, :], sxu[:, :], cS[:, 2:3]),
        ])
        pg.emit(nc)
    _wstack.close()

    pg = Prog(sems)
    with (
        nc.sbuf_tensor("xtg2", [P, 3, ND, GT], F32) as xtg2,
        nc.sbuf_tensor("zq", [P, 2, ND, GT], BF16) as zq,
        nc.sbuf_tensor("yt", [P, 2, GT], F32) as yt,
        nc.sbuf_tensor("sg2", [P, 2, GT], F32) as sg2,
        nc.sbuf_tensor("hb", [P, 4, GT], F32) as hb,
        nc.sbuf_tensor("c1b", [P, 3, GT], F32) as c1b,
        nc.sbuf_tensor("c0b", [P, 3, GT], F32) as c0b,
        nc.sbuf_tensor("rowB", [1, 3, GT], F32) as rowB,
        nc.sbuf_tensor("fstk", [SC, NCHK], BF16) as fstk,
        nc.sbuf_tensor("scr16", [SC, NCHK], F32) as scr16,
    ):
        ldB = {}
        c_done = {}
        z_done = {}
        gu_done = {}
        h_done = {}
        sil_done = {}
        spill = {}
        habs_done = {}
        scan_v = {}
        scan_emitted = [0]

        def emit_loadsB(g):
            if g >= 3:
                pg.wait(Sy, sems[V], z_done[g - 3])
            ldB[g] = [pg.dma(Sy, dL, lambda e, _g=g, _dc=dc: e.dma_start(
                out=xtg2[:, _g % 3, _dc, :], in_=d_xt[_dc * P:(_dc + 1) * P, _g * GT:(_g + 1) * GT]))
                for dc in range(ND)]

        c_tn = {}

        def emit_cchain(g):
            b = g % 3
            g2 = 2 * g
            o = 16 + 4 * (g % 2)
            pg.wait(V, sems[Tn], c_tn.get(g - 2, 0))
            e0 = seq_chain(pg, V, [
                lambda v, _g2=g2, _o=o: v.tensor_scalar_mul(nwt[:, _o:_o + 2], rstd_cols[:, _g2:_g2 + 2], cS[:, 0:1]),
                lambda v, _g2=g2, _o=o: v.tensor_scalar_mul(nwt[:, _o + 2:_o + 4], mrstd_cols[:, _g2:_g2 + 2], cS[:, 1:2]),
            ])
            if g >= 3:
                pg.wait(V, sems[Gp], c_done.get(("pool", g - 3), 0))
            e3 = e0
            for j in range(2):
                e3 = col2row(pg, 5, (lambda _j=j, _o=o: lambda: nwt[:, _o + _j:_o + _j + 1])(),
                             (lambda _b=b, _j=j: lambda: rowB[0:1, _b, _j * P:(_j + 1) * P])(), dep_v=e3)
            e3 = bcast_row(pg, 5, (lambda _b=b: lambda: rowB[0:1, _b, :])(),
                           (lambda _b=b: lambda: c1b[:, _b, :])(), GT, e3)
            for j in range(2):
                e3 = col2row(pg, 5, (lambda _j=j, _o=o: lambda: nwt[:, _o + 2 + _j:_o + 3 + _j])(),
                             (lambda _b=b, _j=j: lambda: rowB[0:1, _b, _j * P:(_j + 1) * P])(), dep_v=e3)
            e3 = bcast_row(pg, 5, (lambda _b=b: lambda: rowB[0:1, _b, :])(),
                           (lambda _b=b: lambda: c0b[:, _b, :])(), GT, e3)
            c_done[g] = e3
            c_tn[g] = sems[Tn].n

        def emit_xq(g):
            b = g % 3
            bz = g % 2
            for t_ in ldB[g]:
                pg.wait_tok(Gp, t_)
            pg.wait(Gp, sems[V], c_done[g])
            for dc in range(ND):
                e_ = dc % 2
                idx = g * ND + dc
                if idx >= 2:
                    pg.wait(Gp, sems[V], z_done.get(divmod(idx - 2, ND), 0))
                p1 = pg.op(Gp, lambda p_, _b=b, _dc=dc, _e=e_: p_.tensor_tensor(
                    yt[:, _e, :], xtg2[:, _b, _dc, :], c1b[:, _b, :], AL.mult))
                pg.wait(Gp, sems[Gp], p1)
                p2 = pg.op(Gp, lambda p_, _b=b, _e=e_: p_.tensor_tensor(
                    yt[:, _e, :], yt[:, _e, :], c0b[:, _b, :], AL.add))
                c_done[("pool", g)] = p2
                pg.wait(V, sems[Gp], p2)
                if idx >= 2 and dc < 2:
                    pg.wait(V, sems[Tn], gu_done.get((g - 2, NI - 1), 0))
                t1 = pg.op(V, lambda v, _e=e_: v.tensor_scalar(yt[:, _e, :], yt[:, _e, :], MAGIC, MAGIC, AL.add, AL.subtract))
                pg.wait(V, sems[V], t1)
                z_done[divmod(idx, ND)] = pg.op(V, lambda v, _bz=bz, _dc=dc, _e=e_: v.tensor_scalar(
                    zq[:, _bz, _dc, :], yt[:, _e, :], -128.0, 127.0, AL.max, AL.min))
            z_done[g] = z_done[(g, ND - 1)]

        def emit_gateup(g):
            b = g % 2
            pg.wait(Tn, sems[V], z_done[g])
            for oc in range(NI):
                e_ = oc % 2
                idx = g * NI + oc
                psG, psU = (0, 1) if e_ == 0 else (2, 3)
                if idx >= 2:
                    prev = divmod(idx - 2, NI)
                    pg.wait(Tn, sems[Sc], sil_done.get(prev, 0))
                    pg.wait(Tn, sems[V], h_done.get(prev, 0))
                for k in range(ND):
                    pg.op(Tn, lambda t, _p=psG, _b=b, _oc=oc, _k=k: t.matmul(
                        ps[_p][:, 0:GT], wq_g[:, _k * DI + _oc * P:_k * DI + (_oc + 1) * P], zq[:, _b, _k, :],
                        start=(_k == 0), stop=(_k == ND - 1)))
                for k in range(ND):
                    pg.op(Tn, lambda t, _p=psU, _b=b, _oc=oc, _k=k: t.matmul(
                        ps[_p][:, 0:GT], wq_u[:, _k * DI + _oc * P:_k * DI + (_oc + 1) * P], zq[:, _b, _k, :],
                        start=(_k == 0), stop=(_k == ND - 1)))
                gu_done[(g, oc)] = sems[Tn].n
                pg.wait(Sc, sems[Tn], gu_done[(g, oc)])
                if idx >= 2:
                    pg.wait(Sc, sems[V], h_done.get(divmod(idx - 2, NI), 0))
                sil_done[(g, oc)] = pg.op(Sc, lambda s, _p=psG, _oc=oc, _e=e_: s.activation(
                    sg2[:, _e, :], ps[_p][:, 0:GT], AF.Silu, bias=0.0, scale=sxg[:, _oc:_oc + 1]))
                hslot = idx % 4
                pg.wait(V, sems[Sc], sil_done[(g, oc)])
                if idx >= 4:
                    pg.wait_tok(V, spill.get(divmod(idx - 4, NI), 0))
                hv = pg.op(V, lambda v, _p=psU, _oc=oc, _e=e_, _h=hslot: v.scalar_tensor_tensor(
                    hb[:, _h, :], ps[_p][:, 0:GT], sxu[:, _oc:_oc + 1], sg2[:, _e, :], AL.mult, AL.mult))
                pg.wait(V, sems[V], hv)
                h_done[(g, oc)] = pg.op(V, lambda v, _oc=oc, _h=hslot: v.tensor_reduce(
                    habs_c[:, _oc:_oc + 1], hb[:, _h, :], AX.X, AL.max, apply_absolute_value=True))
                pg.wait(Sc, sems[V], h_done[(g, oc)])
                spill[(g, oc)] = pg.dma(Sc, dS, lambda s, _g=g, _oc=oc, _h=hslot: s.dma_start(
                    out=d_hsp[_oc, :, _g * GT:(_g + 1) * GT], in_=hb[:, _h, :]))
                if g + 1 < NG and oc % 2 == 0 and oc // 2 < ND:
                    gn, dcn = g + 1, oc // 2
                    bn, bzn, en = gn % 3, gn % 2, (oc // 2) % 2
                    idxn = gn * ND + dcn
                    if dcn == 0:
                        for t_ in ldB[gn]:
                            pg.wait_tok(Gp, t_)
                        pg.wait(Gp, sems[V], c_done[gn])
                    if idxn >= 2:
                        pg.wait(Gp, sems[V], z_done.get(divmod(idxn - 2, ND), 0))
                    p1 = pg.op(Gp, lambda p_, _b=bn, _dc=dcn, _e=en: p_.tensor_tensor(
                        yt[:, _e, :], xtg2[:, _b, _dc, :], c1b[:, _b, :], AL.mult))
                    pg.wait(Gp, sems[Gp], p1)
                    p2 = pg.op(Gp, lambda p_, _b=bn, _e=en: p_.tensor_tensor(
                        yt[:, _e, :], yt[:, _e, :], c0b[:, _b, :], AL.add))
                    c_done[("pool", gn)] = p2
                    pg.wait(V, sems[Gp], p2)
                    if idxn >= 2 and dcn < 2:
                        pg.wait(V, sems[Tn], gu_done.get((gn - 2, NI - 1), 0))
                    t1x = pg.op(V, lambda v, _e=en: v.tensor_scalar(
                        yt[:, _e, :], yt[:, _e, :], MAGIC, MAGIC, AL.add, AL.subtract))
                    pg.wait(V, sems[V], t1x)
                    z_done[divmod(idxn, ND)] = pg.op(V, lambda v, _bz=bzn, _dc=dcn, _e=en: v.tensor_scalar(
                        zq[:, _bz, _dc, :], yt[:, _e, :], -128.0, 127.0, AL.max, AL.min))
                    if dcn == ND - 1:
                        z_done[gn] = z_done[(gn, dcn)]
            pp = g % 2
            pg.wait(V, sems[V], h_done[(g, NI - 1)])
            r1 = pg.op(V, lambda v: v.tensor_reduce(nwt[:, 14:15], habs_c[:, :], AX.X, AL.max))
            pg.wait(V, sems[V], r1)
            habs_done[g] = pg.op(V, lambda v, _pp=pp: v.tensor_tensor(
                mhc[:, 1 - _pp:2 - _pp], mhc[:, _pp:_pp + 1], nwt[:, 14:15], AL.max))

        def emit_scan_steps(n):
            for _ in range(n):
                tstep = scan_emitted[0] + 1
                if tstep >= L:
                    return
                scan_emitted[0] = tstep
                pg.wait(Tn, sems[V], scan_v[tstep - 1])
                pg.op(Tn, lambda t, _t=tstep: t.matmul(
                    ps[4][0:SC, 0:NCHK], at_b[:, :], hs[0:SC, (_t - 1):T:L], start=True, stop=False))
                tq = pg.op(Tn, lambda t, _t=tstep: t.matmul(
                    ps[4][0:SC, 0:NCHK], i16b[:, :], uB[:, _t:T:L], start=False, stop=True))
                pg.wait(V, sems[Tn], tq)
                scan_v[tstep] = pg.op(V, lambda v, _t=tstep: v.tensor_copy(hs[0:SC, _t:T:L], ps[4][0:SC, 0:NCHK]))

        scan_v[0] = pg.op(V, lambda v: v.tensor_copy(hs[0:SC, 0:T:L], uB[:, 0:T:L]))

        for gg in (0, 1):
            emit_loadsB(gg)
            emit_cchain(gg)
        emit_xq(0)
        for g in range(NG):
            emit_gateup(g)
            if g + 2 < NG:
                emit_loadsB(g + 2)
                emit_cchain(g + 2)
            emit_scan_steps(2)
        emit_scan_steps(L)
        pg.wait(V, sems[V], scan_v[L - 1])
        f0 = pg.op(V, lambda v: v.memset(fstk[:, 0:NCHK:CPS], 0.0))
        fprev = f0
        q64 = (L - 1) * SC
        for c in range(1, CPS):
            pg.wait(Tn, sems[V], fprev)
            pg.op(Tn, lambda t, _c=c: t.matmul(
                ps[4][0:SC, 0:4], qstk[:, q64:q64 + SC], fstk[:, (_c - 1):NCHK:CPS], start=True, stop=False))
            tq = pg.op(Tn, lambda t, _c=c: t.matmul(
                ps[4][0:SC, 0:4], i16b[:, :], hs[0:SC, ((_c - 1) * L + L - 1):T:SEQ], start=False, stop=True))
            pg.wait(V, sems[Tn], tq)
            fprev = pg.op(V, lambda v, _c=c: v.tensor_copy(fstk[:, _c:NCHK:CPS], ps[4][0:SC, 0:4]))
        corr_prev = fprev
        for tstep in range(L):
            pg.wait(Tn, sems[V], corr_prev)
            tq = pg.op(Tn, lambda t, _t=tstep: t.matmul(
                ps[4][0:SC, 0:NCHK], qstk[:, _t * SC:(_t + 1) * SC], fstk[:, :], start=True, stop=True))
            pg.wait(V, sems[Tn], tq)
            a1 = pg.op(V, lambda v, _t=tstep: v.tensor_tensor(scr16[:, :], hs[0:SC, _t:T:L], ps[4][0:SC, 0:NCHK], AL.add))
            pg.wait(V, sems[V], a1)
            corr_prev = pg.op(V, lambda v, _t=tstep: v.tensor_copy(hs[0:SC, _t:T:L], scr16[:, :]))

        finh = NG % 2
        fh = col2row(pg, 7, lambda: mhc[:, finh:finh + 1], lambda: rowt[0:1, 0:P], dep_v=habs_done[NG - 1])
        pg.wait(V, sems[V], fh)
        vmh = pg.op(V, lambda v: v.tensor_reduce(sc1[0:1, 2:3], rowt[0:1, 0:P], AX.X, AL.max))
        pg.wait(Sc, sems[V], vmh)
        stg = pg.op(Sc, lambda s: s.activation(rowt[0:1, P:P + 8], ones_r[0:1, 0:8], AF.Identity, bias=0.0, scale=sc1[0:1, 2:3]))
        pg.wait(Gp, sems[Sc], stg)
        tk1 = pg.dma(Gp, dG, lambda e: e.dma_start(out=d_cc_in[:, :], in_=rowt[0:1, P:P + 8]))
        pg.wait_tok(Gp, tk1)
        cc_tok = pg.op(Gp, lambda e: e.collective_compute(
            "AllReduce", AL.max, replica_groups=[list(range(8))],
            ins=[d_cc_in[:, :]], outs=[d_cc_out[:, :]]))
        pg.wait(Gp, sems[Gp], cc_tok)
        tk2 = pg.dma(Gp, dG, lambda e: e.dma_start(out=sc1[0:1, 2:3], in_=d_cc_out[0:1, 0:1]))
        pg.wait_tok(Sc, tk2)
        sgm = pg.op(Sc, lambda s: s.activation(sc1[0:1, 3:4], sc1[0:1, 1:2], AF.Sigmoid, bias=0.0, scale=1.0))
        pg.wait(V, sems[Sc], sgm)
        seq_chain(pg, V, [
            lambda v: v.tensor_tensor(sc1[0:1, 4:5], sc1[0:1, 3:4], sc1[0:1, 2:3], AL.mult),
            lambda v: v.tensor_scalar(sc1[0:1, 4:5], sc1[0:1, 4:5], 1.0 / QMAX, None, AL.mult),
            lambda v: v.reciprocal(sc1[0:1, 5:6], sc1[0:1, 4:5]),
        ])
        vsd = sems[V].n
        pg.wait(Tn, sems[V], vsd)
        tb = pg.op(Tn, lambda t: t.matmul(ps[7][:, 0:2], ones_r[:, :], sc1[0:1, 4:6], start=True, stop=True))
        pg.wait(V, sems[Tn], tb)
        seq_chain(pg, V, [
            lambda v: v.tensor_copy(cS[:, 6:7], ps[7][:, 0:1]),
            lambda v: v.tensor_copy(cS[:, 3:4], ps[7][:, 1:2]),
        ])
        pg.emit(nc)

    pg = Prog(sems)
    with (
        nc.sbuf_tensor("hre", [P, 2, 4, GT], F32) as hre,
        nc.sbuf_tensor("yq", [P, 2, 4 * GT], F32) as yq,
        nc.sbuf_tensor("zh", [P, 2, NI, GT], BF16) as zh,
        nc.sbuf_tensor("xres", [P, 2, D], F32) as xres,
        nc.sbuf_tensor("outb", [P, 2, D], F32) as outb,
    ):
        wdl = [pg.dma(Sy, dL, lambda e, _o=oc: e.dma_start(out=wq_d[:, _o * D:(_o + 1) * D], in_=d_wdq[_o, :, :]))
               for oc in range(NI)]
        for t_ in wdl:
            pg.wait_tok(Tn, t_)
        ldh = {}
        sA_done = {}
        zq_done = {}
        acc_done = {}
        ev_done = {}
        outd = {}

        BW = 4
        NB = (NI + BW - 1) // BW

        def emit_hq(g):
            b = g % 2
            for ib in range(NB):
                o0 = ib * BW
                nch = min(BW, NI - o0)
                idx = g * NB + ib
                e_ = idx % 2
                if idx >= 2:
                    pg.wait(Sy, sems[V], sA_done[idx - 2])
                ldh[idx] = pg.dma(Sy, dL, lambda e, _g=g, _o=o0, _n=nch, _s=idx % 2: e.dma_start(
                    out=hre[:, _s, 0:_n, :], in_=d_hsp[_o:_o + _n, :, _g * GT:(_g + 1) * GT].rearrange("c p t -> p c t")))
                Qe = V
                pg.wait_tok(Qe, ldh[idx])
                if idx >= 2:
                    pg.wait(Qe, sems[V], zq_done.get(idx - 2, (None, 0))[1] if False else 0)
                if idx >= 2:
                    pt = zq_done[idx - 2]
                    pg.wait(Qe, pt[0], pt[1])
                sA_done[idx] = pg.op(Qe, lambda v, _s=idx % 2, _e=e_, _n=nch: v.tensor_scalar(
                    yq[:, _e, 0:_n * GT], hre[:, _s, 0:_n, :].rearrange("p c t -> p (c t)"),
                    cS[:, 3:4], MAGIC, AL.mult, AL.add))
                pg.wait(Qe, pg.sems[Qe], sA_done[idx])
                if g >= 2 and ib == 0:
                    pg.wait(Qe, sems[Tn], acc_done.get((g - 2, NTT - 1), 0))
                t0 = pg.op(Qe, lambda v, _e=e_, _n=nch: v.tensor_scalar(
                    yq[:, _e, 0:_n * GT], yq[:, _e, 0:_n * GT], MAGIC, -128.0, AL.subtract, AL.max))
                pg.wait(Qe, pg.sems[Qe], t0)
                zq_done[idx] = (pg.sems[Qe], pg.op(Qe, lambda v, _b=b, _o=o0, _n=nch, _e=e_: v.tensor_scalar(
                    zh[:, _b, _o:_o + _n, :].rearrange("p c t -> p (c t)"), yq[:, _e, 0:_n * GT],
                    127.0, cS[:, 6:7], AL.min, AL.mult)))
            zq_done[g] = zq_done[g * NB + NB - 1]

        def emit_tiles(g):
            b = g % 2
            pg.wait(Tn, zq_done[g][0], zq_done[g][1])
            if g >= 1:
                pg.wait(Sy, sems[V], ev_done.get((g - 1, NTT - 1), 0))
            ldx = [pg.dma(Sy, dL, lambda e, _g=g, _b=b, _tt=tt: e.dma_start(
                out=xres[:, (_g * NTT + _tt) % 2, :], in_=d_x[(_g * NTT + _tt) * P:(_g * NTT + _tt + 1) * P, :]))
                for tt in range(NTT)]
            for tt in range(NTT):
                ti = g * NTT + tt
                pA, pB = (0, 1) if tt == 0 else (2, 3)
                if ti >= 2:
                    pg.wait(Tn, sems[V], ev_done.get((g - 1, tt), 0))
                for half, bk in ((0, pA), (1, pB)):
                    pg.op(Tn, lambda t, _bk=bk, _h=half, _ti=ti: t.matmul(
                        ps[_bk][:, 0:512], hs[0:SC + 1, _ti * P:(_ti + 1) * P], wct[0:SC + 1, _h * 512:(_h + 1) * 512],
                        start=True, stop=False))
                    for oc in range(NI):
                        pg.op(Tn, lambda t, _bk=bk, _h=half, _b=b, _o=oc, _tt=tt: t.matmul(
                            ps[_bk][:, 0:512], zh[:, _b, _o, _tt * P:(_tt + 1) * P],
                            wq_d[:, _o * D + _h * 512:_o * D + _h * 512 + 512],
                            start=False, stop=(_o == NI - 1)))
                acc_done[(g, tt)] = sems[Tn].n
                pg.wait(V, sems[Tn], acc_done[(g, tt)])
                pg.wait_tok(V, ldx[tt])
                if ti >= 2:
                    pg.wait_tok(V, outd.get(ti - 2, 0))
                pg.op(V, lambda v, _pA=pA, _s=ti % 2: v.tensor_tensor(
                    outb[:, _s, 0:512], ps[_pA][:, 0:512], xres[:, _s, 0:512], AL.add))
                ev_done[(g, tt)] = pg.op(V, lambda v, _pB=pB, _s=ti % 2: v.tensor_tensor(
                    outb[:, _s, 512:1024], ps[_pB][:, 0:512], xres[:, _s, 512:1024], AL.add))
                pg.wait(Gp, sems[V], ev_done[(g, tt)])
                outd[ti] = pg.dma(Gp, dG, lambda e, _ti=ti, _s=ti % 2: e.dma_start(
                    out=d_out[_ti * P:(_ti + 1) * P, :], in_=outb[:, _s, :]))

        emit_hq(0)
        for g in range(NG):
            if g + 1 < NG:
                emit_hq(g + 1)
            emit_tiles(g)
        pg.wait_lanes(Gp, dG)
        pg.emit(nc)
    return nc


_NC_CACHE = {}


def kernel(L0_attn_x, L0_attn_W_in, L0_attn_b_in, L0_attn_A, L0_attn_Bm, L0_attn_Cm,
           L0_attn_W_gate, L0_attn_b_gate, L0_attn_W_out, L0_attn_b_out,
           L0_ffn_clip_x, L0_ffn_clip_down, L0_ffn_w_gate, L0_ffn_w_up, L0_ffn_w_down):
    x = np.asarray(L0_attn_x)
    B, S, Dm = x.shape
    xr = np.ascontiguousarray(x.reshape(B * S, Dm))
    f32 = np.float32

    w_in = np.asarray(L0_attn_W_in)
    shared = {
        "w2t": np.ascontiguousarray(w_in[Dm:2 * Dm, :].T.astype(f32)),
        "wgt": np.ascontiguousarray(np.asarray(L0_attn_W_gate).T.astype(f32)),
        "bmt": np.ascontiguousarray(np.asarray(L0_attn_Bm).T.astype(f32)),
        "cm": np.ascontiguousarray(np.asarray(L0_attn_Cm).astype(f32)),
        "wot": np.ascontiguousarray(np.asarray(L0_attn_W_out).T.astype(f32)),
        "anat": np.ascontiguousarray(np.asarray(L0_attn_A).astype(f32)),
        "at": np.ascontiguousarray(np.asarray(L0_attn_A).T.astype(f32)),
        "b2c": np.ascontiguousarray(np.asarray(L0_attn_b_in)[Dm:2 * Dm].astype(f32).reshape(ND, P).T),
        "bgc": np.ascontiguousarray(np.asarray(L0_attn_b_gate).astype(f32).reshape(ND, P).T),
        "bout": np.ascontiguousarray(np.asarray(L0_attn_b_out).astype(f32).reshape(1, Dm)),
        "clipx": np.asarray(L0_ffn_clip_x, f32).reshape(1, 1),
        "clipd": np.asarray(L0_ffn_clip_down, f32).reshape(1, 1),
        "fg_n": np.ascontiguousarray(np.asarray(L0_ffn_w_gate).astype(f32)),
        "fu_n": np.ascontiguousarray(np.asarray(L0_ffn_w_up).astype(f32)),
        "fd_n": np.ascontiguousarray(np.asarray(L0_ffn_w_down).astype(f32)),
        "fg_t": np.ascontiguousarray(np.asarray(L0_ffn_w_gate).astype(f32).T),
        "fu_t": np.ascontiguousarray(np.asarray(L0_ffn_w_up).astype(f32).T),
        "fd_t": np.ascontiguousarray(np.asarray(L0_ffn_w_down).astype(f32).T),
        "eye": np.eye(P, dtype=f32),
        "hones": np.ones((1, T), f32),
        "ones_r": np.ones((1, P), f32),
        "ones_c": np.ones((P, 1), f32),
    }

    if "nc" not in _NC_CACHE:
        _NC_CACHE["nc"] = build_nc()
    nc = _NC_CACHE["nc"]

    in_maps = []
    for c in range(8):
        sl = slice(c * T, (c + 1) * T)
        m = dict(shared)
        m["x"] = np.ascontiguousarray(xr[sl])
        m["xt"] = np.ascontiguousarray(xr[sl].T)
        in_maps.append(m)

    res = run_bass_kernel_spmd(nc, in_maps, core_ids=list(range(8)))
    out = np.concatenate([np.asarray(r["out"]) for r in res.results], axis=0)
    return out.reshape(B, S, Dm).astype(np.float32)
